# version 06
# baseline (speedup 1.0000x reference)
import numpy as np
import concourse.bass as bass
import concourse.mybir as mybir
from concourse.bass_isa import ReduceOp
from contextlib import ExitStack

F32 = mybir.dt.float32
BF16 = mybir.dt.bfloat16
U16 = mybir.dt.uint16
ALU = mybir.AluOpType
AF = mybir.ActivationFunctionType
AX = mybir.AxisListType

EPS64C = float(np.float32(np.finfo(np.float64).eps))


def make_cfg(D=2048, TC=2048, W=192, H=8192, SUBW=448, n_cores=8, ecg=None, dcg=None):
    c = {}
    c["D"] = D; c["TC"] = TC; c["W"] = W; c["H"] = H
    c["ND"] = D // 128
    c["NH"] = H // 128
    c["SPAN"] = W + TC
    assert c["SPAN"] % SUBW == 0
    c["SUBW"] = SUBW
    c["NSUB"] = c["SPAN"] // SUBW
    c["HCOLS"] = W + 1 + TC
    c["MODN"] = 6 * D
    c["NMOD"] = c["MODN"] // 128
    c["STRIP"] = min(512, TC)
    assert TC % c["STRIP"] == 0
    c["NSTRIP"] = TC // c["STRIP"]
    c["ECG"] = ecg or min(8, c["ND"])
    assert c["ND"] % c["ECG"] == 0
    c["NB4"] = H // 256
    c["DCG"] = dcg or min(8, c["ND"])
    c["n_cores"] = n_cores
    return c


class Ledger:
    def __init__(self):
        self.sems = []

    def new(self, stack, nc, name):
        h = stack.enter_context(nc.semaphore(name))
        s = Sem(h, name)
        self.sems.append(s)
        return s


class Sem:
    def __init__(self, h, name):
        self.h = h
        self.name = name
        self.n = 0

    def inc(self, instr, k=1):
        instr.then_inc(self.h, k)
        self.n += k
        return self.n

    def dma(self, instr):
        return self.inc(instr, 16)


def host_prepare(cfg, inputs):
    D, TC, W, H = cfg["D"], cfg["TC"], cfg["W"], cfg["H"]
    n = cfg["n_cores"]
    T = TC * n
    x = np.ascontiguousarray(inputs["L0_attn_x"].reshape(T, D))
    wkT = np.ascontiguousarray(inputs["L0_attn_Wk"].T)
    wvT = np.ascontiguousarray(inputs["L0_attn_Wv"].T)
    wrT = np.ascontiguousarray(inputs["L0_attn_Wr"].T)
    woT = np.ascontiguousarray(inputs["L0_attn_Wo"].T)
    wguT = np.ascontiguousarray(inputs["L0_ffn_w_gate_up"].T)
    wprT = np.ascontiguousarray(inputs["L0_ffn_w_proj"].T)
    wmod = inputs["L0_wmod"]
    ND, NMOD = cfg["ND"], cfg["NMOD"]
    wmod_t = np.ascontiguousarray(
        wmod.reshape(ND, 128, NMOD, 128).transpose(2, 1, 0, 3).reshape(NMOD, 128, ND * 128))

    def as_cols(v):
        return np.ascontiguousarray(np.asarray(v).reshape(ND, 128).T)

    pvecs = np.concatenate([
        as_cols(inputs["L0_c"]), as_cols(inputs["L0_attn_time_decay"]),
        as_cols(inputs["L0_attn_time_first"]), as_cols(inputs["L0_attn_time_mix_k"]),
        as_cols(inputs["L0_attn_time_mix_v"]), as_cols(inputs["L0_attn_time_mix_r"]),
    ], axis=1)

    in_maps = []
    for c in range(n):
        t0 = c * TC
        lo = t0 - W - 1
        xs = np.zeros((cfg["HCOLS"], D), dtype=np.float32)
        s = max(0, lo)
        xs[s - lo:] = x[s:t0 + TC]
        xT = np.ascontiguousarray(xs.T)
        wmask = np.ones((1, cfg["SPAN"]), dtype=np.float32)
        if c == 0:
            wmask[0, :W] = 0.0
        in_maps.append({
            "xT": xT, "wkT": wkT, "wvT": wvT, "wrT": wrT, "woT": woT,
            "wguT": wguT, "wprT": wprT, "wmod_t": wmod_t,
            "pvecs": pvecs, "wmask": wmask,
        })
    return in_maps


def host_gather(cfg, results):
    D, TC, n = cfg["D"], cfg["TC"], cfg["n_cores"]
    out = np.empty((1, TC * n, D), dtype=np.float32)
    for c in range(n):
        out[0, c * TC:(c + 1) * TC] = results[c]["outT"].T
    return out



class Ctx:
    pass


def barrier(ctx):
    nc, blk = ctx.nc, ctx.blk
    engs = [("sync", ctx.eS), ("scalar", ctx.eA), ("vector", ctx.eV),
            ("tensor", ctx.eP), ("gpsimd", ctx.eG)]
    for name, sem in engs:
        def mk(sem=sem, v=sem.n):
            def f(e):
                e.wait_ge(sem.h, v)
                sem.inc(e.nop(), 1)
            return f
        getattr(blk, name)(mk())
    dmas = [ctx.r0, ctx.r1, ctx.r2, ctx.r3, ctx.w0, ctx.w1,
            ctx.dsa, ctx.dsb, ctx.dsc, ctx.out_sem, ctx.out2]
    snap = [(sem, sem.n) for _, sem in engs] + [(x, x.n) for x in dmas]
    for name, _ in engs:
        def mk2():
            def f(e):
                for sem, v in snap:
                    e.wait_ge(sem.h, v)
            return f
        getattr(blk, name)(mk2())


def emit_P0(ctx):
    c = ctx.cfg
    nc, blk = ctx.nc, ctx.blk
    ND, NMOD, W = c["ND"], c["NMOD"], c["W"]

    vec_done = []

    def f_sync(e):
        ctx.dsa.dma(e.dma_start(ctx.vecs.ap(), ctx.in_pvecs))
        ctx.dsa.dma(e.dma_start(ctx.mrow.ap(), ctx.in_wmask[:, 0:W]))
        vec_done.append(ctx.dsa.n)
        pe_base = ctx.eP.n
        for m in range(NMOD):
            if m >= 2:
                e.wait_ge(ctx.eP.h, pe_base + m - 1)
            s = [ctx.dsb, ctx.dsc][m % 2]
            s.dma(e.dma_start(ctx.slab[m % 2].ap(), ctx.in_wmod[m, :, :]))
            ctx.slab_loaded[m] = s.n
    blk.sync(f_sync)

    def f_act(e):
        e.wait_ge(ctx.dsa.h, vec_done[0])
        v = ctx.vecs.ap()
        e.activation(ctx.silu_c.ap(), v[:, 0:ND], AF.Silu)
        e.activation(ctx.tw.ap(), v[:, ND:2 * ND], AF.Exp)
        ctx.eA.inc(e.activation(ctx.eu16.ap(), v[:, 2 * ND:3 * ND], AF.Exp), 1)
        e.wait_ge(ctx.eA.h, ctx.eA.n)
        ctx.eA.inc(e.activation(ctx.lam16.ap(), ctx.tw.ap(), AF.Exp, scale=-1.0), 1)
    blk.scalar(f_act)
    act_ready = ctx.eA.n

    def f_vec(e):
        e.wait_ge(ctx.dsa.h, vec_done[0])
        e.memset(ctx.g2.ap()[0:64, 64:128], 0.0)
        e.memset(ctx.g2.ap()[64:128, 0:64], 0.0)
        e.memset(ctx.g2.ap()[0:64, 0:64], 1.0 / 64.0)
        e.memset(ctx.g2.ap()[64:128, 64:128], 1.0 / 64.0)
        e.memset(ctx.zs.ap(), 0.0)
        e.memset(ctx.ones128.ap(), 1.0)
        e.memset(ctx.onesrow.ap(), 1.0)
        e.memset(ctx.epsln.ap(), 1e-5)
        e.memset(ctx.epsrms.ap(), EPS64C)
        e.memset(ctx.macc.ap(), -1e30)
        v = ctx.vecs.ap()
        for i in range(3):
            e.tensor_scalar(ctx.tm1m[i].ap(), v[:, (3 + i) * ND:(4 + i) * ND],
                            -1.0, 1.0, ALU.mult, ALU.add)
        ctx.eV.inc(e.tensor_scalar(ctx.mrow.ap(), ctx.mrow.ap(), -1.0, 1e30,
                                   ALU.add, ALU.mult), 1)
    blk.vector(f_vec)
    vec_ready = ctx.eV.n

    def f_pe(e):
        e.wait_ge(ctx.eA.h, act_ready)
        e.wait_ge(ctx.eV.h, vec_ready)
        for m in range(NMOD):
            s = [ctx.dsb, ctx.dsc][m % 2]
            e.wait_ge(s.h, ctx.slab_loaded[m])
            sl = ctx.slab[m % 2].ap()
            for dc in range(ND):
                i = e.matmul(ctx.ps[7].ap()[:, m:m + 1],
                             sl[:, 128 * dc:128 * (dc + 1)],
                             ctx.silu_c.ap()[:, dc:dc + 1],
                             start=(dc == 0), stop=(dc == ND - 1))
                if dc == ND - 1:
                    ctx.eP.inc(i, 1)
        ctx.eP.inc(e.matmul(ctx.ps[6].ap()[:, 0:W], ctx.onesrow.ap(),
                            ctx.mrow.ap(), start=True, stop=True), 1)
    blk.tensor(f_pe)
    pe_mod = ctx.eP.n

    def f_vec2(e):
        e.wait_ge(ctx.eP.h, pe_mod)
        ctx.eV.inc(e.tensor_copy(ctx.mod_sb.ap(), ctx.ps[7].ap()[:, 0:ctx.cfg["NMOD"]]), 1)
        e.tensor_copy(ctx.maskneg.ap(), ctx.ps[6].ap()[:, 0:W])
        e.wait_ge(ctx.eV.h, ctx.eV.n)
        m = ctx.mod_sb.ap()
        e.tensor_scalar(ctx.c1a.ap(), m[:, 1 * ND:2 * ND], 1.0, None, ALU.add)
        e.tensor_scalar(ctx.c1b.ap(), m[:, 4 * ND:5 * ND], 1.0, None, ALU.add)
        for (g_t, q) in ((ctx.ga_cl, 2), (ctx.gb_cl, 5)):
            mq = m[:, q * ND:(q + 1) * ND]
            e.wait_ge(ctx.eV.h, ctx.eV.n)
            ctx.eV.inc(e.tensor_tensor(ctx.tmpg.ap(), mq, ctx.zs.ap(), ALU.is_ge), 1)
            ctx.eV.inc(e.tensor_scalar(ctx.tmpg2.ap(), mq, -1.0, None, ALU.mult), 1)
            e.wait_ge(ctx.eV.h, ctx.eV.n)
            ctx.eV.inc(e.tensor_scalar(ctx.tmpg.ap(), ctx.tmpg.ap(), 2.0, -1.0,
                                       ALU.mult, ALU.add), 1)
            ctx.eV.inc(e.tensor_tensor(ctx.tmpg2.ap(), ctx.tmpg2.ap(), mq,
                                       ALU.max), 1)
            e.wait_ge(ctx.eV.h, ctx.eV.n)
            ctx.eV.inc(e.tensor_scalar(ctx.tmpg2.ap(), ctx.tmpg2.ap(), 0.25,
                                       None, ALU.max), 1)
            e.wait_ge(ctx.eV.h, ctx.eV.n)
            ctx.eV.inc(e.tensor_tensor(g_t.ap(), ctx.tmpg.ap(), ctx.tmpg2.ap(),
                                       ALU.mult), 1)
    blk.vector(f_vec2)


def emit_P2a(ctx):
    c = ctx.cfg
    blk = ctx.blk
    ND, D = c["ND"], c["D"]
    mats_in = [ctx.in_wkT, ctx.in_wvT, ctx.in_wrT, ctx.in_woT]
    mats_hi = [ctx.s_wk_hi, ctx.s_wv_hi, ctx.s_wr_hi, ctx.s_wo_hi]
    mats_lo = [ctx.s_wk_lo, ctx.s_wv_lo, ctx.s_wr_lo, ctx.s_wo_lo]
    n_it = 4 * ND
    eA0, eV0 = ctx.eA.n, ctx.eV.n
    r = [ctx.r0, ctx.r1]
    w = [ctx.w0, ctx.w1]
    loaded = {}

    def f_sync(e):
        e.wait_ge(ctx.r0.h, ctx.r0.n)
        e.wait_ge(ctx.r1.h, ctx.r1.n)
        for it in range(n_it):
            if it >= 2:
                e.wait_ge(ctx.eA.h, eA0 + (it - 1))
                e.wait_ge(ctx.eV.h, eV0 + (it - 1))
            mat, dc = divmod(it, ND)
            sl = it % 2
            r[sl].dma(e.dma_start(ctx.wblkF[sl].ap(),
                                  mats_in[mat][128 * dc:128 * (dc + 1), :]))
            loaded[it] = r[sl].n
    blk.sync(f_sync)

    whi_done = {}

    def f_act(e):
        for it in range(n_it):
            mat, dc = divmod(it, ND)
            sl = it % 2
            e.wait_ge(r[sl].h, loaded[it])
            if it >= 2:
                e.wait_ge(w[sl].h, whi_done.get(it - 2, 0))
            i = e.activation(ctx.whi_t[sl], ctx.wblkF[sl].ap(), AF.Copy)
            ctx.eA.inc(i, 1)
            assert ctx.eA.n == eA0 + it + 1
            e.wait_ge(ctx.eA.h, ctx.eA.n)
            w[sl].dma(e.dma_start(mats_hi[mat][128 * dc:128 * (dc + 1), :],
                                  ctx.whi_t[sl]))
            whi_done[it] = w[sl].n
            e.wait_ge(ctx.eV.h, eV0 + it + 1)
            w[sl].dma(e.dma_start(mats_lo[mat][128 * dc:128 * (dc + 1), :],
                                  ctx.wlo_t[sl]))
            whi_done[it] = w[sl].n
    blk.scalar(f_act)

    def f_vec(e):
        for it in range(n_it):
            sl = it % 2
            e.wait_ge(r[sl].h, loaded[it])
            e.wait_ge(ctx.eA.h, eA0 + it + 1)
            if it >= 2:
                e.wait_ge(w[sl].h, whi_done.get(it - 2, 0))
            i = e.tensor_tensor(ctx.wlo_t[sl], ctx.wblkF[sl].ap(),
                                ctx.whi_t[sl], ALU.subtract)
            ctx.eV.inc(i, 1)
            assert ctx.eV.n == eV0 + it + 1
    blk.vector(f_vec)


def emit_LN(ctx, src, dst_hi, dst_lo, c1_t, shift_comp, widths, tag):
    c = ctx.cfg
    blk = ctx.blk
    ND, D, W = c["ND"], c["D"], c["W"]
    NSUB = len(widths)
    offs = np.cumsum([0] + list(widths)).tolist()
    r = [ctx.r0, ctx.r1]
    w = [ctx.w0, ctx.w1]

    for sub in range(NSUB):
        wd, off = widths[sub], offs[sub]
        eP0, eA0, eV0 = ctx.eP.n, ctx.eA.n, ctx.eV.n
        loadedA = {}

        def f_syncA(e, wd=wd, off=off, eP0=eP0, loadedA=loadedA):
            e.wait_ge(ctx.r0.h, ctx.r0.n)
            e.wait_ge(ctx.r1.h, ctx.r1.n)
            e.wait_ge(ctx.eV.h, ctx.eV.n)
            for dc in range(ND):
                if dc >= 2:
                    e.wait_ge(ctx.eP.h, eP0 + dc - 1)
                sl = dc % 2
                r[sl].dma(e.dma_start(ctx.pFa_sl[sl][:, 0:wd],
                                      src[128 * dc:128 * (dc + 1), off:off + wd]))
                loadedA[dc] = r[sl].n
        blk.sync(f_syncA)

        def f_actA(e, wd=wd, eA0=eA0, eP0=eP0, loadedA=loadedA):
            for dc in range(ND):
                sl = dc % 2
                e.wait_ge(r[sl].h, loadedA[dc])
                if dc >= 2:
                    e.wait_ge(ctx.eP.h, eP0 + dc - 1)
                i = e.activation(ctx.pFb_sl[sl][:, 0:wd],
                                 ctx.pFa_sl[sl][:, 0:wd], AF.Square)
                ctx.eA.inc(i, 1)
                assert ctx.eA.n == eA0 + dc + 1
        blk.scalar(f_actA)

        eVpre = ctx.eV.n

        def f_peA(e, wd=wd, eA0=eA0, eP0=eP0, loadedA=loadedA, eVpre=eVpre):
            e.wait_ge(ctx.eV.h, eVpre)
            for dc in range(ND):
                sl = dc % 2
                e.wait_ge(r[sl].h, loadedA[dc])
                e.wait_ge(ctx.eA.h, eA0 + dc + 1)
                e.matmul(ctx.ps[6].ap()[0:1, 0:wd], ctx.ones128.ap(),
                         ctx.pFa_sl[sl][:, 0:wd],
                         start=(dc == 0), stop=(dc == ND - 1))
                i = e.matmul(ctx.ps[5].ap()[0:1, 0:wd], ctx.ones128.ap(),
                             ctx.pFb_sl[sl][:, 0:wd],
                             start=(dc == 0), stop=(dc == ND - 1))
                ctx.eP.inc(i, 1)
                assert ctx.eP.n == eP0 + dc + 1
        blk.tensor(f_peA)

        S = 512
        rws = ctx.rows2.ap()
        r0s, r1s, r2s = rws[:, 0:wd], rws[:, S:S + wd], rws[:, 2 * S:2 * S + wd]

        def f_vecS(e, wd=wd, eP0=eP0, r0s=r0s, r1s=r1s, r2s=r2s):
            e.wait_ge(ctx.eP.h, eP0 + ND)
            e.tensor_copy(r0s, ctx.ps[6].ap()[0:1, 0:wd])
            ctx.eV.inc(e.tensor_copy(r1s, ctx.ps[5].ap()[0:1, 0:wd]), 1)
            e.wait_ge(ctx.eV.h, ctx.eV.n)
            ctx.eV.inc(e.tensor_scalar(r2s, r0s, 1.0 / ctx.cfg["D"], None, ALU.mult), 1)
            e.wait_ge(ctx.eV.h, ctx.eV.n)
            ctx.eV.inc(e.tensor_tensor(r0s, r2s, r2s, ALU.mult), 1)
            e.wait_ge(ctx.eV.h, ctx.eV.n)
            ctx.eV.inc(e.scalar_tensor_tensor(r1s, r1s, 1.0 / ctx.cfg["D"], r0s,
                                              ALU.mult, ALU.subtract), 1)
        blk.vector(f_vecS)
        v_var = ctx.eV.n

        def f_actS(e, r1s=r1s, v_var=v_var):
            e.wait_ge(ctx.eV.h, v_var)
            ctx.eA.inc(e.activation(r1s, r1s, AF.Sqrt, bias=ctx.epsln.ap()), 1)
        blk.scalar(f_actS)
        a_sd = ctx.eA.n

        def f_vecS2(e, r1s=r1s, r2s=r2s, a_sd=a_sd):
            e.wait_ge(ctx.eA.h, a_sd)
            ctx.eV.inc(e.reciprocal(r1s, r1s), 1)
            ctx.eV.inc(e.tensor_scalar(r2s, r2s, -1.0, None, ALU.mult), 1)
        blk.vector(f_vecS2)
        v_rows = ctx.eV.n

        def f_peB(e, wd=wd, r1s=r1s, r2s=r2s, v_rows=v_rows):
            e.wait_ge(ctx.eV.h, v_rows)
            e.matmul(ctx.ps[4].ap()[:, 0:wd], ctx.onesrow.ap(), r2s,
                     start=True, stop=True)
            ctx.eP.inc(e.matmul(ctx.ps[3].ap()[:, 0:wd], ctx.onesrow.ap(), r1s,
                                start=True, stop=True), 1)
        blk.tensor(f_peB)
        p_bcast = ctx.eP.n

        eV1, eA1 = ctx.eV.n, ctx.eA.n
        loadedB = {}

        def f_syncB(e, wd=wd, off=off, eV1=eV1, loadedB=loadedB):
            e.wait_ge(ctx.r0.h, ctx.r0.n)
            e.wait_ge(ctx.r1.h, ctx.r1.n)
            e.wait_ge(ctx.eP.h, ctx.eP.n)
            for dc in range(ND):
                if dc >= 2:
                    e.wait_ge(ctx.eV.h, eV1 + 4 * (dc - 1))
                sl = dc % 2
                r[sl].dma(e.dma_start(ctx.pFa_sl[sl][:, 0:wd],
                                      src[128 * dc:128 * (dc + 1), off:off + wd]))
                loadedB[dc] = r[sl].n
        blk.sync(f_syncB)

        def f_vecB(e, wd=wd, eV1=eV1, eA1=eA1, p_bcast=p_bcast, loadedB=loadedB):
            e.wait_ge(ctx.eP.h, p_bcast)
            wb = [ctx.w0.n, ctx.w1.n]
            for dc in range(ND):
                sl = dc % 2
                e.wait_ge(r[sl].h, loadedB[dc])
                e.wait_ge(ctx.w0.h if sl == 0 else ctx.w1.h,
                          ctx.hst_done.get((tag, sub, dc - 2), wb[sl]))
                ctx.eV.inc(e.tensor_tensor(ctx.pFc_sl[sl][:, 0:wd],
                                           ctx.pFa_sl[sl][:, 0:wd],
                                           ctx.ps[4].ap()[:, 0:wd], ALU.add), 1)
                e.wait_ge(ctx.eV.h, ctx.eV.n)
                ctx.eV.inc(e.tensor_tensor(ctx.pFc_sl[sl][:, 0:wd],
                                           ctx.pFc_sl[sl][:, 0:wd],
                                           ctx.ps[3].ap()[:, 0:wd], ALU.mult), 1)
                e.wait_ge(ctx.eV.h, ctx.eV.n)
                ctx.eV.inc(e.scalar_tensor_tensor(
                    ctx.pFd_sl[sl][:, 0:wd], ctx.pFc_sl[sl][:, 0:wd],
                    c1_t[:, dc:dc + 1],
                    ctx.mod_sb.ap()[:, shift_comp * ND + dc:shift_comp * ND + dc + 1]
                    .broadcast_to([128, wd]),
                    ALU.mult, ALU.add), 1)
                e.wait_ge(ctx.eA.h, eA1 + dc + 1)
                ctx.eV.inc(e.tensor_tensor(ctx.hlo_t[sl].ap()[:, 0:wd],
                                           ctx.pFd_sl[sl][:, 0:wd],
                                           ctx.hhi_t[sl].ap()[:, 0:wd],
                                           ALU.subtract), 1)
                assert ctx.eV.n == eV1 + 4 * (dc + 1)
        blk.vector(f_vecB)

        def f_actB(e, wd=wd, off=off, eV1=eV1, eA1=eA1):
            wb = [ctx.w0.n, ctx.w1.n]
            for dc in range(ND):
                sl = dc % 2
                e.wait_ge(ctx.eV.h, eV1 + 4 * dc + 3)
                e.wait_ge(ctx.w0.h if sl == 0 else ctx.w1.h,
                          ctx.hst_done.get((tag, sub, dc - 2), wb[sl]))
                ctx.eA.inc(e.activation(ctx.hhi_t[sl].ap()[:, 0:wd],
                                        ctx.pFd_sl[sl][:, 0:wd], AF.Copy), 1)
                assert ctx.eA.n == eA1 + dc + 1
                e.wait_ge(ctx.eA.h, ctx.eA.n)
                ws = w[sl]
                ws.dma(e.dma_start(dst_hi[128 * dc:128 * (dc + 1), off:off + wd],
                                   ctx.hhi_t[sl].ap()[:, 0:wd]))
                e.wait_ge(ctx.eV.h, eV1 + 4 * (dc + 1))
                ws.dma(e.dma_start(dst_lo[128 * dc:128 * (dc + 1), off:off + wd],
                                   ctx.hlo_t[sl].ap()[:, 0:wd]))
                ctx.hst_done[(tag, sub, dc)] = ws.n
        blk.scalar(f_actB)



def emit_P2(ctx):
    c = ctx.cfg
    blk = ctx.blk
    ND, SUBW, NSUB, ECG = c["ND"], c["SUBW"], c["NSUB"], c["ECG"]
    NG = ND // ECG
    s_dst = [ctx.s_k, ctx.s_v, ctx.s_sr]
    w_hi = [ctx.s_wk_hi, ctx.s_wv_hi, ctx.s_wr_hi]
    w_lo = [ctx.s_wk_lo, ctx.s_wv_lo, ctx.s_wr_lo]
    WD = ECG * 128
    r = [ctx.r0, ctx.r1]
    rw = [ctx.r2, ctx.r3]
    w = [ctx.w0, ctx.w1]
    HS = SUBW + 1

    for sub in range(NSUB):
        off = sub * SUBW
        hs_loaded = []

        def f_syncH(e, off=off):
            e.wait_ge(ctx.eV.h, ctx.eV.n)
            e.wait_ge(ctx.r0.h, ctx.r0.n)
            for dc in range(ND):
                r[0].dma(e.dma_start(ctx.bfA.ap()[:, dc * HS:dc * HS + HS],
                                     ctx.s_h_hi[128 * dc:128 * (dc + 1), off:off + HS]))
                r[0].dma(e.dma_start(ctx.bfB.ap()[:, dc * HS:dc * HS + HS],
                                     ctx.s_h_lo[128 * dc:128 * (dc + 1), off:off + HS]))
            hs_loaded.append(r[0].n)
        blk.sync(f_syncH)

        for mat in range(3):
            eV0, eA0 = ctx.eV.n, ctx.eA.n
            XO = ctx.XOFF

            def f_vecM(e, mat=mat, eV0=eV0, eA0=eA0, hs_loaded=hs_loaded, XO=XO):
                e.wait_ge(r[0].h, hs_loaded[0])
                e.wait_ge(ctx.eP.h, ctx.eP.n)
                tm = ctx.vecs.ap()[:, (3 + mat) * ND:(4 + mat) * ND]
                for dc in range(ND):
                    e.wait_ge(ctx.eV.h, ctx.eV.n)
                    hh = ctx.bfA.ap()[:, dc * HS:dc * HS + HS]
                    hl = ctx.bfB.ap()[:, dc * HS:dc * HS + HS]
                    ctx.eV.inc(e.tensor_tensor(ctx.pFa_sl[0][:, 0:SUBW],
                                               hh[:, 1:HS], hl[:, 1:HS], ALU.add), 1)
                    ctx.eV.inc(e.tensor_tensor(ctx.pFb_sl[0][:, 0:SUBW],
                                               hh[:, 0:SUBW], hl[:, 0:SUBW], ALU.add), 1)
                    e.wait_ge(ctx.eV.h, ctx.eV.n)
                    ctx.eV.inc(e.tensor_scalar(ctx.pFb_sl[0][:, 0:SUBW],
                                               ctx.pFb_sl[0][:, 0:SUBW],
                                               ctx.tm1m[mat].ap()[:, dc:dc + 1],
                                               None, ALU.mult), 1)
                    e.wait_ge(ctx.eV.h, ctx.eV.n)
                    ctx.eV.inc(e.scalar_tensor_tensor(
                        ctx.pFc_sl[0][:, 0:SUBW], ctx.pFa_sl[0][:, 0:SUBW],
                        tm[:, dc:dc + 1], ctx.pFb_sl[0][:, 0:SUBW],
                        ALU.mult, ALU.add), 1)
                    e.wait_ge(ctx.eA.h, eA0 + dc + 1)
                    ctx.eV.inc(e.tensor_tensor(
                        ctx.bfB.ap()[:, XO + dc * SUBW:XO + (dc + 1) * SUBW],
                        ctx.pFc_sl[0][:, 0:SUBW],
                        ctx.bfA.ap()[:, XO + dc * SUBW:XO + (dc + 1) * SUBW],
                        ALU.subtract), 1)
                    assert ctx.eV.n == eV0 + 5 * (dc + 1)
            blk.vector(f_vecM)

            def f_actM(e, eV0=eV0, eA0=eA0, XO=XO):
                for dc in range(ND):
                    e.wait_ge(ctx.eV.h, eV0 + 5 * dc + 4)
                    ctx.eA.inc(e.activation(
                        ctx.bfA.ap()[:, XO + dc * SUBW:XO + (dc + 1) * SUBW],
                        ctx.pFc_sl[0][:, 0:SUBW], AF.Copy), 1)
                    assert ctx.eA.n == eA0 + dc + 1
            blk.scalar(f_actM)
            mix_done = ctx.eV.n

            for ecg in range(NG):
                gidx = ctx.p2_gidx
                ctx.p2_gidx += 1
                bk = 0 if ECG * 2 > 8 else ECG * (gidx % 2)
                eP0 = ctx.eP.n
                wl_cnt = {}

                def f_syncW(e, mat=mat, ecg=ecg, eP0=eP0, wl_cnt=wl_cnt):
                    e.wait_ge(ctx.r2.h, ctx.r2.n)
                    e.wait_ge(ctx.r3.h, ctx.r3.n)
                    for dc in range(ND):
                        sl = dc % 2
                        e.wait_ge(ctx.eP.h, eP0 + dc - 1 if dc >= 2 else eP0)
                        rw[sl].dma(e.dma_start(
                            ctx.wst_hi[sl].ap()[:, 0:WD],
                            w_hi[mat][128 * dc:128 * (dc + 1), ecg * WD:(ecg + 1) * WD]))
                        rw[sl].dma(e.dma_start(
                            ctx.wst_lo[sl].ap()[:, 0:WD],
                            w_lo[mat][128 * dc:128 * (dc + 1), ecg * WD:(ecg + 1) * WD]))
                        wl_cnt[dc] = rw[sl].n
                    ctx.p2_wl[(sub, mat, ecg)] = dict(wl_cnt)
                blk.sync(f_syncW)

                def f_peM(e, mat=mat, ecg=ecg, eP0=eP0, mix_done=mix_done,
                          wl_cnt=wl_cnt, XO=XO, gidx=gidx, bk=bk):
                    e.wait_ge(ctx.eV.h, mix_done)
                    prev = gidx - (2 if ECG * 2 <= 8 else 1)
                    for dc in range(ND):
                        sl = dc % 2
                        e.wait_ge(rw[sl].h, wl_cnt[dc])
                        xh = ctx.bfA.ap()[:, XO + dc * SUBW:XO + (dc + 1) * SUBW]
                        xl = ctx.bfB.ap()[:, XO + dc * SUBW:XO + (dc + 1) * SUBW]
                        for ec in range(ECG):
                            if dc == 0:
                                eng, v = ctx.p2_bankfree.get((prev, bk + ec), (None, 0))
                                if eng is not None:
                                    e.wait_ge(eng, v)
                            wh = ctx.wst_hi[sl].ap()[:, 128 * ec:128 * (ec + 1)]
                            wl = ctx.wst_lo[sl].ap()[:, 128 * ec:128 * (ec + 1)]
                            st = (dc == 0)
                            e.matmul(ctx.ps[bk + ec].ap()[:, 0:SUBW], wh, xh,
                                     start=st, stop=False)
                            e.matmul(ctx.ps[bk + ec].ap()[:, 0:SUBW], wh, xl,
                                     start=False, stop=False)
                            i = e.matmul(ctx.ps[bk + ec].ap()[:, 0:SUBW], wl, xh,
                                         start=False, stop=(dc == ND - 1))
                        ctx.eP.inc(i, 1)
                        assert ctx.eP.n == eP0 + dc + 1
                blk.tensor(f_peM)
                grp_done = ctx.eP.n

                def f_evac(e_v, e_a=None):
                    pass

                if mat < 2:
                    eVb = ctx.eV.n
                    wbE = [ctx.w0.n, ctx.w1.n]

                    def st_pred(ec, wbE=wbE):
                        return wbE[ec % 2] + 16 * (ec // 2 + 1)

                    def f_vecE(e, mat=mat, ecg=ecg, sub=sub, grp_done=grp_done,
                               eVb=eVb, wbE=wbE, st_pred=st_pred, bk=bk, gidx=gidx):
                        e.wait_ge(ctx.eP.h, grp_done)
                        for ec in range(ECG):
                            sl = ec % 2
                            e.wait_ge(w[sl].h,
                                      st_pred(ec - 2) if ec >= 2 else wbE[sl])
                            ctx.eV.inc(e.tensor_copy(ctx.pFd_sl[sl][:, 0:SUBW],
                                                     ctx.ps[bk + ec].ap()[:, 0:SUBW]), 1)
                            ctx.p2_bankfree[(gidx, bk + ec)] = (ctx.eV.h, ctx.eV.n)
                            if mat == 0:
                                e.wait_ge(ctx.eV.h, ctx.eV.n)
                                ec_g = ecg * ECG + ec
                                ctx.eV.inc(e.reduce_max(ctx.tmax.ap(),
                                                        ctx.pFd_sl[sl][:, 0:SUBW],
                                                        AX.X), 1)
                                e.wait_ge(ctx.eV.h, ctx.eV.n)
                                ctx.eV.inc(e.tensor_tensor(
                                    ctx.macc.ap()[:, ec_g:ec_g + 1],
                                    ctx.macc.ap()[:, ec_g:ec_g + 1],
                                    ctx.tmax.ap(), ALU.max), 1)
                            assert ctx.eV.n == eVb + (3 if mat == 0 else 1) * (ec + 1)
                    blk.vector(f_vecE)
                    ctx.p2_gdone[gidx] = (ctx.eV.n, ctx.eA.n)

                    def f_actE(e, mat=mat, ecg=ecg, sub=sub, eVb=eVb, grp=grp_done,
                               st_pred=st_pred):
                        k = 3 if mat == 0 else 1
                        for ec in range(ECG):
                            sl = ec % 2
                            e.wait_ge(ctx.eV.h, eVb + k * ec + 1)
                            ec_g = ecg * ECG + ec
                            ws = w[sl]
                            ws.dma(e.dma_start(
                                s_dst[mat][128 * ec_g:128 * (ec_g + 1),
                                           sub * SUBW:(sub + 1) * SUBW],
                                ctx.pFd_sl[sl][:, 0:SUBW]))
                            ctx.p2_st[(sub, mat, ecg, ec)] = ws.n
                            assert ws.n == st_pred(ec)
                    blk.scalar(f_actE)
                else:
                    eAb = ctx.eA.n

                    def f_actE2(e, ecg=ecg, sub=sub, grp_done=grp_done, eAb=eAb,
                                bk=bk, gidx=gidx):
                        e.wait_ge(ctx.eP.h, grp_done)
                        for ec in range(ECG):
                            sl = ec % 2
                            e.wait_ge(w[sl].h,
                                      ctx.p2_st.get((sub, 2, ecg, ec - 2), 0)
                                      if ec >= 2 else w[sl].n)
                            ctx.eA.inc(e.activation(ctx.pFd_sl[sl][:, 0:SUBW],
                                                    ctx.ps[bk + ec].ap()[:, 0:SUBW],
                                                    AF.Sigmoid), 1)
                            ctx.p2_bankfree[(gidx, bk + ec)] = (ctx.eA.h, ctx.eA.n)
                            e.wait_ge(ctx.eA.h, ctx.eA.n)
                            ec_g = ecg * ECG + ec
                            ws = w[sl]
                            ws.dma(e.dma_start(
                                ctx.s_sr[128 * ec_g:128 * (ec_g + 1),
                                         sub * SUBW:(sub + 1) * SUBW],
                                ctx.pFd_sl[sl][:, 0:SUBW]))
                            ctx.p2_st[(sub, 2, ecg, ec)] = ws.n
                            assert ctx.eA.n == eAb + ec + 1
                    blk.scalar(f_actE2)
                    ctx.p2_gdone[gidx] = (ctx.eV.n, ctx.eA.n)


def emit_P3(ctx):
    c = ctx.cfg
    blk = ctx.blk
    ND, W, SPAN, STRIP, NSTRIP = c["ND"], c["W"], c["SPAN"], c["STRIP"], c["NSTRIP"]
    r = [ctx.r0, ctx.r1]
    w = [ctx.w0, ctx.w1]
    ka_s = [ctx.pF_a.ap(), ctx.pF_a2.ap()]
    va_s = [ctx.pF_b.ap(), ctx.pF_b2.ap()]
    kvsem = [ctx.r0, ctx.dsb]
    vend = {}
    aend = {}
    sr_last = [0, 0]

    def f_neg(e):
        ctx.eV.inc(e.tensor_scalar(ctx.negm.ap(), ctx.macc.ap(), -1.0, None,
                                   ALU.mult), 1)
    blk.vector(f_neg)

    for ec in range(ND):
        cs = ec % 2
        ka, va = ka_s[cs], va_s[cs]
        kv_loaded = []
        vb = ctx.eV.n
        ab = ctx.eA.n
        wb = [ctx.w0.n, ctx.w1.n]
        a_kp = ab + 1

        def yh_done(si, a_kp=a_kp):
            return a_kp + si + 1

        def v_y(si, vb=vb):
            return vb + 1 + 3 + 6 * si + 6

        def sr_used(si, vb=vb):
            return vb + 1 + 3 + 6 * si + 5

        def yst_done(si, wb=wb):
            return wb[si % 2] + 32 * (si // 2 + 1)

        def f_sync(e, ec=ec, cs=cs, ka=ka, va=va, kv_loaded=kv_loaded):
            if ec >= 2:
                e.wait_ge(ctx.eV.h, vend[ec - 2])
                e.wait_ge(ctx.eA.h, aend[ec - 2])
            e.wait_ge(kvsem[cs].h, kvsem[cs].n)
            kvsem[cs].dma(e.dma_start(ka[:, 0:SPAN],
                                      ctx.s_k[128 * ec:128 * (ec + 1), :]))
            kvsem[cs].dma(e.dma_start(va[:, 0:SPAN],
                                      ctx.s_v[128 * ec:128 * (ec + 1), :]))
            kv_loaded.append(kvsem[cs].n)
        blk.sync(f_sync)

        sr_loaded = {}

        srl_prev = list(sr_last)

        def f_sync2(e, ec=ec, sr_used=sr_used, srl_prev=srl_prev):
            e.wait_ge(ctx.r1.h, ctx.r1.n)
            e.wait_ge(ctx.dsa.h, ctx.dsa.n)
            for si in range(NSTRIP):
                sl = si % 2
                ss = [ctx.r1, ctx.dsa][sl]
                e.wait_ge(ctx.eV.h,
                          sr_used(si - 2) if si >= 2 else srl_prev[sl])
                ss.dma(e.dma_start(ctx.pSc_sl[sl][:, 0:STRIP],
                                   ctx.s_sr[128 * ec:128 * (ec + 1),
                                            W + si * STRIP:W + (si + 1) * STRIP]))
                sr_loaded[si] = (ss, ss.n)
        blk.sync(f_sync2)
        for si in range(NSTRIP):
            sr_last[si % 2] = sr_used(si)

        def f_vec1(e, ec=ec, cs=cs, ka=ka, kv_loaded=kv_loaded):
            e.wait_ge(kvsem[cs].h, kv_loaded[0])
            ctx.eV.inc(e.tensor_tensor(ka[:, 0:W], ka[:, 0:W],
                                       ctx.maskneg.ap(), ALU.add), 1)
        blk.vector(f_vec1)
        v_mask = ctx.eV.n

        def f_act1(e, ec=ec, ka=ka, v_mask=v_mask, a_kp=a_kp):
            e.wait_ge(ctx.eV.h, v_mask)
            ctx.eA.inc(e.activation(ka[:, 0:SPAN], ka[:, 0:SPAN],
                                    AF.Exp, bias=ctx.negm.ap()[:, ec:ec + 1]), 1)
            assert ctx.eA.n == a_kp
        blk.scalar(f_act1)

        def f_vec2(e, ec=ec, cs=cs, ka=ka, va=va, sr_loaded=sr_loaded, a_kp=a_kp,
                   vb=vb, wb=wb, yh_done=yh_done, v_y=v_y, sr_used=sr_used,
                   yst_done=yst_done):
            e.wait_ge(ctx.eA.h, a_kp)
            ctx.eV.inc(e.tensor_tensor(va[:, 0:SPAN], ka[:, 0:SPAN],
                                       va[:, 0:SPAN], ALU.mult), 1)
            e.wait_ge(ctx.eV.h, ctx.eV.n)
            lam_b = ctx.lam16.ap()[:, ec:ec + 1].broadcast_to([128, SPAN])
            ctx.eV.inc(e.tensor_tensor_scan(ctx.pF_c.ap()[:, 0:SPAN], lam_b,
                                            va[:, 0:SPAN], 0.0,
                                            ALU.mult, ALU.add), 1)
            ctx.eV.inc(e.tensor_tensor_scan(ctx.pF_d.ap()[:, 0:SPAN], lam_b,
                                            ka[:, 0:SPAN], 0.0,
                                            ALU.mult, ALU.add), 1)
            e.wait_ge(ctx.eV.h, ctx.eV.n)
            eu_c = ctx.eu16.ap()[:, ec:ec + 1]
            for si in range(NSTRIP):
                sl = si % 2
                ks = W + si * STRIP
                if si >= 2:
                    e.wait_ge(ctx.eA.h, yh_done(si - 2))
                ctx.eV.inc(e.scalar_tensor_tensor(
                    ctx.pSa_sl[sl][:, 0:STRIP], va[:, ks:ks + STRIP],
                    eu_c, ctx.pF_c.ap()[:, ks - 1:ks - 1 + STRIP], ALU.mult, ALU.add), 1)
                ctx.eV.inc(e.scalar_tensor_tensor(
                    ctx.pSb_sl[sl][:, 0:STRIP], ka[:, ks:ks + STRIP],
                    eu_c, ctx.pF_d.ap()[:, ks - 1:ks - 1 + STRIP], ALU.mult, ALU.add), 1)
                e.wait_ge(ctx.eV.h, ctx.eV.n)
                ctx.eV.inc(e.reciprocal(ctx.pSb_sl[sl][:, 0:STRIP],
                                        ctx.pSb_sl[sl][:, 0:STRIP]), 1)
                e.wait_ge(ctx.eV.h, ctx.eV.n)
                ctx.eV.inc(e.tensor_tensor(ctx.pSa_sl[sl][:, 0:STRIP],
                                           ctx.pSa_sl[sl][:, 0:STRIP],
                                           ctx.pSb_sl[sl][:, 0:STRIP], ALU.mult), 1)
                e.wait_ge(ctx.eV.h, ctx.eV.n)
                e.wait_ge(sr_loaded[si][0].h, sr_loaded[si][1])
                ctx.eV.inc(e.tensor_tensor(ctx.pSa_sl[sl][:, 0:STRIP],
                                           ctx.pSa_sl[sl][:, 0:STRIP],
                                           ctx.pSc_sl[sl][:, 0:STRIP], ALU.mult), 1)
                assert ctx.eV.n == sr_used(si)
                e.wait_ge(ctx.eA.h, yh_done(si))
                e.wait_ge(w[sl].h, yst_done(si - 2) if si >= 2 else wb[sl])
                ctx.eV.inc(e.tensor_tensor(ctx.ylo_t[sl].ap()[:, 0:STRIP],
                                           ctx.pSa_sl[sl][:, 0:STRIP],
                                           ctx.yhi_t[sl].ap()[:, 0:STRIP],
                                           ALU.subtract), 1)
                assert ctx.eV.n == v_y(si)
        blk.vector(f_vec2)
        vend[ec] = ctx.eV.n

        def f_act2(e, ec=ec, a_kp=a_kp, wb=wb, yh_done=yh_done, v_y=v_y,
                   sr_used=sr_used, yst_done=yst_done):
            for si in range(NSTRIP):
                sl = si % 2
                e.wait_ge(ctx.eV.h, sr_used(si))
                e.wait_ge(w[sl].h, yst_done(si - 2) if si >= 2 else wb[sl])
                ctx.eA.inc(e.activation(ctx.yhi_t[sl].ap()[:, 0:STRIP],
                                        ctx.pSa_sl[sl][:, 0:STRIP], AF.Copy), 1)
                assert ctx.eA.n == yh_done(si)
                e.wait_ge(ctx.eA.h, ctx.eA.n)
                ws = w[sl]
                ws.dma(e.dma_start(ctx.s_y_hi[128 * ec:128 * (ec + 1),
                                              si * STRIP:(si + 1) * STRIP],
                                   ctx.yhi_t[sl].ap()[:, 0:STRIP]))
                e.wait_ge(ctx.eV.h, v_y(si))
                ws.dma(e.dma_start(ctx.s_y_lo[128 * ec:128 * (ec + 1),
                                              si * STRIP:(si + 1) * STRIP],
                                   ctx.ylo_t[sl].ap()[:, 0:STRIP]))
                assert ws.n == yst_done(si)
        blk.scalar(f_act2)
        aend[ec] = ctx.eA.n



def emit_P3b(ctx):
    c = ctx.cfg
    blk = ctx.blk
    ND, STRIP, NSTRIP, DCG, W = c["ND"], c["STRIP"], c["NSTRIP"], c["DCG"], c["W"]
    NG = ND // DCG
    WD = DCG * 128
    r = [ctx.r0, ctx.r1]
    rw = [ctx.r2, ctx.r3]
    w = [ctx.w0, ctx.w1]

    for sub in range(NSTRIP):
        y_loaded = []

        def f_syncY(e, sub=sub):
            e.wait_ge(ctx.eP.h, ctx.eP.n)
            e.wait_ge(ctx.r0.h, ctx.r0.n)
            for ec in range(ND):
                r[0].dma(e.dma_start(ctx.bfA.ap()[:, ec * STRIP:(ec + 1) * STRIP],
                                     ctx.s_y_hi[128 * ec:128 * (ec + 1),
                                                sub * STRIP:(sub + 1) * STRIP]))
                r[0].dma(e.dma_start(ctx.bfB.ap()[:, ec * STRIP:(ec + 1) * STRIP],
                                     ctx.s_y_lo[128 * ec:128 * (ec + 1),
                                                sub * STRIP:(sub + 1) * STRIP]))
            y_loaded.append(r[0].n)
        blk.sync(f_syncY)

        for dcg in range(NG):
            gidx = ctx.p3b_gidx
            ctx.p3b_gidx += 1
            bk = 0 if DCG * 2 > 8 else DCG * (gidx % 2)
            eP0 = ctx.eP.n
            wo_cnt = {}

            def f_syncW(e, dcg=dcg, eP0=eP0, wo_cnt=wo_cnt):
                e.wait_ge(ctx.r2.h, ctx.r2.n)
                e.wait_ge(ctx.r3.h, ctx.r3.n)
                for ec in range(ND):
                    sl = ec % 2
                    e.wait_ge(ctx.eP.h, eP0 + ec - 1 if ec >= 2 else eP0)
                    rw[sl].dma(e.dma_start(
                        ctx.wst_hi[sl].ap()[:, 0:WD],
                        ctx.s_wo_hi[128 * ec:128 * (ec + 1), dcg * WD:(dcg + 1) * WD]))
                    rw[sl].dma(e.dma_start(
                        ctx.wst_lo[sl].ap()[:, 0:WD],
                        ctx.s_wo_lo[128 * ec:128 * (ec + 1), dcg * WD:(dcg + 1) * WD]))
                    wo_cnt[ec] = rw[sl].n
            blk.sync(f_syncW)

            def f_pe(e, dcg=dcg, eP0=eP0, wo_cnt=wo_cnt, y_loaded=y_loaded,
                     gidx=gidx, bk=bk):
                e.wait_ge(r[0].h, y_loaded[0])
                prev = gidx - (2 if DCG * 2 <= 8 else 1)
                for ec in range(ND):
                    sl = ec % 2
                    e.wait_ge(rw[sl].h, wo_cnt[ec])
                    yh = ctx.bfA.ap()[:, ec * STRIP:(ec + 1) * STRIP]
                    yl = ctx.bfB.ap()[:, ec * STRIP:(ec + 1) * STRIP]
                    for dci in range(DCG):
                        if ec == 0:
                            v = ctx.p3b_bankfree.get((prev, bk + dci), 0)
                            e.wait_ge(ctx.eV.h, v)
                        wh = ctx.wst_hi[sl].ap()[:, 128 * dci:128 * (dci + 1)]
                        wl = ctx.wst_lo[sl].ap()[:, 128 * dci:128 * (dci + 1)]
                        e.matmul(ctx.ps[bk + dci].ap()[:, 0:STRIP], wh, yh,
                                 start=(ec == 0), stop=False)
                        e.matmul(ctx.ps[bk + dci].ap()[:, 0:STRIP], wh, yl,
                                 start=False, stop=False)
                        i = e.matmul(ctx.ps[bk + dci].ap()[:, 0:STRIP], wl, yh,
                                     start=False, stop=(ec == ND - 1))
                    ctx.eP.inc(i, 1)
                    assert ctx.eP.n == eP0 + ec + 1
            blk.tensor(f_pe)
            grp_done = ctx.eP.n

            xin_cnt = {}

            eVx = ctx.eV.n

            def f_syncX(e, sub=sub, dcg=dcg, xin_cnt=xin_cnt, eVx=eVx):
                e.wait_ge(ctx.dsb.h, ctx.dsb.n)
                e.wait_ge(ctx.dsc.h, ctx.dsc.n)
                for dci in range(DCG):
                    sl = dci % 2
                    xs = [ctx.dsb, ctx.dsc][sl]
                    e.wait_ge(ctx.eV.h, eVx + dci - 1 if dci >= 2 else eVx)
                    xs.dma(e.dma_start(
                        ctx.pSc_sl[sl][:, 0:STRIP],
                        ctx.in_xT[128 * (dcg * DCG + dci):128 * (dcg * DCG + dci + 1),
                                  W + 1 + sub * STRIP:W + 1 + (sub + 1) * STRIP]))
                    xin_cnt[dci] = (xs, xs.n)
            blk.sync(f_syncX)

            eV0 = ctx.eV.n
            wb3 = [ctx.w0.n, ctx.w1.n]

            def stp3(dci, wb3=wb3):
                return wb3[dci % 2] + 16 * (dci // 2 + 1)

            def f_vecE(e, sub=sub, dcg=dcg, grp_done=grp_done, xin_cnt=xin_cnt,
                       eV0=eV0, wb3=wb3, stp3=stp3, bk=bk):
                e.wait_ge(ctx.eP.h, grp_done)
                for dci in range(DCG):
                    sl = dci % 2
                    dc_g = dcg * DCG + dci
                    e.wait_ge(xin_cnt[dci][0].h, xin_cnt[dci][1])
                    e.wait_ge(ctx.w0.h if sl == 0 else ctx.w1.h,
                              stp3(dci - 2) if dci >= 2 else wb3[sl])
                    ctx.eV.inc(e.scalar_tensor_tensor(
                        ctx.pSa_sl[sl][:, 0:STRIP], ctx.ps[bk + dci].ap()[:, 0:STRIP],
                        ctx.ga_cl.ap()[:, dc_g:dc_g + 1], ctx.pSc_sl[sl][:, 0:STRIP],
                        ALU.mult, ALU.add), 1)
                    ctx.p3b_bankfree[(gidx, bk + dci)] = ctx.eV.n
                    assert ctx.eV.n == eV0 + dci + 1
            blk.vector(f_vecE)
            ctx.p3b_gdone[gidx] = ctx.eV.n

            def f_actS(e, sub=sub, dcg=dcg, eV0=eV0, stp3=stp3):
                for dci in range(DCG):
                    sl = dci % 2
                    dc_g = dcg * DCG + dci
                    e.wait_ge(ctx.eV.h, eV0 + dci + 1)
                    ws = w[sl]
                    ws.dma(e.dma_start(
                        ctx.s_x1[128 * dc_g:128 * (dc_g + 1),
                                 sub * STRIP:(sub + 1) * STRIP],
                        ctx.pSa_sl[sl][:, 0:STRIP]))
                    assert ws.n == stp3(dci)
            blk.scalar(f_actS)


class QuantStream:

    def __init__(self, ctx, items, tag):
        self.ctx = ctx
        self.items = items
        self.tag = tag
        self.pos = 0

    def emit_batch(self, k):
        ctx = self.ctx
        blk = ctx.blk
        its = list(range(self.pos, min(self.pos + k, len(self.items))))
        self.pos = min(self.pos + k, len(self.items))
        if not its:
            return
        items = self.items
        r = [ctx.r2, ctx.r3]
        w = [ctx.w0, ctx.w1]
        eA0, eV0, eP0, eG0 = ctx.eA.n, ctx.eV.n, ctx.eP.n, ctx.eG.n
        wb = [ctx.w0.n, ctx.w1.n]
        base = its[0]
        loaded = {}

        def st_pred(j, wb=wb, base=base):
            jj = j - base
            return wb[jj % 2] + 16 * (jj // 2 + 1)

        def f_sync(e):
            e.wait_ge(ctx.r2.h, ctx.r2.n)
            e.wait_ge(ctx.r3.h, ctx.r3.n)
            for n_, it in enumerate(its):
                sl = it % 2
                e.wait_ge(ctx.eA.h, eA0 + 2 * (n_ - 1) if n_ >= 2 else eA0)
                srcs, dst = items[it]
                for (ap, co, wd) in srcs:
                    r[sl].dma(e.dma_start(ctx.wgublk[sl].ap()[:, co:co + wd], ap))
                loaded[it] = r[sl].n
        blk.sync(f_sync)

        def f_act(e):
            def store(j):
                jl = j % 2
                TQj = sum(x[2] for x in items[j][0])
                e.wait_ge(ctx.eV.h, eV0 + 6 * (j - base + 1))
                w[jl].dma(e.dma_start(items[j][1], ctx.q_l[jl].ap()[:, 0:TQj]))
                assert w[jl].n == st_pred(j)
            for n_, it in enumerate(its):
                sl = it % 2
                TQ = sum(x[2] for x in items[it][0])
                if n_ >= 1:
                    store(its[n_ - 1])
                e.wait_ge(r[sl].h, loaded[it])
                e.wait_ge(ctx.eP.h, eP0 + n_ if n_ >= 1 else eP0)
                e.wait_ge(ctx.eV.h, eV0 + 6 * (n_ - 1) if n_ >= 2 else eV0)
                e.wait_ge(ctx.eG.h, eG0 + 2 * (n_ - 1) if n_ >= 2 else eG0)
                ctx.eA.inc(e.activation(ctx.q_wbf[sl].ap()[:, 0:TQ],
                                        ctx.wgublk[sl].ap()[:, 0:TQ], AF.Copy), 1)
                ctx.eA.inc(e.activation(ctx.q_wab[sl].ap()[:, 0:TQ],
                                        ctx.wgublk[sl].ap()[:, 0:TQ], AF.Abs), 1)
                assert ctx.eA.n == eA0 + 2 * (n_ + 1)
            store(its[-1])
        blk.scalar(f_act)

        def f_pe(e):
            for n_, it in enumerate(its):
                sl = it % 2
                TQ = sum(x[2] for x in items[it][0])
                e.wait_ge(ctx.eA.h, eA0 + 2 * (n_ + 1))
                e.wait_ge(ctx.eV.h, eV0 + 6 * (n_ - 2) + 1 if n_ >= 2 else eV0)
                ctx.eP.inc(e.matmul(ctx.ps[sl].ap()[:, 0:TQ], ctx.g2.ap(),
                                    ctx.q_wab[sl].ap()[:, 0:TQ],
                                    start=True, stop=True), 1)
                assert ctx.eP.n == eP0 + n_ + 1
        blk.tensor(f_pe)

        def f_vec(e):
            for n_, it in enumerate(its):
                sl = it % 2
                TQ = sum(x[2] for x in items[it][0])
                wbf = ctx.q_wbf[sl].ap()[:, 0:TQ]
                sb_ = ctx.q_sb[sl].ap()[:, 0:TQ]
                e.wait_ge(ctx.eP.h, eP0 + n_ + 1)
                e.wait_ge(w[sl].h, st_pred(its[n_ - 2]) if n_ >= 2 else wb[sl])
                e.wait_ge(ctx.eG.h, eG0 + 2 * n_ if n_ >= 1 else eG0)
                ctx.eV.inc(e.tensor_scalar(sb_, ctx.ps[sl].ap()[:, 0:TQ], 1e-8,
                                           None, ALU.max), 1)
                e.wait_ge(ctx.eV.h, ctx.eV.n)
                ctx.eV.inc(e.scalar_tensor_tensor(ctx.q_a[sl].ap()[:, 0:TQ], wbf,
                                                  2.0, sb_, ALU.mult, ALU.is_gt), 1)
                ctx.eV.inc(e.scalar_tensor_tensor(ctx.q_b[sl].ap()[:, 0:TQ], wbf,
                                                  -2.0, sb_, ALU.mult, ALU.is_gt), 1)
                e.wait_ge(ctx.eG.h, eG0 + 2 * n_ + 1)
                ctx.eV.inc(e.tensor_tensor(ctx.q_wh[sl].ap()[:, 0:TQ],
                                           ctx.q_q[sl].ap()[:, 0:TQ], sb_,
                                           ALU.mult), 1)
                e.wait_ge(ctx.eG.h, eG0 + 2 * (n_ + 1))
                ctx.eV.inc(e.tensor_tensor(ctx.q_fwd[sl].ap()[:, 0:TQ], wbf,
                                           ctx.q_d[sl].ap()[:, 0:TQ], ALU.add), 1)
                e.wait_ge(ctx.eV.h, ctx.eV.n)
                ctx.eV.inc(e.tensor_copy(ctx.q_l[sl].ap()[:, 0:TQ].bitcast(U16),
                                         ctx.q_fwd[sl].ap()[:, 0:TQ].bitcast(U16)), 1)
                assert ctx.eV.n == eV0 + 6 * (n_ + 1)
        blk.vector(f_vec)

        def f_gp(e):
            for n_, it in enumerate(its):
                sl = it % 2
                TQ = sum(x[2] for x in items[it][0])
                wbf = ctx.q_wbf[sl].ap()[:, 0:TQ]
                e.wait_ge(ctx.eV.h, eV0 + 6 * n_ + 3)
                ctx.eG.inc(e.tensor_tensor(ctx.q_q[sl].ap()[:, 0:TQ],
                                           ctx.q_a[sl].ap()[:, 0:TQ],
                                           ctx.q_b[sl].ap()[:, 0:TQ],
                                           ALU.subtract), 1)
                e.wait_ge(ctx.eV.h, eV0 + 6 * n_ + 4)
                e.wait_ge(ctx.eG.h, ctx.eG.n)
                ctx.eG.inc(e.tensor_tensor(ctx.q_d[sl].ap()[:, 0:TQ],
                                           ctx.q_wh[sl].ap()[:, 0:TQ], wbf,
                                           ALU.subtract), 1)
                assert ctx.eG.n == eG0 + 2 * (n_ + 1)
        blk.gpsimd(f_gp)

    def emit_rest(self):
        while self.pos < len(self.items):
            self.emit_batch(32)


def emit_P4b(ctx):
    c = ctx.cfg
    blk = ctx.blk
    ND, STRIP, NSTRIP, NB4, TC, H = (c["ND"], c["STRIP"], c["NSTRIP"], c["NB4"],
                                     c["TC"], c["H"])
    r = [ctx.r0, ctx.r1]
    rw = [ctx.r2, ctx.r3]
    w = [ctx.w0, ctx.w1]

    def f_vz(e):
        ctx.eV.inc(e.memset(ctx.racc_pw.ap(), 0.0), 1)
    blk.vector(f_vz)
    racc_zeroed = ctx.eV.n

    for sub in range(NSTRIP):
        h2_loaded = []

        def f_syncH(e, sub=sub):
            e.wait_ge(ctx.eP.h, ctx.eP.n)
            e.wait_ge(ctx.r0.h, ctx.r0.n)
            for dc in range(ND):
                r[0].dma(e.dma_start(ctx.bfA.ap()[:, dc * STRIP:(dc + 1) * STRIP],
                                     ctx.s_h2_hi[128 * dc:128 * (dc + 1),
                                                 sub * STRIP:(sub + 1) * STRIP]))
                r[0].dma(e.dma_start(ctx.bfB.ap()[:, dc * STRIP:(dc + 1) * STRIP],
                                     ctx.s_h2_lo[128 * dc:128 * (dc + 1),
                                                 sub * STRIP:(sub + 1) * STRIP]))
            h2_loaded.append(r[0].n)
        blk.sync(f_syncH)

        for band in range(NB4):
            bk = 4 * (band % 2)
            eP0 = ctx.eP.n
            q_cnt = {}

            def f_syncQ(e, sub=sub, band=band, eP0=eP0, q_cnt=q_cnt):
                e.wait_ge(ctx.r2.h, ctx.r2.n)
                e.wait_ge(ctx.r3.h, ctx.r3.n)
                for kd in range(ND):
                    sl = kd % 2
                    e.wait_ge(ctx.eP.h, eP0 + kd - 1 if kd >= 2 else eP0)
                    rw[sl].dma(e.dma_start(
                        ctx.qst[sl].ap()[:, 0:512],
                        ctx.s_qgu[band, kd, :, :]))
                    q_cnt[kd] = rw[sl].n
            blk.sync(f_syncQ)

            def f_pe(e, sub=sub, band=band, bk=bk, eP0=eP0, q_cnt=q_cnt,
                     h2_loaded=h2_loaded):
                e.wait_ge(r[0].h, h2_loaded[0])
                e.wait_ge(ctx.eV.h, ctx.p4_ev.get((sub, band - 2), ctx.eV.n))
                e.wait_ge(ctx.eA.h, ctx.p4_ea.get((sub, band - 2), ctx.eA.n))
                for kd in range(ND):
                    sl = kd % 2
                    e.wait_ge(rw[sl].h, q_cnt[kd])
                    xh = ctx.bfA.ap()[:, kd * STRIP:(kd + 1) * STRIP]
                    xl = ctx.bfB.ap()[:, kd * STRIP:(kd + 1) * STRIP]
                    for hc in range(4):
                        wq = ctx.qst[sl].ap()[:, 128 * hc:128 * (hc + 1)]
                        e.matmul(ctx.ps[bk + hc].ap()[:, 0:STRIP], wq, xh,
                                 start=(kd == 0), stop=False)
                        i = e.matmul(ctx.ps[bk + hc].ap()[:, 0:STRIP], wq, xl,
                                     start=False, stop=(kd == ND - 1))
                    ctx.eP.inc(i, 1)
                    assert ctx.eP.n == eP0 + kd + 1
            blk.tensor(f_pe)
            grp_done = ctx.eP.n

            eA0, eV0 = ctx.eA.n, ctx.eV.n
            wbs = [ctx.w0.n, ctx.w1.n]

            def hst_done(j):
                return wbs[j % 2] + 32 * (j // 2 + 1)

            def f_actE(e, sub=sub, band=band, bk=bk, grp_done=grp_done,
                       eA0=eA0, eV0=eV0, wbs=wbs, hst_done=hst_done):
                e.wait_ge(ctx.eP.h, grp_done)
                for j in range(2):
                    sl = j % 2
                    e.wait_ge(ctx.eV.h, ctx.p4_sga.get((sub, band - 1, j), eV0))
                    ctx.eA.inc(e.activation(ctx.pSa_sl[sl][:, 0:STRIP],
                                            ctx.ps[bk + j].ap()[:, 0:STRIP],
                                            AF.Silu), 1)
                    e.wait_ge(ctx.eV.h, eV0 + 3 * j + 1)
                    e.wait_ge(w[sl].h, wbs[sl])
                    ctx.eA.inc(e.activation(ctx.yhi_t[sl].ap()[:, 0:STRIP],
                                            ctx.pSb_sl[sl][:, 0:STRIP], AF.Copy), 1)
                    e.wait_ge(ctx.eA.h, ctx.eA.n)
                    hrow = 128 * (2 * band + j)
                    ws = w[sl]
                    ws.dma(e.dma_start(ctx.s_hm_hi[hrow:hrow + 128,
                                                   sub * STRIP:(sub + 1) * STRIP],
                                       ctx.yhi_t[sl].ap()[:, 0:STRIP]))
                    e.wait_ge(ctx.eV.h, eV0 + 3 * j + 2)
                    ws.dma(e.dma_start(ctx.s_hm_lo[hrow:hrow + 128,
                                                   sub * STRIP:(sub + 1) * STRIP],
                                       ctx.ylo_t[sl].ap()[:, 0:STRIP]))
                    assert ws.n == hst_done(j)
                    e.wait_ge(ctx.eV.h, ctx.p4_sqv.get((sub, band - 1, j), eV0))
                    ctx.eA.inc(e.activation(ctx.pSc_sl[sl][:, 0:STRIP],
                                            ctx.pSb_sl[sl][:, 0:STRIP], AF.Square), 1)
                    assert ctx.eA.n == eA0 + 3 * (j + 1)
            blk.scalar(f_actE)
            ctx.p4_ea[(sub, band)] = ctx.eA.n

            def f_vecE(e, sub=sub, band=band, bk=bk, eA0=eA0, eV0=eV0,
                       racc_zeroed=racc_zeroed):
                for j in range(2):
                    sl = j % 2
                    e.wait_ge(ctx.eA.h, eA0 + 3 * j + 1)
                    e.wait_ge(ctx.eA.h, ctx.p4_hmfree.get((sub, band - 1, j), eA0))
                    ctx.eV.inc(e.tensor_tensor(ctx.pSb_sl[sl][:, 0:STRIP],
                                               ctx.pSa_sl[sl][:, 0:STRIP],
                                               ctx.ps[bk + 2 + j].ap()[:, 0:STRIP],
                                               ALU.mult), 1)
                    ctx.p4_sga[(sub, band, j)] = ctx.eV.n
                    e.wait_ge(ctx.eA.h, eA0 + 3 * j + 2)
                    e.wait_ge(w[sl].h, hst_done(j - 2) if j >= 2 else wbs[sl])
                    ctx.eV.inc(e.tensor_tensor(ctx.ylo_t[sl].ap()[:, 0:STRIP],
                                               ctx.pSb_sl[sl][:, 0:STRIP],
                                               ctx.yhi_t[sl].ap()[:, 0:STRIP],
                                               ALU.subtract), 1)
                    e.wait_ge(ctx.eA.h, eA0 + 3 * (j + 1))
                    e.wait_ge(ctx.eV.h, max(ctx.eV.n, racc_zeroed))
                    ctx.eV.inc(e.tensor_tensor(
                        ctx.racc_pw.ap()[:, sub * STRIP:(sub + 1) * STRIP],
                        ctx.racc_pw.ap()[:, sub * STRIP:(sub + 1) * STRIP],
                        ctx.pSc_sl[sl][:, 0:STRIP], ALU.add), 1)
                    ctx.p4_sqv[(sub, band, j)] = ctx.eV.n
                    assert ctx.eV.n == eV0 + 3 * (j + 1)
            blk.vector(f_vecE)
            ctx.p4_ev[(sub, band)] = ctx.eV.n
            ctx.p4_hmfree[(sub, band, 0)] = eA0 + 3
            ctx.p4_hmfree[(sub, band, 1)] = eA0 + 6

    eVr = ctx.eV.n

    def f_peR(e, eVr=eVr):
        e.wait_ge(ctx.eV.h, eVr)
        e.wait_ge(ctx.eA.h, ctx.eA.n)
        for sub in range(NSTRIP):
            ctx.eP.inc(e.matmul(ctx.ps[sub].ap()[0:1, 0:STRIP], ctx.ones128.ap(),
                                ctx.racc_pw.ap()[:, sub * STRIP:(sub + 1) * STRIP],
                                start=True, stop=True), 1)
    blk.tensor(f_peR)
    pR = ctx.eP.n

    def f_vecR(e, pR=pR):
        e.wait_ge(ctx.eP.h, pR)
        for sub in range(NSTRIP):
            ctx.eV.inc(e.tensor_copy(
                ctx.rows1.ap()[:, sub * STRIP:(sub + 1) * STRIP],
                ctx.ps[sub].ap()[0:1, 0:STRIP]), 1)
    blk.vector(f_vecR)
    ctx.p4_gp_done = 0


def emit_P5b(ctx):
    c = ctx.cfg
    blk = ctx.blk
    ND, NH, STRIP, NSTRIP, TC, H, DCG = (c["ND"], c["NH"], c["STRIP"],
                                         c["NSTRIP"], c["TC"], c["H"], c["DCG"])
    NG = ND // DCG
    WD = DCG * 128
    r = [ctx.r0, ctx.r1]
    rw = [ctx.r2, ctx.r3]
    w = [ctx.w0, ctx.w1]

    def f_vI(e):
        e.wait_ge(ctx.eV.h, ctx.eV.n)
        ctx.eV.inc(e.tensor_scalar(ctx.rows1.ap()[:, 0:TC], ctx.rows1.ap()[:, 0:TC],
                                   1.0 / H, None, ALU.mult), 1)
    blk.vector(f_vI)

    def f_aI(e):
        e.wait_ge(ctx.eV.h, ctx.eV.n)
        ctx.eA.inc(e.activation(ctx.rows1.ap()[:, 0:TC], ctx.rows1.ap()[:, 0:TC],
                                AF.Sqrt, bias=ctx.epsrms.ap()), 1)
    blk.scalar(f_aI)

    def f_vI2(e):
        e.wait_ge(ctx.eA.h, ctx.eA.n)
        ctx.eV.inc(e.reciprocal(ctx.rows1.ap()[:, 0:TC], ctx.rows1.ap()[:, 0:TC]), 1)
    blk.vector(f_vI2)
    inv_ready = ctx.eV.n

    for sub in range(NSTRIP):
        def f_peI(e, sub=sub):
            e.wait_ge(ctx.eV.h, ctx.eV.n)
            e.wait_ge(ctx.eP.h, ctx.eP.n)
            ctx.eP.inc(e.matmul(ctx.ps[7].ap()[:, 0:STRIP], ctx.onesrow.ap(),
                                ctx.rows1.ap()[:, sub * STRIP:(sub + 1) * STRIP],
                                start=True, stop=True), 1)
        blk.tensor(f_peI)
        p_invb = ctx.eP.n

        def f_vInvb(e, sub=sub, p_invb=p_invb):
            e.wait_ge(ctx.eP.h, p_invb)
            e.wait_ge(ctx.eV.h, ctx.eV.n)
            ctx.eV.inc(e.tensor_copy(ctx.invb.ap()[:, 0:STRIP],
                                     ctx.ps[7].ap()[:, 0:STRIP]), 1)
        blk.vector(f_vInvb)
        ctx.p5_invb_free[sub] = ctx.eV.n
        invb_ready = ctx.eV.n

        for dch in range(NG):
            gidx = ctx.p5_gidx
            ctx.p5_gidx += 1
            bk5 = 0 if DCG * 2 > 8 else DCG * (gidx % 2)
            eP0 = ctx.eP.n
            ld_cnt = {}

            def f_syncL(e, sub=sub, dch=dch, eP0=eP0, ld_cnt=ld_cnt):
                for x_ in (ctx.r0, ctx.r1, ctx.r2, ctx.r3):
                    e.wait_ge(x_.h, x_.n)
                for hk in range(NH):
                    sl = hk % 2
                    e.wait_ge(ctx.eP.h, eP0 + hk - 1 if hk >= 2 else eP0)
                    rw[sl].dma(e.dma_start(
                        ctx.qst[sl].ap()[:, 0:WD],
                        ctx.s_qp[hk, :, dch * WD:(dch + 1) * WD]))
                    r[sl].dma(e.dma_start(
                        ctx.hm_hi[sl].ap()[:, 0:STRIP],
                        ctx.s_hm_hi[128 * hk:128 * (hk + 1),
                                    sub * STRIP:(sub + 1) * STRIP]))
                    r[sl].dma(e.dma_start(
                        ctx.hm_lo[sl].ap()[:, 0:STRIP],
                        ctx.s_hm_lo[128 * hk:128 * (hk + 1),
                                    sub * STRIP:(sub + 1) * STRIP]))
                    ld_cnt[hk] = (rw[sl].n, r[sl].n)
            blk.sync(f_syncL)

            def f_pe(e, sub=sub, dch=dch, eP0=eP0, ld_cnt=ld_cnt,
                     invb_ready=invb_ready, gidx=gidx, bk5=bk5):
                e.wait_ge(ctx.eV.h, invb_ready)
                prev = gidx - (2 if DCG * 2 <= 8 else 1)
                for hk in range(NH):
                    sl = hk % 2
                    e.wait_ge(rw[sl].h, ld_cnt[hk][0])
                    e.wait_ge(r[sl].h, ld_cnt[hk][1])
                    for dci in range(DCG):
                        if hk == 0:
                            v = ctx.p5_bankfree.get((prev, bk5 + dci), 0)
                            e.wait_ge(ctx.eV.h, v)
                        wq = ctx.qst[sl].ap()[:, 128 * dci:128 * (dci + 1)]
                        e.matmul(ctx.ps[bk5 + dci].ap()[:, 0:STRIP], wq,
                                 ctx.hm_hi[sl].ap()[:, 0:STRIP],
                                 start=(hk == 0), stop=False)
                        i = e.matmul(ctx.ps[bk5 + dci].ap()[:, 0:STRIP], wq,
                                     ctx.hm_lo[sl].ap()[:, 0:STRIP],
                                     start=False, stop=(hk == NH - 1))
                    ctx.eP.inc(i, 1)
                    assert ctx.eP.n == eP0 + hk + 1
            blk.tensor(f_pe)
            grp_done = ctx.eP.n

            x1_cnt = {}

            eVx = ctx.eV.n

            def f_syncX(e, sub=sub, dch=dch, x1_cnt=x1_cnt, eVx=eVx):
                e.wait_ge(ctx.dsb.h, ctx.dsb.n)
                e.wait_ge(ctx.dsc.h, ctx.dsc.n)
                for dci in range(DCG):
                    sl = dci % 2
                    xs = [ctx.dsb, ctx.dsc][sl]
                    e.wait_ge(ctx.eV.h, eVx + 2 * (dci - 1) if dci >= 2 else eVx)
                    xs.dma(e.dma_start(
                        ctx.pSc_sl[sl][:, 0:STRIP],
                        ctx.s_x1[128 * (dch * DCG + dci):128 * (dch * DCG + dci + 1),
                                 sub * STRIP:(sub + 1) * STRIP]))
                    x1_cnt[dci] = (xs, xs.n)
            blk.sync(f_syncX)

            eV0 = ctx.eV.n
            obs = [ctx.out_sem.n, ctx.out2.n]
            osm = [ctx.out_sem, ctx.out2]

            def ostp(dci, obs=obs):
                return obs[dci % 2] + 16 * (dci // 2 + 1)

            def f_vecE(e, sub=sub, dch=dch, grp_done=grp_done, x1_cnt=x1_cnt,
                       eV0=eV0, invb_ready=invb_ready, obs=obs, osm=osm, ostp=ostp,
                       bk5=bk5):
                e.wait_ge(ctx.eP.h, grp_done)
                e.wait_ge(ctx.eV.h, invb_ready)
                for dci in range(DCG):
                    sl = dci % 2
                    dc_g = dch * DCG + dci
                    e.wait_ge(x1_cnt[dci][0].h, x1_cnt[dci][1])
                    ctx.eV.inc(e.tensor_tensor(ctx.pSa_sl[sl][:, 0:STRIP],
                                               ctx.ps[bk5 + dci].ap()[:, 0:STRIP],
                                               ctx.invb.ap()[:, 0:STRIP],
                                               ALU.mult), 1)
                    ctx.p5_bankfree[(gidx, bk5 + dci)] = ctx.eV.n
                    e.wait_ge(ctx.eV.h, ctx.eV.n)
                    e.wait_ge(osm[sl].h,
                              ostp(dci - 2) if dci >= 2 else obs[sl])
                    ctx.eV.inc(e.scalar_tensor_tensor(
                        ctx.pSb_sl[sl][:, 0:STRIP], ctx.pSa_sl[sl][:, 0:STRIP],
                        ctx.gb_cl.ap()[:, dc_g:dc_g + 1], ctx.pSc_sl[sl][:, 0:STRIP],
                        ALU.mult, ALU.add), 1)
                    assert ctx.eV.n == eV0 + 2 * (dci + 1)
            blk.vector(f_vecE)
            ctx.p5_gdone[gidx] = ctx.eV.n

            def f_actS(e, sub=sub, dch=dch, eV0=eV0, obs=obs, osm=osm, ostp=ostp):
                for dci in range(DCG):
                    sl = dci % 2
                    dc_g = dch * DCG + dci
                    e.wait_ge(ctx.eV.h, eV0 + 2 * (dci + 1))
                    e.wait_ge(osm[sl].h,
                              ostp(dci - 2) if dci >= 2 else obs[sl])
                    osm[sl].dma(e.dma_start(
                        ctx.out_ap[128 * dc_g:128 * (dc_g + 1),
                                   sub * STRIP:(sub + 1) * STRIP],
                        ctx.pSb_sl[sl][:, 0:STRIP]))
                    assert osm[sl].n == ostp(dci)
            blk.scalar(f_actS)


def emit_fin(ctx):
    c = ctx.cfg

    def f_fin(e):
        e.wait_ge(ctx.out_sem.h, ctx.out_sem.n)
        e.wait_ge(ctx.out2.h, ctx.out2.n)
    ctx.blk.scalar(f_fin)

    def f_fin2(e):
        e.wait_ge(ctx.out_sem.h, ctx.out_sem.n)
        e.wait_ge(ctx.out2.h, ctx.out2.n)
    ctx.blk.gpsimd(f_fin2)


def build_nc(cfg, debug_scratch=(), phases=99):
    c = cfg
    D, TC, W, H = c["D"], c["TC"], c["W"], c["H"]
    ND, NH, NMOD, SPAN, SUBW, STRIP = (c["ND"], c["NH"], c["NMOD"], c["SPAN"],
                                       c["SUBW"], c["STRIP"])
    NB4 = c["NB4"]
    nc = bass.Bass("TRN2", target_bir_lowering=False)
    ctx = Ctx()
    ctx.nc = nc
    ctx.cfg = cfg

    def dram(name, shape, dt, kind="Internal"):
        if name in debug_scratch:
            kind = "ExternalOutput"
        return nc.dram_tensor(name, shape, dt, kind=kind).ap()

    ctx.in_xT = dram("xT", [D, c["HCOLS"]], F32, "ExternalInput")
    ctx.in_wkT = dram("wkT", [D, D], F32, "ExternalInput")
    ctx.in_wvT = dram("wvT", [D, D], F32, "ExternalInput")
    ctx.in_wrT = dram("wrT", [D, D], F32, "ExternalInput")
    ctx.in_woT = dram("woT", [D, D], F32, "ExternalInput")
    ctx.in_wguT = dram("wguT", [D, 2 * H], F32, "ExternalInput")
    ctx.in_wprT = dram("wprT", [H, D], F32, "ExternalInput")
    ctx.in_wmod = dram("wmod_t", [NMOD, 128, ND * 128], F32, "ExternalInput")
    ctx.in_pvecs = dram("pvecs", [128, 6 * ND], F32, "ExternalInput")
    ctx.in_wmask = dram("wmask", [1, SPAN], F32, "ExternalInput")
    ctx.out_ap = dram("outT", [D, TC], F32, "ExternalOutput")

    ctx.s_wk_hi = dram("s_wk_hi", [D, D], BF16)
    ctx.s_wk_lo = dram("s_wk_lo", [D, D], BF16)
    ctx.s_wv_hi = dram("s_wv_hi", [D, D], BF16)
    ctx.s_wv_lo = dram("s_wv_lo", [D, D], BF16)
    ctx.s_wr_hi = dram("s_wr_hi", [D, D], BF16)
    ctx.s_wr_lo = dram("s_wr_lo", [D, D], BF16)
    ctx.s_wo_hi = dram("s_wo_hi", [D, D], BF16)
    ctx.s_wo_lo = dram("s_wo_lo", [D, D], BF16)
    ctx.s_h_hi = dram("s_h_hi", [D, c["HCOLS"]], BF16)
    ctx.s_h_lo = dram("s_h_lo", [D, c["HCOLS"]], BF16)
    ctx.s_k = dram("s_k", [D, SPAN], F32)
    ctx.s_v = dram("s_v", [D, SPAN], F32)
    ctx.s_sr = dram("s_sr", [D, SPAN], F32)
    ctx.s_y_hi = dram("s_y_hi", [D, TC], BF16)
    ctx.s_y_lo = dram("s_y_lo", [D, TC], BF16)
    ctx.s_x1 = dram("s_x1", [D, TC], F32)
    ctx.s_h2_hi = dram("s_h2_hi", [D, TC], BF16)
    ctx.s_h2_lo = dram("s_h2_lo", [D, TC], BF16)
    ctx.s_qgu = dram("s_qgu", [NB4, ND, 128, 512], BF16)
    ctx.s_hm_hi = dram("s_hm_hi", [H, TC], BF16)
    ctx.s_hm_lo = dram("s_hm_lo", [H, TC], BF16)
    ctx.s_qp = dram("s_qp", [NH, 128, D], BF16)

    SW = max(SUBW + 1, STRIP)
    PFW = max(SPAN, 2 * SW)
    BFW = max(ND * (2 * SUBW + 1) + 2, ND * STRIP)
    WSTW = c["ECG"] * 128
    QSTW = max(512, c["DCG"] * 128)

    def sb(name, shape, dt):
        return nc.alloc_sbuf_tensor(name, shape, dt)

    ctx.vecs = sb("vecs", [128, 6 * ND], F32)
    ctx.mod_sb = sb("mod_sb", [128, NMOD], F32)
    for nm in ["c1a", "c1b", "ga_cl", "gb_cl", "tmpg", "tmpg2", "silu_c", "tw",
               "eu16", "lam16", "negm", "macc"]:
        setattr(ctx, nm, sb(nm, [128, ND], F32))
    ctx.tm1m = [sb(f"tm1m{i}", [128, ND], F32) for i in range(3)]
    ctx.zs = sb("zs", [128, ND], F32)
    ctx.ones128 = sb("ones128", [128, 1], F32)
    ctx.tmax = sb("tmax", [128, 1], F32)
    ctx.maskneg = sb("maskneg", [128, W], F32)
    ctx.pF_a = sb("pF_a", [128, PFW], F32)
    ctx.pF_b = sb("pF_b", [128, PFW], F32)
    ctx.pF_a2 = sb("pF_a2", [128, c["SPAN"]], F32)
    ctx.pF_b2 = sb("pF_b2", [128, c["SPAN"]], F32)
    ctx.pF_c = sb("pF_c", [128, PFW], F32)
    ctx.pF_d = sb("pF_d", [128, PFW], F32)
    ctx.pFa_sl = [ctx.pF_a.ap()[:, i * SW:(i + 1) * SW] for i in range(2)]
    ctx.pFb_sl = [ctx.pF_b.ap()[:, i * SW:(i + 1) * SW] for i in range(2)]
    ctx.pFc_sl = [ctx.pF_c.ap()[:, i * SW:(i + 1) * SW] for i in range(2)]
    ctx.pFd_sl = [ctx.pF_d.ap()[:, i * SW:(i + 1) * SW] for i in range(2)]
    ctx.pS_a = sb("pS_a", [128, 2 * STRIP], F32)
    ctx.pS_b = sb("pS_b", [128, 2 * STRIP], F32)
    ctx.pS_c = sb("pS_c", [128, 2 * STRIP], F32)
    ctx.pSa_sl = [ctx.pS_a.ap()[:, i * STRIP:(i + 1) * STRIP] for i in range(2)]
    ctx.pSb_sl = [ctx.pS_b.ap()[:, i * STRIP:(i + 1) * STRIP] for i in range(2)]
    ctx.pSc_sl = [ctx.pS_c.ap()[:, i * STRIP:(i + 1) * STRIP] for i in range(2)]
    ctx.wblkF = [sb(f"wblkF{i}", [128, max(D, ND * 128)], F32) for i in range(2)]
    ctx.slab = ctx.wblkF
    ctx.wgublk = [sb(f"wgublk{i}", [128, 512], F32) for i in range(2)]
    ctx.invb = sb("invb", [128, STRIP], F32)
    ctx.rows1 = sb("rows1", [1, TC], F32)
    ctx.rows2 = sb("rows2", [1, 3 * 512], F32)
    ctx.onesrow = sb("onesrow", [1, 128], F32)
    ctx.mrow = sb("mrow", [1, W], F32)
    ctx.epsln = sb("epsln", [1, 1], F32)
    ctx.epsrms = sb("epsrms", [1, 1], F32)
    ctx.racc_pw = sb("racc_pw", [128, TC], F32)

    ctx.bfA = sb("bfA", [128, BFW], BF16)
    ctx.bfB = sb("bfB", [128, BFW], BF16)
    ctx.XOFF = ND * (SUBW + 1)
    ctx.wst_hi = [sb(f"wsth{i}", [128, WSTW], BF16) for i in range(2)]
    ctx.wst_lo = [sb(f"wstl{i}", [128, WSTW], BF16) for i in range(2)]
    ctx.whi_t = [ctx.bfA.ap()[:, i * D:(i + 1) * D] for i in range(2)]
    ctx.wlo_t = [ctx.bfB.ap()[:, i * D:(i + 1) * D] for i in range(2)]
    ctx.hhi_t = [sb(f"hhit{i}", [128, SW], BF16) for i in range(2)]
    ctx.hlo_t = [sb(f"hlot{i}", [128, SW], BF16) for i in range(2)]
    ctx.yhi_t = [sb(f"yhit{i}", [128, STRIP], BF16) for i in range(2)]
    ctx.ylo_t = [sb(f"ylot{i}", [128, STRIP], BF16) for i in range(2)]
    for nm in ["q_wbf", "q_wab", "q_l"]:
        setattr(ctx, nm, [sb(f"{nm}{i}", [128, 512], BF16) for i in range(2)])
    for nm in ["q_sb", "q_a", "q_b", "q_q", "q_wh", "q_d", "q_fwd"]:
        t = sb(nm, [128, 512], BF16)
        setattr(ctx, nm, [t, t])
    ctx.qst = [sb(f"qst{i}", [128, QSTW], BF16) for i in range(2)]
    ctx.hm_hi = [sb(f"hmhi{i}", [128, STRIP], BF16) for i in range(2)]
    ctx.hm_lo = [sb(f"hmlo{i}", [128, STRIP], BF16) for i in range(2)]
    ctx.g2 = sb("g2", [128, 128], BF16)

    ctx.ps = [nc.alloc_psum_tensor(f"ps{i}", [128, 512], F32) for i in range(8)]

    ctx.sbuf_left = nc.sbuf_bytes_remaining

    ctx.slab_loaded = {}
    ctx.hst_done = {}
    ctx.p2_wl = {}
    ctx.p2_st = {}
    ctx.p2_gdone = {}
    ctx.p2_bankfree = {}
    ctx.p2_gidx = 0
    ctx.p3b_gdone = {}
    ctx.p3b_bankfree = {}
    ctx.p3b_gidx = 0
    ctx.p3b_st = {}
    ctx.p3b_xfree = {}
    ctx.p4_ev = {}
    ctx.p4_ea = {}
    ctx.p4_sga = {}
    ctx.p4_sqv = {}
    ctx.p4_hmfree = {}
    ctx.p4_gp_done = 0
    ctx.p5_gdone = {}
    ctx.p5_bankfree = {}
    ctx.p5_gidx = 0
    ctx.p5_st = {}
    ctx.p5_xfree = {}
    ctx.p5_invb_free = {}

    with ExitStack() as stack:
        led = Ledger()
        for nm in ["eS", "eA", "eV", "eP", "eG", "r0", "r1", "r2", "r3",
                   "w0", "w1", "dsa", "dsb", "dsc", "out_sem", "out2"]:
            setattr(ctx, nm, led.new(stack, nc, nm))
        blk = stack.enter_context(nc.Block())
        ctx.blk = blk

        ph = 0

        def nxt():
            nonlocal ph
            ph += 1
            return ph <= phases
        if nxt():
            emit_P0(ctx)
            barrier(ctx)
        if nxt():
            emit_P2a(ctx)
            barrier(ctx)
        if nxt():
            ln1w = [SUBW + 1] + [SUBW] * (c["NSUB"] - 1)
            emit_LN(ctx, ctx.in_xT, ctx.s_h_hi, ctx.s_h_lo, ctx.c1a.ap(), 0, ln1w, "ln1")
            barrier(ctx)
        if nxt():
            emit_P2(ctx)
            barrier(ctx)
        items4 = []
        for band in range(NB4):
            for kd in range(ND):
                items4.append((
                    [(ctx.in_wguT[128 * kd:128 * (kd + 1), 256 * band:256 * (band + 1)], 0, 256),
                     (ctx.in_wguT[128 * kd:128 * (kd + 1), H + 256 * band:H + 256 * (band + 1)], 256, 256)],
                    ctx.s_qgu[band, kd, :, :]))
        items5 = []
        QW = min(512, D)
        for hk in range(NH):
            for b in range(D // QW):
                items5.append((
                    [(ctx.in_wprT[128 * hk:128 * (hk + 1), QW * b:QW * (b + 1)], 0, QW)],
                    ctx.s_qp[hk, :, QW * b:QW * (b + 1)]))
        qs = QuantStream(ctx, items4 + items5, "q")
        ctx.quant_stream = qs
        if nxt():
            emit_P3(ctx)
            barrier(ctx)
        if nxt():
            emit_P3b(ctx)
            barrier(ctx)
        if nxt():
            ln2w = [STRIP] * c["NSTRIP"]
            emit_LN(ctx, ctx.s_x1, ctx.s_h2_hi, ctx.s_h2_lo, ctx.c1b.ap(), 3, ln2w, "ln2")
            barrier(ctx)
        if nxt():
            qs.emit_rest()
            barrier(ctx)
        if nxt():
            emit_P4b(ctx)
            barrier(ctx)
        if nxt():
            barrier(ctx)
        if nxt():
            emit_P5b(ctx)
        emit_fin(ctx)
    return nc




CFG = make_cfg(D=2048, TC=2048, W=192, H=8192, SUBW=448, n_cores=8)
_NC_CACHE = []


def kernel(**inputs):
    from concourse.bass_utils import run_bass_kernel_spmd
    cfg = CFG
    in_maps = host_prepare(cfg, inputs)
    if not _NC_CACHE:
        _NC_CACHE.append(build_nc(cfg))
    nc = _NC_CACHE[0]
    r = run_bass_kernel_spmd(nc, in_maps, core_ids=list(range(cfg["n_cores"])))
    return host_gather(cfg, r.results)


# version 07
# speedup vs baseline: 1.0716x; 1.0716x over previous
import numpy as np
import concourse.bass as bass
import concourse.mybir as mybir
from concourse.bass_isa import ReduceOp
from contextlib import ExitStack

F32 = mybir.dt.float32
BF16 = mybir.dt.bfloat16
U16 = mybir.dt.uint16
ALU = mybir.AluOpType
AF = mybir.ActivationFunctionType
AX = mybir.AxisListType

EPS64C = float(np.float32(np.finfo(np.float64).eps))


def make_cfg(D=2048, TC=2048, W=192, H=8192, SUBW=448, n_cores=8, ecg=None, dcg=None):
    c = {}
    c["D"] = D; c["TC"] = TC; c["W"] = W; c["H"] = H
    c["ND"] = D // 128
    c["NH"] = H // 128
    c["SPAN"] = W + TC
    assert c["SPAN"] % SUBW == 0
    c["SUBW"] = SUBW
    c["NSUB"] = c["SPAN"] // SUBW
    c["HCOLS"] = W + 1 + TC
    c["MODN"] = 6 * D
    c["NMOD"] = c["MODN"] // 128
    c["STRIP"] = min(512, TC)
    assert TC % c["STRIP"] == 0
    c["NSTRIP"] = TC // c["STRIP"]
    c["ECG"] = ecg or min(8, c["ND"])
    assert c["ND"] % c["ECG"] == 0
    c["NB4"] = H // 256
    c["DCG"] = dcg or min(8, c["ND"])
    c["n_cores"] = n_cores
    return c


class Ledger:
    def __init__(self):
        self.sems = []

    def new(self, stack, nc, name):
        h = stack.enter_context(nc.semaphore(name))
        s = Sem(h, name)
        self.sems.append(s)
        return s


class Sem:
    def __init__(self, h, name):
        self.h = h
        self.name = name
        self.n = 0

    def inc(self, instr, k=1):
        instr.then_inc(self.h, k)
        self.n += k
        return self.n

    def dma(self, instr):
        return self.inc(instr, 16)


def host_prepare(cfg, inputs):
    D, TC, W, H = cfg["D"], cfg["TC"], cfg["W"], cfg["H"]
    n = cfg["n_cores"]
    T = TC * n
    x = np.ascontiguousarray(inputs["L0_attn_x"].reshape(T, D))
    wkT = np.ascontiguousarray(inputs["L0_attn_Wk"].T)
    wvT = np.ascontiguousarray(inputs["L0_attn_Wv"].T)
    wrT = np.ascontiguousarray(inputs["L0_attn_Wr"].T)
    woT = np.ascontiguousarray(inputs["L0_attn_Wo"].T)
    wguT = np.ascontiguousarray(inputs["L0_ffn_w_gate_up"].T)
    wprT = np.ascontiguousarray(inputs["L0_ffn_w_proj"].T)
    wmod = inputs["L0_wmod"]
    ND, NMOD = cfg["ND"], cfg["NMOD"]
    wmod_t = np.ascontiguousarray(
        wmod.reshape(ND, 128, NMOD, 128).transpose(2, 1, 0, 3).reshape(NMOD, 128, ND * 128))

    def as_cols(v):
        return np.ascontiguousarray(np.asarray(v).reshape(ND, 128).T)

    pvecs = np.concatenate([
        as_cols(inputs["L0_c"]), as_cols(inputs["L0_attn_time_decay"]),
        as_cols(inputs["L0_attn_time_first"]), as_cols(inputs["L0_attn_time_mix_k"]),
        as_cols(inputs["L0_attn_time_mix_v"]), as_cols(inputs["L0_attn_time_mix_r"]),
    ], axis=1)

    in_maps = []
    for c in range(n):
        t0 = c * TC
        lo = t0 - W - 1
        xs = np.zeros((cfg["HCOLS"], D), dtype=np.float32)
        s = max(0, lo)
        xs[s - lo:] = x[s:t0 + TC]
        xT = np.ascontiguousarray(xs.T)
        wmask = np.ones((1, cfg["SPAN"]), dtype=np.float32)
        if c == 0:
            wmask[0, :W] = 0.0
        in_maps.append({
            "xT": xT, "wkT": wkT, "wvT": wvT, "wrT": wrT, "woT": woT,
            "wguT": wguT, "wprT": wprT, "wmod_t": wmod_t,
            "pvecs": pvecs, "wmask": wmask,
        })
    return in_maps


def host_gather(cfg, results):
    D, TC, n = cfg["D"], cfg["TC"], cfg["n_cores"]
    out = np.empty((1, TC * n, D), dtype=np.float32)
    for c in range(n):
        out[0, c * TC:(c + 1) * TC] = results[c]["outT"].T
    return out



class Ctx:
    pass


def barrier(ctx):
    nc, blk = ctx.nc, ctx.blk
    engs = [("sync", ctx.eS), ("scalar", ctx.eA), ("vector", ctx.eV),
            ("tensor", ctx.eP), ("gpsimd", ctx.eG)]
    for name, sem in engs:
        def mk(sem=sem, v=sem.n):
            def f(e):
                e.wait_ge(sem.h, v)
                sem.inc(e.nop(), 1)
            return f
        getattr(blk, name)(mk())
    dmas = [ctx.r0, ctx.r1, ctx.r2, ctx.r3, ctx.w0, ctx.w1,
            ctx.dsa, ctx.dsb, ctx.dsc, ctx.out_sem, ctx.out2]
    snap = [(sem, sem.n) for _, sem in engs] + [(x, x.n) for x in dmas]
    for name, _ in engs:
        def mk2():
            def f(e):
                for sem, v in snap:
                    e.wait_ge(sem.h, v)
            return f
        getattr(blk, name)(mk2())


def emit_P0(ctx):
    c = ctx.cfg
    nc, blk = ctx.nc, ctx.blk
    ND, NMOD, W = c["ND"], c["NMOD"], c["W"]

    vec_done = []

    def f_sync(e):
        ctx.dsa.dma(e.dma_start(ctx.vecs.ap(), ctx.in_pvecs))
        ctx.dsa.dma(e.dma_start(ctx.mrow.ap(), ctx.in_wmask[:, 0:W]))
        vec_done.append(ctx.dsa.n)
        pe_base = ctx.eP.n
        for m in range(NMOD):
            if m >= 2:
                e.wait_ge(ctx.eP.h, pe_base + m - 1)
            s = [ctx.dsb, ctx.dsc][m % 2]
            s.dma(e.dma_start(ctx.slab[m % 2].ap(), ctx.in_wmod[m, :, :]))
            ctx.slab_loaded[m] = s.n
    blk.sync(f_sync)

    def f_act(e):
        e.wait_ge(ctx.dsa.h, vec_done[0])
        v = ctx.vecs.ap()
        e.activation(ctx.silu_c.ap(), v[:, 0:ND], AF.Silu)
        e.activation(ctx.tw.ap(), v[:, ND:2 * ND], AF.Exp)
        ctx.eA.inc(e.activation(ctx.eu16.ap(), v[:, 2 * ND:3 * ND], AF.Exp), 1)
        e.wait_ge(ctx.eA.h, ctx.eA.n)
        ctx.eA.inc(e.activation(ctx.lam16.ap(), ctx.tw.ap(), AF.Exp, scale=-1.0), 1)
    blk.scalar(f_act)
    act_ready = ctx.eA.n

    def f_vec(e):
        e.wait_ge(ctx.dsa.h, vec_done[0])
        e.memset(ctx.g2.ap()[0:64, 64:128], 0.0)
        e.memset(ctx.g2.ap()[64:128, 0:64], 0.0)
        e.memset(ctx.g2.ap()[0:64, 0:64], 1.0 / 64.0)
        e.memset(ctx.g2.ap()[64:128, 64:128], 1.0 / 64.0)
        e.memset(ctx.zs.ap(), 0.0)
        e.memset(ctx.ones128.ap(), 1.0)
        e.memset(ctx.onesrow.ap(), 1.0)
        e.memset(ctx.epsln.ap(), 1e-5)
        e.memset(ctx.epsrms.ap(), EPS64C)
        e.memset(ctx.macc.ap(), -1e30)
        v = ctx.vecs.ap()
        for i in range(3):
            e.tensor_scalar(ctx.tm1m[i].ap(), v[:, (3 + i) * ND:(4 + i) * ND],
                            -1.0, 1.0, ALU.mult, ALU.add)
        ctx.eV.inc(e.tensor_scalar(ctx.mrow.ap(), ctx.mrow.ap(), -1.0, 1e30,
                                   ALU.add, ALU.mult), 1)
    blk.vector(f_vec)
    vec_ready = ctx.eV.n

    def f_pe(e):
        e.wait_ge(ctx.eA.h, act_ready)
        e.wait_ge(ctx.eV.h, vec_ready)
        for m in range(NMOD):
            s = [ctx.dsb, ctx.dsc][m % 2]
            e.wait_ge(s.h, ctx.slab_loaded[m])
            sl = ctx.slab[m % 2].ap()
            for dc in range(ND):
                i = e.matmul(ctx.ps[7].ap()[:, m:m + 1],
                             sl[:, 128 * dc:128 * (dc + 1)],
                             ctx.silu_c.ap()[:, dc:dc + 1],
                             start=(dc == 0), stop=(dc == ND - 1))
                if dc == ND - 1:
                    ctx.eP.inc(i, 1)
        ctx.eP.inc(e.matmul(ctx.ps[6].ap()[:, 0:W], ctx.onesrow.ap(),
                            ctx.mrow.ap(), start=True, stop=True), 1)
    blk.tensor(f_pe)
    pe_mod = ctx.eP.n

    def f_vec2(e):
        e.wait_ge(ctx.eP.h, pe_mod)
        ctx.eV.inc(e.tensor_copy(ctx.mod_sb.ap(), ctx.ps[7].ap()[:, 0:ctx.cfg["NMOD"]]), 1)
        e.tensor_copy(ctx.maskneg.ap(), ctx.ps[6].ap()[:, 0:W])
        e.wait_ge(ctx.eV.h, ctx.eV.n)
        m = ctx.mod_sb.ap()
        e.tensor_scalar(ctx.c1a.ap(), m[:, 1 * ND:2 * ND], 1.0, None, ALU.add)
        e.tensor_scalar(ctx.c1b.ap(), m[:, 4 * ND:5 * ND], 1.0, None, ALU.add)
        for (g_t, q) in ((ctx.ga_cl, 2), (ctx.gb_cl, 5)):
            mq = m[:, q * ND:(q + 1) * ND]
            e.wait_ge(ctx.eV.h, ctx.eV.n)
            ctx.eV.inc(e.tensor_tensor(ctx.tmpg.ap(), mq, ctx.zs.ap(), ALU.is_ge), 1)
            ctx.eV.inc(e.tensor_scalar(ctx.tmpg2.ap(), mq, -1.0, None, ALU.mult), 1)
            e.wait_ge(ctx.eV.h, ctx.eV.n)
            ctx.eV.inc(e.tensor_scalar(ctx.tmpg.ap(), ctx.tmpg.ap(), 2.0, -1.0,
                                       ALU.mult, ALU.add), 1)
            ctx.eV.inc(e.tensor_tensor(ctx.tmpg2.ap(), ctx.tmpg2.ap(), mq,
                                       ALU.max), 1)
            e.wait_ge(ctx.eV.h, ctx.eV.n)
            ctx.eV.inc(e.tensor_scalar(ctx.tmpg2.ap(), ctx.tmpg2.ap(), 0.25,
                                       None, ALU.max), 1)
            e.wait_ge(ctx.eV.h, ctx.eV.n)
            ctx.eV.inc(e.tensor_tensor(g_t.ap(), ctx.tmpg.ap(), ctx.tmpg2.ap(),
                                       ALU.mult), 1)
    blk.vector(f_vec2)


def emit_P2a(ctx):
    c = ctx.cfg
    blk = ctx.blk
    ND, D = c["ND"], c["D"]
    mats_in = [ctx.in_wkT, ctx.in_wvT, ctx.in_wrT, ctx.in_woT]
    mats_hi = [ctx.s_wk_hi, ctx.s_wv_hi, ctx.s_wr_hi, ctx.s_wo_hi]
    mats_lo = [ctx.s_wk_lo, ctx.s_wv_lo, ctx.s_wr_lo, ctx.s_wo_lo]
    n_it = 4 * ND
    eA0, eV0 = ctx.eA.n, ctx.eV.n
    r = [ctx.r0, ctx.r1]
    w = [ctx.w0, ctx.w1]
    loaded = {}

    def f_sync(e):
        e.wait_ge(ctx.r0.h, ctx.r0.n)
        e.wait_ge(ctx.r1.h, ctx.r1.n)
        for it in range(n_it):
            if it >= 2:
                e.wait_ge(ctx.eA.h, eA0 + (it - 1))
                e.wait_ge(ctx.eV.h, eV0 + (it - 1))
            mat, dc = divmod(it, ND)
            sl = it % 2
            r[sl].dma(e.dma_start(ctx.wblkF[sl].ap(),
                                  mats_in[mat][128 * dc:128 * (dc + 1), :]))
            loaded[it] = r[sl].n
    blk.sync(f_sync)

    whi_done = {}

    def f_act(e):
        for it in range(n_it):
            mat, dc = divmod(it, ND)
            sl = it % 2
            e.wait_ge(r[sl].h, loaded[it])
            if it >= 2:
                e.wait_ge(w[sl].h, whi_done.get(it - 2, 0))
            i = e.activation(ctx.whi_t[sl], ctx.wblkF[sl].ap(), AF.Copy)
            ctx.eA.inc(i, 1)
            assert ctx.eA.n == eA0 + it + 1
            e.wait_ge(ctx.eA.h, ctx.eA.n)
            w[sl].dma(e.dma_start(mats_hi[mat][128 * dc:128 * (dc + 1), :],
                                  ctx.whi_t[sl]))
            whi_done[it] = w[sl].n
            e.wait_ge(ctx.eV.h, eV0 + it + 1)
            w[sl].dma(e.dma_start(mats_lo[mat][128 * dc:128 * (dc + 1), :],
                                  ctx.wlo_t[sl]))
            whi_done[it] = w[sl].n
    blk.scalar(f_act)

    def f_vec(e):
        for it in range(n_it):
            sl = it % 2
            e.wait_ge(r[sl].h, loaded[it])
            e.wait_ge(ctx.eA.h, eA0 + it + 1)
            if it >= 2:
                e.wait_ge(w[sl].h, whi_done.get(it - 2, 0))
            i = e.tensor_tensor(ctx.wlo_t[sl], ctx.wblkF[sl].ap(),
                                ctx.whi_t[sl], ALU.subtract)
            ctx.eV.inc(i, 1)
            assert ctx.eV.n == eV0 + it + 1
    blk.vector(f_vec)


def emit_LN(ctx, src, dst_hi, dst_lo, c1_t, shift_comp, widths, tag):
    c = ctx.cfg
    blk = ctx.blk
    ND, D, W = c["ND"], c["D"], c["W"]
    NSUB = len(widths)
    offs = np.cumsum([0] + list(widths)).tolist()
    r = [ctx.r0, ctx.r1]
    w = [ctx.w0, ctx.w1]

    for sub in range(NSUB):
        wd, off = widths[sub], offs[sub]
        eP0, eA0, eV0 = ctx.eP.n, ctx.eA.n, ctx.eV.n
        loadedA = {}

        def f_syncA(e, wd=wd, off=off, eP0=eP0, loadedA=loadedA):
            e.wait_ge(ctx.r0.h, ctx.r0.n)
            e.wait_ge(ctx.r1.h, ctx.r1.n)
            e.wait_ge(ctx.eV.h, ctx.eV.n)
            for dc in range(ND):
                if dc >= 2:
                    e.wait_ge(ctx.eP.h, eP0 + dc - 1)
                sl = dc % 2
                r[sl].dma(e.dma_start(ctx.pFa_sl[sl][:, 0:wd],
                                      src[128 * dc:128 * (dc + 1), off:off + wd]))
                loadedA[dc] = r[sl].n
        blk.sync(f_syncA)

        def f_actA(e, wd=wd, eA0=eA0, eP0=eP0, loadedA=loadedA):
            for dc in range(ND):
                sl = dc % 2
                e.wait_ge(r[sl].h, loadedA[dc])
                if dc >= 2:
                    e.wait_ge(ctx.eP.h, eP0 + dc - 1)
                i = e.activation(ctx.pFb_sl[sl][:, 0:wd],
                                 ctx.pFa_sl[sl][:, 0:wd], AF.Square)
                ctx.eA.inc(i, 1)
                assert ctx.eA.n == eA0 + dc + 1
        blk.scalar(f_actA)

        eVpre = ctx.eV.n

        def f_peA(e, wd=wd, eA0=eA0, eP0=eP0, loadedA=loadedA, eVpre=eVpre):
            e.wait_ge(ctx.eV.h, eVpre)
            for dc in range(ND):
                sl = dc % 2
                e.wait_ge(r[sl].h, loadedA[dc])
                e.wait_ge(ctx.eA.h, eA0 + dc + 1)
                e.matmul(ctx.ps[6].ap()[0:1, 0:wd], ctx.ones128.ap(),
                         ctx.pFa_sl[sl][:, 0:wd],
                         start=(dc == 0), stop=(dc == ND - 1))
                i = e.matmul(ctx.ps[5].ap()[0:1, 0:wd], ctx.ones128.ap(),
                             ctx.pFb_sl[sl][:, 0:wd],
                             start=(dc == 0), stop=(dc == ND - 1))
                ctx.eP.inc(i, 1)
                assert ctx.eP.n == eP0 + dc + 1
        blk.tensor(f_peA)

        S = 512
        rws = ctx.rows2.ap()
        r0s, r1s, r2s = rws[:, 0:wd], rws[:, S:S + wd], rws[:, 2 * S:2 * S + wd]

        def f_vecS(e, wd=wd, eP0=eP0, r0s=r0s, r1s=r1s, r2s=r2s):
            e.wait_ge(ctx.eP.h, eP0 + ND)
            e.tensor_copy(r0s, ctx.ps[6].ap()[0:1, 0:wd])
            ctx.eV.inc(e.tensor_copy(r1s, ctx.ps[5].ap()[0:1, 0:wd]), 1)
            e.wait_ge(ctx.eV.h, ctx.eV.n)
            ctx.eV.inc(e.tensor_scalar(r2s, r0s, 1.0 / ctx.cfg["D"], None, ALU.mult), 1)
            e.wait_ge(ctx.eV.h, ctx.eV.n)
            ctx.eV.inc(e.tensor_tensor(r0s, r2s, r2s, ALU.mult), 1)
            e.wait_ge(ctx.eV.h, ctx.eV.n)
            ctx.eV.inc(e.scalar_tensor_tensor(r1s, r1s, 1.0 / ctx.cfg["D"], r0s,
                                              ALU.mult, ALU.subtract), 1)
        blk.vector(f_vecS)
        v_var = ctx.eV.n

        def f_actS(e, r1s=r1s, v_var=v_var):
            e.wait_ge(ctx.eV.h, v_var)
            ctx.eA.inc(e.activation(r1s, r1s, AF.Sqrt, bias=ctx.epsln.ap()), 1)
        blk.scalar(f_actS)
        a_sd = ctx.eA.n

        def f_vecS2(e, r1s=r1s, r2s=r2s, a_sd=a_sd):
            e.wait_ge(ctx.eA.h, a_sd)
            ctx.eV.inc(e.reciprocal(r1s, r1s), 1)
            ctx.eV.inc(e.tensor_scalar(r2s, r2s, -1.0, None, ALU.mult), 1)
        blk.vector(f_vecS2)
        v_rows = ctx.eV.n

        def f_peB(e, wd=wd, r1s=r1s, r2s=r2s, v_rows=v_rows):
            e.wait_ge(ctx.eV.h, v_rows)
            e.matmul(ctx.ps[4].ap()[:, 0:wd], ctx.onesrow.ap(), r2s,
                     start=True, stop=True)
            ctx.eP.inc(e.matmul(ctx.ps[3].ap()[:, 0:wd], ctx.onesrow.ap(), r1s,
                                start=True, stop=True), 1)
        blk.tensor(f_peB)
        p_bcast = ctx.eP.n

        eV1, eA1 = ctx.eV.n, ctx.eA.n
        loadedB = {}

        def f_syncB(e, wd=wd, off=off, eV1=eV1, loadedB=loadedB):
            e.wait_ge(ctx.r0.h, ctx.r0.n)
            e.wait_ge(ctx.r1.h, ctx.r1.n)
            e.wait_ge(ctx.eP.h, ctx.eP.n)
            for dc in range(ND):
                if dc >= 2:
                    e.wait_ge(ctx.eV.h, eV1 + 4 * (dc - 1))
                sl = dc % 2
                r[sl].dma(e.dma_start(ctx.pFa_sl[sl][:, 0:wd],
                                      src[128 * dc:128 * (dc + 1), off:off + wd]))
                loadedB[dc] = r[sl].n
        blk.sync(f_syncB)

        def f_vecB(e, wd=wd, eV1=eV1, eA1=eA1, p_bcast=p_bcast, loadedB=loadedB):
            e.wait_ge(ctx.eP.h, p_bcast)
            wb = [ctx.w0.n, ctx.w1.n]
            for dc in range(ND):
                sl = dc % 2
                e.wait_ge(r[sl].h, loadedB[dc])
                e.wait_ge(ctx.w0.h if sl == 0 else ctx.w1.h,
                          ctx.hst_done.get((tag, sub, dc - 2), wb[sl]))
                ctx.eV.inc(e.tensor_tensor(ctx.pFc_sl[sl][:, 0:wd],
                                           ctx.pFa_sl[sl][:, 0:wd],
                                           ctx.ps[4].ap()[:, 0:wd], ALU.add), 1)
                e.wait_ge(ctx.eV.h, ctx.eV.n)
                ctx.eV.inc(e.tensor_tensor(ctx.pFc_sl[sl][:, 0:wd],
                                           ctx.pFc_sl[sl][:, 0:wd],
                                           ctx.ps[3].ap()[:, 0:wd], ALU.mult), 1)
                e.wait_ge(ctx.eV.h, ctx.eV.n)
                ctx.eV.inc(e.scalar_tensor_tensor(
                    ctx.pFd_sl[sl][:, 0:wd], ctx.pFc_sl[sl][:, 0:wd],
                    c1_t[:, dc:dc + 1],
                    ctx.mod_sb.ap()[:, shift_comp * ND + dc:shift_comp * ND + dc + 1]
                    .broadcast_to([128, wd]),
                    ALU.mult, ALU.add), 1)
                e.wait_ge(ctx.eA.h, eA1 + dc + 1)
                ctx.eV.inc(e.tensor_tensor(ctx.hlo_t[sl].ap()[:, 0:wd],
                                           ctx.pFd_sl[sl][:, 0:wd],
                                           ctx.hhi_t[sl].ap()[:, 0:wd],
                                           ALU.subtract), 1)
                assert ctx.eV.n == eV1 + 4 * (dc + 1)
        blk.vector(f_vecB)

        def f_actB(e, wd=wd, off=off, eV1=eV1, eA1=eA1):
            wb = [ctx.w0.n, ctx.w1.n]
            for dc in range(ND):
                sl = dc % 2
                e.wait_ge(ctx.eV.h, eV1 + 4 * dc + 3)
                e.wait_ge(ctx.w0.h if sl == 0 else ctx.w1.h,
                          ctx.hst_done.get((tag, sub, dc - 2), wb[sl]))
                ctx.eA.inc(e.activation(ctx.hhi_t[sl].ap()[:, 0:wd],
                                        ctx.pFd_sl[sl][:, 0:wd], AF.Copy), 1)
                assert ctx.eA.n == eA1 + dc + 1
                e.wait_ge(ctx.eA.h, ctx.eA.n)
                ws = w[sl]
                ws.dma(e.dma_start(dst_hi[128 * dc:128 * (dc + 1), off:off + wd],
                                   ctx.hhi_t[sl].ap()[:, 0:wd]))
                e.wait_ge(ctx.eV.h, eV1 + 4 * (dc + 1))
                ws.dma(e.dma_start(dst_lo[128 * dc:128 * (dc + 1), off:off + wd],
                                   ctx.hlo_t[sl].ap()[:, 0:wd]))
                ctx.hst_done[(tag, sub, dc)] = ws.n
        blk.scalar(f_actB)



def emit_P2(ctx):
    c = ctx.cfg
    blk = ctx.blk
    ND, SUBW, NSUB, ECG = c["ND"], c["SUBW"], c["NSUB"], c["ECG"]
    NG = ND // ECG
    s_dst = [ctx.s_k, ctx.s_v, ctx.s_sr]
    w_hi = [ctx.s_wk_hi, ctx.s_wv_hi, ctx.s_wr_hi]
    w_lo = [ctx.s_wk_lo, ctx.s_wv_lo, ctx.s_wr_lo]
    WD = ECG * 128
    r = [ctx.r0, ctx.r1]
    rw = [ctx.r2, ctx.r3]
    w = [ctx.w0, ctx.w1]
    HS = SUBW + 1

    for sub in range(NSUB):
        off = sub * SUBW
        hs_loaded = []

        def f_syncH(e, off=off):
            e.wait_ge(ctx.eV.h, ctx.eV.n)
            e.wait_ge(ctx.r0.h, ctx.r0.n)
            for dc in range(ND):
                r[0].dma(e.dma_start(ctx.bfA.ap()[:, dc * HS:dc * HS + HS],
                                     ctx.s_h_hi[128 * dc:128 * (dc + 1), off:off + HS]))
                r[0].dma(e.dma_start(ctx.bfB.ap()[:, dc * HS:dc * HS + HS],
                                     ctx.s_h_lo[128 * dc:128 * (dc + 1), off:off + HS]))
            hs_loaded.append(r[0].n)
        blk.sync(f_syncH)

        for mat in range(3):
            eV0, eA0 = ctx.eV.n, ctx.eA.n
            XO = ctx.XOFF

            def f_vecM(e, mat=mat, eV0=eV0, eA0=eA0, hs_loaded=hs_loaded, XO=XO):
                e.wait_ge(r[0].h, hs_loaded[0])
                e.wait_ge(ctx.eP.h, ctx.eP.n)
                tm = ctx.vecs.ap()[:, (3 + mat) * ND:(4 + mat) * ND]
                for dc in range(ND):
                    e.wait_ge(ctx.eV.h, ctx.eV.n)
                    hh = ctx.bfA.ap()[:, dc * HS:dc * HS + HS]
                    hl = ctx.bfB.ap()[:, dc * HS:dc * HS + HS]
                    ctx.eV.inc(e.tensor_tensor(ctx.pFa_sl[0][:, 0:SUBW],
                                               hh[:, 1:HS], hl[:, 1:HS], ALU.add), 1)
                    ctx.eV.inc(e.tensor_tensor(ctx.pFb_sl[0][:, 0:SUBW],
                                               hh[:, 0:SUBW], hl[:, 0:SUBW], ALU.add), 1)
                    e.wait_ge(ctx.eV.h, ctx.eV.n)
                    ctx.eV.inc(e.tensor_scalar(ctx.pFb_sl[0][:, 0:SUBW],
                                               ctx.pFb_sl[0][:, 0:SUBW],
                                               ctx.tm1m[mat].ap()[:, dc:dc + 1],
                                               None, ALU.mult), 1)
                    e.wait_ge(ctx.eV.h, ctx.eV.n)
                    ctx.eV.inc(e.scalar_tensor_tensor(
                        ctx.pFc_sl[0][:, 0:SUBW], ctx.pFa_sl[0][:, 0:SUBW],
                        tm[:, dc:dc + 1], ctx.pFb_sl[0][:, 0:SUBW],
                        ALU.mult, ALU.add), 1)
                    e.wait_ge(ctx.eA.h, eA0 + dc + 1)
                    ctx.eV.inc(e.tensor_tensor(
                        ctx.bfB.ap()[:, XO + dc * SUBW:XO + (dc + 1) * SUBW],
                        ctx.pFc_sl[0][:, 0:SUBW],
                        ctx.bfA.ap()[:, XO + dc * SUBW:XO + (dc + 1) * SUBW],
                        ALU.subtract), 1)
                    assert ctx.eV.n == eV0 + 5 * (dc + 1)
            blk.vector(f_vecM)

            def f_actM(e, eV0=eV0, eA0=eA0, XO=XO):
                for dc in range(ND):
                    e.wait_ge(ctx.eV.h, eV0 + 5 * dc + 4)
                    ctx.eA.inc(e.activation(
                        ctx.bfA.ap()[:, XO + dc * SUBW:XO + (dc + 1) * SUBW],
                        ctx.pFc_sl[0][:, 0:SUBW], AF.Copy), 1)
                    assert ctx.eA.n == eA0 + dc + 1
            blk.scalar(f_actM)
            mix_done = ctx.eV.n

            for ecg in range(NG):
                gidx = ctx.p2_gidx
                ctx.p2_gidx += 1
                bk = 0 if ECG * 2 > 8 else ECG * (gidx % 2)
                eP0 = ctx.eP.n
                wl_cnt = {}

                def f_syncW(e, mat=mat, ecg=ecg, eP0=eP0, wl_cnt=wl_cnt):
                    e.wait_ge(ctx.r2.h, ctx.r2.n)
                    e.wait_ge(ctx.r3.h, ctx.r3.n)
                    for dc in range(ND):
                        sl = dc % 2
                        e.wait_ge(ctx.eP.h, eP0 + dc - 1 if dc >= 2 else eP0)
                        rw[sl].dma(e.dma_start(
                            ctx.wst_hi[sl].ap()[:, 0:WD],
                            w_hi[mat][128 * dc:128 * (dc + 1), ecg * WD:(ecg + 1) * WD]))
                        rw[sl].dma(e.dma_start(
                            ctx.wst_lo[sl].ap()[:, 0:WD],
                            w_lo[mat][128 * dc:128 * (dc + 1), ecg * WD:(ecg + 1) * WD]))
                        wl_cnt[dc] = rw[sl].n
                    ctx.p2_wl[(sub, mat, ecg)] = dict(wl_cnt)
                blk.sync(f_syncW)

                def f_peM(e, mat=mat, ecg=ecg, eP0=eP0, mix_done=mix_done,
                          wl_cnt=wl_cnt, XO=XO, gidx=gidx, bk=bk):
                    e.wait_ge(ctx.eV.h, mix_done)
                    prev = gidx - (2 if ECG * 2 <= 8 else 1)
                    for dc in range(ND):
                        sl = dc % 2
                        e.wait_ge(rw[sl].h, wl_cnt[dc])
                        xh = ctx.bfA.ap()[:, XO + dc * SUBW:XO + (dc + 1) * SUBW]
                        xl = ctx.bfB.ap()[:, XO + dc * SUBW:XO + (dc + 1) * SUBW]
                        for ec in range(ECG):
                            if dc == 0:
                                eng, v = ctx.p2_bankfree.get((prev, bk + ec), (None, 0))
                                if eng is not None:
                                    e.wait_ge(eng, v)
                            wh = ctx.wst_hi[sl].ap()[:, 128 * ec:128 * (ec + 1)]
                            wl = ctx.wst_lo[sl].ap()[:, 128 * ec:128 * (ec + 1)]
                            st = (dc == 0)
                            e.matmul(ctx.ps[bk + ec].ap()[:, 0:SUBW], wh, xh,
                                     start=st, stop=False)
                            e.matmul(ctx.ps[bk + ec].ap()[:, 0:SUBW], wh, xl,
                                     start=False, stop=False)
                            i = e.matmul(ctx.ps[bk + ec].ap()[:, 0:SUBW], wl, xh,
                                         start=False, stop=(dc == ND - 1))
                        ctx.eP.inc(i, 1)
                        assert ctx.eP.n == eP0 + dc + 1
                blk.tensor(f_peM)
                grp_done = ctx.eP.n

                def f_evac(e_v, e_a=None):
                    pass

                if mat < 2:
                    eVb = ctx.eV.n
                    wbE = [ctx.w0.n, ctx.w1.n]

                    def st_pred(ec, wbE=wbE):
                        return wbE[ec % 2] + 16 * (ec // 2 + 1)

                    def f_vecE(e, mat=mat, ecg=ecg, sub=sub, grp_done=grp_done,
                               eVb=eVb, wbE=wbE, st_pred=st_pred, bk=bk, gidx=gidx):
                        e.wait_ge(ctx.eP.h, grp_done)
                        for ec in range(ECG):
                            sl = ec % 2
                            e.wait_ge(w[sl].h,
                                      st_pred(ec - 2) if ec >= 2 else wbE[sl])
                            ctx.eV.inc(e.tensor_copy(ctx.pFd_sl[sl][:, 0:SUBW],
                                                     ctx.ps[bk + ec].ap()[:, 0:SUBW]), 1)
                            ctx.p2_bankfree[(gidx, bk + ec)] = (ctx.eV.h, ctx.eV.n)
                            if mat == 0:
                                e.wait_ge(ctx.eV.h, ctx.eV.n)
                                ec_g = ecg * ECG + ec
                                ctx.eV.inc(e.reduce_max(ctx.tmax.ap(),
                                                        ctx.pFd_sl[sl][:, 0:SUBW],
                                                        AX.X), 1)
                                e.wait_ge(ctx.eV.h, ctx.eV.n)
                                ctx.eV.inc(e.tensor_tensor(
                                    ctx.macc.ap()[:, ec_g:ec_g + 1],
                                    ctx.macc.ap()[:, ec_g:ec_g + 1],
                                    ctx.tmax.ap(), ALU.max), 1)
                            assert ctx.eV.n == eVb + (3 if mat == 0 else 1) * (ec + 1)
                    blk.vector(f_vecE)
                    ctx.p2_gdone[gidx] = (ctx.eV.n, ctx.eA.n)

                    def f_actE(e, mat=mat, ecg=ecg, sub=sub, eVb=eVb, grp=grp_done,
                               st_pred=st_pred):
                        k = 3 if mat == 0 else 1
                        for ec in range(ECG):
                            sl = ec % 2
                            e.wait_ge(ctx.eV.h, eVb + k * ec + 1)
                            ec_g = ecg * ECG + ec
                            ws = w[sl]
                            ws.dma(e.dma_start(
                                s_dst[mat][128 * ec_g:128 * (ec_g + 1),
                                           sub * SUBW:(sub + 1) * SUBW],
                                ctx.pFd_sl[sl][:, 0:SUBW]))
                            ctx.p2_st[(sub, mat, ecg, ec)] = ws.n
                            assert ws.n == st_pred(ec)
                    blk.scalar(f_actE)
                else:
                    eAb = ctx.eA.n

                    def f_actE2(e, ecg=ecg, sub=sub, grp_done=grp_done, eAb=eAb,
                                bk=bk, gidx=gidx):
                        e.wait_ge(ctx.eP.h, grp_done)
                        for ec in range(ECG):
                            sl = ec % 2
                            e.wait_ge(w[sl].h,
                                      ctx.p2_st.get((sub, 2, ecg, ec - 2), 0)
                                      if ec >= 2 else w[sl].n)
                            ctx.eA.inc(e.activation(ctx.pFd_sl[sl][:, 0:SUBW],
                                                    ctx.ps[bk + ec].ap()[:, 0:SUBW],
                                                    AF.Sigmoid), 1)
                            ctx.p2_bankfree[(gidx, bk + ec)] = (ctx.eA.h, ctx.eA.n)
                            e.wait_ge(ctx.eA.h, ctx.eA.n)
                            ec_g = ecg * ECG + ec
                            ws = w[sl]
                            ws.dma(e.dma_start(
                                ctx.s_sr[128 * ec_g:128 * (ec_g + 1),
                                         sub * SUBW:(sub + 1) * SUBW],
                                ctx.pFd_sl[sl][:, 0:SUBW]))
                            ctx.p2_st[(sub, 2, ecg, ec)] = ws.n
                            assert ctx.eA.n == eAb + ec + 1
                    blk.scalar(f_actE2)
                    ctx.p2_gdone[gidx] = (ctx.eV.n, ctx.eA.n)


def emit_P3(ctx):
    c = ctx.cfg
    blk = ctx.blk
    ND, W, SPAN, STRIP, NSTRIP = c["ND"], c["W"], c["SPAN"], c["STRIP"], c["NSTRIP"]
    r = [ctx.r0, ctx.r1]
    w = [ctx.w0, ctx.w1]
    ka_s = [ctx.pF_a.ap(), ctx.pF_a2.ap()]
    va_s = [ctx.pF_b.ap(), ctx.pF_b2.ap()]
    kvsem = [ctx.r0, ctx.dsb]
    vend = {}
    aend = {}
    sr_last = [0, 0]

    def f_neg(e):
        ctx.eV.inc(e.tensor_scalar(ctx.negm.ap(), ctx.macc.ap(), -1.0, None,
                                   ALU.mult), 1)
    blk.vector(f_neg)

    for ec in range(ND):
        cs = ec % 2
        ka, va = ka_s[cs], va_s[cs]
        kv_loaded = []
        vb = ctx.eV.n
        ab = ctx.eA.n
        wb = [ctx.w0.n, ctx.w1.n]
        a_kp = ab + 1

        def yh_done(si, a_kp=a_kp):
            return a_kp + si + 1

        def v_y(si, vb=vb):
            return vb + 1 + 3 + 6 * si + 6

        def sr_used(si, vb=vb):
            return vb + 1 + 3 + 6 * si + 5

        def yst_done(si, wb=wb):
            return wb[si % 2] + 32 * (si // 2 + 1)

        def f_sync(e, ec=ec, cs=cs, ka=ka, va=va, kv_loaded=kv_loaded):
            if ec >= 2:
                e.wait_ge(ctx.eV.h, vend[ec - 2])
                e.wait_ge(ctx.eA.h, aend[ec - 2])
            e.wait_ge(kvsem[cs].h, kvsem[cs].n)
            kvsem[cs].dma(e.dma_start(ka[:, 0:SPAN],
                                      ctx.s_k[128 * ec:128 * (ec + 1), :]))
            kvsem[cs].dma(e.dma_start(va[:, 0:SPAN],
                                      ctx.s_v[128 * ec:128 * (ec + 1), :]))
            kv_loaded.append(kvsem[cs].n)
        blk.sync(f_sync)

        sr_loaded = {}

        srl_prev = list(sr_last)

        def f_sync2(e, ec=ec, sr_used=sr_used, srl_prev=srl_prev):
            e.wait_ge(ctx.r1.h, ctx.r1.n)
            e.wait_ge(ctx.dsa.h, ctx.dsa.n)
            for si in range(NSTRIP):
                sl = si % 2
                ss = [ctx.r1, ctx.dsa][sl]
                e.wait_ge(ctx.eV.h,
                          sr_used(si - 2) if si >= 2 else srl_prev[sl])
                ss.dma(e.dma_start(ctx.pSc_sl[sl][:, 0:STRIP],
                                   ctx.s_sr[128 * ec:128 * (ec + 1),
                                            W + si * STRIP:W + (si + 1) * STRIP]))
                sr_loaded[si] = (ss, ss.n)
        blk.sync(f_sync2)
        for si in range(NSTRIP):
            sr_last[si % 2] = sr_used(si)

        def f_vec1(e, ec=ec, cs=cs, ka=ka, kv_loaded=kv_loaded):
            e.wait_ge(kvsem[cs].h, kv_loaded[0])
            ctx.eV.inc(e.tensor_tensor(ka[:, 0:W], ka[:, 0:W],
                                       ctx.maskneg.ap(), ALU.add), 1)
        blk.vector(f_vec1)
        v_mask = ctx.eV.n

        def f_act1(e, ec=ec, ka=ka, v_mask=v_mask, a_kp=a_kp):
            e.wait_ge(ctx.eV.h, v_mask)
            ctx.eA.inc(e.activation(ka[:, 0:SPAN], ka[:, 0:SPAN],
                                    AF.Exp, bias=ctx.negm.ap()[:, ec:ec + 1]), 1)
            assert ctx.eA.n == a_kp
        blk.scalar(f_act1)

        def f_vec2(e, ec=ec, cs=cs, ka=ka, va=va, sr_loaded=sr_loaded, a_kp=a_kp,
                   vb=vb, wb=wb, yh_done=yh_done, v_y=v_y, sr_used=sr_used,
                   yst_done=yst_done):
            e.wait_ge(ctx.eA.h, a_kp)
            ctx.eV.inc(e.tensor_tensor(va[:, 0:SPAN], ka[:, 0:SPAN],
                                       va[:, 0:SPAN], ALU.mult), 1)
            e.wait_ge(ctx.eV.h, ctx.eV.n)
            lam_b = ctx.lam16.ap()[:, ec:ec + 1].broadcast_to([128, SPAN])
            ctx.eV.inc(e.tensor_tensor_scan(ctx.pF_c.ap()[:, 0:SPAN], lam_b,
                                            va[:, 0:SPAN], 0.0,
                                            ALU.mult, ALU.add), 1)
            ctx.eV.inc(e.tensor_tensor_scan(ctx.pF_d.ap()[:, 0:SPAN], lam_b,
                                            ka[:, 0:SPAN], 0.0,
                                            ALU.mult, ALU.add), 1)
            e.wait_ge(ctx.eV.h, ctx.eV.n)
            eu_c = ctx.eu16.ap()[:, ec:ec + 1]
            for si in range(NSTRIP):
                sl = si % 2
                ks = W + si * STRIP
                if si >= 2:
                    e.wait_ge(ctx.eA.h, yh_done(si - 2))
                ctx.eV.inc(e.scalar_tensor_tensor(
                    ctx.pSa_sl[sl][:, 0:STRIP], va[:, ks:ks + STRIP],
                    eu_c, ctx.pF_c.ap()[:, ks - 1:ks - 1 + STRIP], ALU.mult, ALU.add), 1)
                ctx.eV.inc(e.scalar_tensor_tensor(
                    ctx.pSb_sl[sl][:, 0:STRIP], ka[:, ks:ks + STRIP],
                    eu_c, ctx.pF_d.ap()[:, ks - 1:ks - 1 + STRIP], ALU.mult, ALU.add), 1)
                e.wait_ge(ctx.eV.h, ctx.eV.n)
                ctx.eV.inc(e.reciprocal(ctx.pSb_sl[sl][:, 0:STRIP],
                                        ctx.pSb_sl[sl][:, 0:STRIP]), 1)
                e.wait_ge(ctx.eV.h, ctx.eV.n)
                ctx.eV.inc(e.tensor_tensor(ctx.pSa_sl[sl][:, 0:STRIP],
                                           ctx.pSa_sl[sl][:, 0:STRIP],
                                           ctx.pSb_sl[sl][:, 0:STRIP], ALU.mult), 1)
                e.wait_ge(ctx.eV.h, ctx.eV.n)
                e.wait_ge(sr_loaded[si][0].h, sr_loaded[si][1])
                ctx.eV.inc(e.tensor_tensor(ctx.pSa_sl[sl][:, 0:STRIP],
                                           ctx.pSa_sl[sl][:, 0:STRIP],
                                           ctx.pSc_sl[sl][:, 0:STRIP], ALU.mult), 1)
                assert ctx.eV.n == sr_used(si)
                e.wait_ge(ctx.eA.h, yh_done(si))
                e.wait_ge(w[sl].h, yst_done(si - 2) if si >= 2 else wb[sl])
                ctx.eV.inc(e.tensor_tensor(ctx.ylo_t[sl].ap()[:, 0:STRIP],
                                           ctx.pSa_sl[sl][:, 0:STRIP],
                                           ctx.yhi_t[sl].ap()[:, 0:STRIP],
                                           ALU.subtract), 1)
                assert ctx.eV.n == v_y(si)
        blk.vector(f_vec2)
        vend[ec] = ctx.eV.n

        def f_act2(e, ec=ec, a_kp=a_kp, wb=wb, yh_done=yh_done, v_y=v_y,
                   sr_used=sr_used, yst_done=yst_done):
            for si in range(NSTRIP):
                sl = si % 2
                e.wait_ge(ctx.eV.h, sr_used(si))
                e.wait_ge(w[sl].h, yst_done(si - 2) if si >= 2 else wb[sl])
                ctx.eA.inc(e.activation(ctx.yhi_t[sl].ap()[:, 0:STRIP],
                                        ctx.pSa_sl[sl][:, 0:STRIP], AF.Copy), 1)
                assert ctx.eA.n == yh_done(si)
                e.wait_ge(ctx.eA.h, ctx.eA.n)
                ws = w[sl]
                ws.dma(e.dma_start(ctx.s_y_hi[128 * ec:128 * (ec + 1),
                                              si * STRIP:(si + 1) * STRIP],
                                   ctx.yhi_t[sl].ap()[:, 0:STRIP]))
                e.wait_ge(ctx.eV.h, v_y(si))
                ws.dma(e.dma_start(ctx.s_y_lo[128 * ec:128 * (ec + 1),
                                              si * STRIP:(si + 1) * STRIP],
                                   ctx.ylo_t[sl].ap()[:, 0:STRIP]))
                assert ws.n == yst_done(si)
        blk.scalar(f_act2)
        aend[ec] = ctx.eA.n



def emit_P3b(ctx):
    c = ctx.cfg
    blk = ctx.blk
    ND, STRIP, NSTRIP, DCG, W = c["ND"], c["STRIP"], c["NSTRIP"], c["DCG"], c["W"]
    NG = ND // DCG
    WD = DCG * 128
    r = [ctx.r0, ctx.r1]
    rw = [ctx.r2, ctx.r3]
    w = [ctx.w0, ctx.w1]

    for sub in range(NSTRIP):
        y_loaded = []

        def f_syncY(e, sub=sub):
            e.wait_ge(ctx.eP.h, ctx.eP.n)
            e.wait_ge(ctx.r0.h, ctx.r0.n)
            for ec in range(ND):
                r[0].dma(e.dma_start(ctx.bfA.ap()[:, ec * STRIP:(ec + 1) * STRIP],
                                     ctx.s_y_hi[128 * ec:128 * (ec + 1),
                                                sub * STRIP:(sub + 1) * STRIP]))
                r[0].dma(e.dma_start(ctx.bfB.ap()[:, ec * STRIP:(ec + 1) * STRIP],
                                     ctx.s_y_lo[128 * ec:128 * (ec + 1),
                                                sub * STRIP:(sub + 1) * STRIP]))
            y_loaded.append(r[0].n)
        blk.sync(f_syncY)

        for dcg in range(NG):
            gidx = ctx.p3b_gidx
            ctx.p3b_gidx += 1
            bk = 0 if DCG * 2 > 8 else DCG * (gidx % 2)
            eP0 = ctx.eP.n
            wo_cnt = {}

            def f_syncW(e, dcg=dcg, eP0=eP0, wo_cnt=wo_cnt):
                e.wait_ge(ctx.r2.h, ctx.r2.n)
                e.wait_ge(ctx.r3.h, ctx.r3.n)
                for ec in range(ND):
                    sl = ec % 2
                    e.wait_ge(ctx.eP.h, eP0 + ec - 1 if ec >= 2 else eP0)
                    rw[sl].dma(e.dma_start(
                        ctx.wst_hi[sl].ap()[:, 0:WD],
                        ctx.s_wo_hi[128 * ec:128 * (ec + 1), dcg * WD:(dcg + 1) * WD]))
                    rw[sl].dma(e.dma_start(
                        ctx.wst_lo[sl].ap()[:, 0:WD],
                        ctx.s_wo_lo[128 * ec:128 * (ec + 1), dcg * WD:(dcg + 1) * WD]))
                    wo_cnt[ec] = rw[sl].n
            blk.sync(f_syncW)

            def f_pe(e, dcg=dcg, eP0=eP0, wo_cnt=wo_cnt, y_loaded=y_loaded,
                     gidx=gidx, bk=bk):
                e.wait_ge(r[0].h, y_loaded[0])
                prev = gidx - (2 if DCG * 2 <= 8 else 1)
                for ec in range(ND):
                    sl = ec % 2
                    e.wait_ge(rw[sl].h, wo_cnt[ec])
                    yh = ctx.bfA.ap()[:, ec * STRIP:(ec + 1) * STRIP]
                    yl = ctx.bfB.ap()[:, ec * STRIP:(ec + 1) * STRIP]
                    for dci in range(DCG):
                        if ec == 0:
                            v = ctx.p3b_bankfree.get((prev, bk + dci), 0)
                            e.wait_ge(ctx.eV.h, v)
                        wh = ctx.wst_hi[sl].ap()[:, 128 * dci:128 * (dci + 1)]
                        wl = ctx.wst_lo[sl].ap()[:, 128 * dci:128 * (dci + 1)]
                        e.matmul(ctx.ps[bk + dci].ap()[:, 0:STRIP], wh, yh,
                                 start=(ec == 0), stop=False)
                        e.matmul(ctx.ps[bk + dci].ap()[:, 0:STRIP], wh, yl,
                                 start=False, stop=False)
                        i = e.matmul(ctx.ps[bk + dci].ap()[:, 0:STRIP], wl, yh,
                                     start=False, stop=(ec == ND - 1))
                    ctx.eP.inc(i, 1)
                    assert ctx.eP.n == eP0 + ec + 1
            blk.tensor(f_pe)
            grp_done = ctx.eP.n

            xin_cnt = {}

            eVx = ctx.eV.n

            def f_syncX(e, sub=sub, dcg=dcg, xin_cnt=xin_cnt, eVx=eVx):
                e.wait_ge(ctx.dsb.h, ctx.dsb.n)
                e.wait_ge(ctx.dsc.h, ctx.dsc.n)
                for dci in range(DCG):
                    sl = dci % 2
                    xs = [ctx.dsb, ctx.dsc][sl]
                    e.wait_ge(ctx.eV.h, eVx + dci - 1 if dci >= 2 else eVx)
                    xs.dma(e.dma_start(
                        ctx.pSc_sl[sl][:, 0:STRIP],
                        ctx.in_xT[128 * (dcg * DCG + dci):128 * (dcg * DCG + dci + 1),
                                  W + 1 + sub * STRIP:W + 1 + (sub + 1) * STRIP]))
                    xin_cnt[dci] = (xs, xs.n)
            blk.sync(f_syncX)

            eV0 = ctx.eV.n
            wb3 = [ctx.w0.n, ctx.w1.n]

            def stp3(dci, wb3=wb3):
                return wb3[dci % 2] + 16 * (dci // 2 + 1)

            def f_vecE(e, sub=sub, dcg=dcg, grp_done=grp_done, xin_cnt=xin_cnt,
                       eV0=eV0, wb3=wb3, stp3=stp3, bk=bk):
                e.wait_ge(ctx.eP.h, grp_done)
                for dci in range(DCG):
                    sl = dci % 2
                    dc_g = dcg * DCG + dci
                    e.wait_ge(xin_cnt[dci][0].h, xin_cnt[dci][1])
                    e.wait_ge(ctx.w0.h if sl == 0 else ctx.w1.h,
                              stp3(dci - 2) if dci >= 2 else wb3[sl])
                    ctx.eV.inc(e.scalar_tensor_tensor(
                        ctx.pSa_sl[sl][:, 0:STRIP], ctx.ps[bk + dci].ap()[:, 0:STRIP],
                        ctx.ga_cl.ap()[:, dc_g:dc_g + 1], ctx.pSc_sl[sl][:, 0:STRIP],
                        ALU.mult, ALU.add), 1)
                    ctx.p3b_bankfree[(gidx, bk + dci)] = ctx.eV.n
                    assert ctx.eV.n == eV0 + dci + 1
            blk.vector(f_vecE)
            ctx.p3b_gdone[gidx] = ctx.eV.n

            def f_actS(e, sub=sub, dcg=dcg, eV0=eV0, stp3=stp3):
                for dci in range(DCG):
                    sl = dci % 2
                    dc_g = dcg * DCG + dci
                    e.wait_ge(ctx.eV.h, eV0 + dci + 1)
                    ws = w[sl]
                    ws.dma(e.dma_start(
                        ctx.s_x1[128 * dc_g:128 * (dc_g + 1),
                                 sub * STRIP:(sub + 1) * STRIP],
                        ctx.pSa_sl[sl][:, 0:STRIP]))
                    assert ws.n == stp3(dci)
            blk.scalar(f_actS)


class QuantStream:

    def __init__(self, ctx, items, tag):
        self.ctx = ctx
        self.items = items
        self.tag = tag
        self.pos = 0

    def emit_batch(self, k):
        ctx = self.ctx
        blk = ctx.blk
        its = list(range(self.pos, min(self.pos + k, len(self.items))))
        self.pos = min(self.pos + k, len(self.items))
        if not its:
            return
        items = self.items
        r = [ctx.r2, ctx.r3]
        w = [ctx.w0, ctx.w1]
        eA0, eV0, eP0, eG0 = ctx.eA.n, ctx.eV.n, ctx.eP.n, ctx.eG.n
        wb = [ctx.w0.n, ctx.w1.n]
        base = its[0]
        loaded = {}

        def st_pred(j, wb=wb, base=base):
            jj = j - base
            return wb[jj % 2] + 16 * (jj // 2 + 1)

        def f_sync(e):
            e.wait_ge(ctx.r2.h, ctx.r2.n)
            e.wait_ge(ctx.r3.h, ctx.r3.n)
            for n_, it in enumerate(its):
                sl = it % 2
                e.wait_ge(ctx.eA.h, eA0 + 2 * (n_ - 1) if n_ >= 2 else eA0)
                srcs, dst = items[it]
                for (ap, co, wd) in srcs:
                    r[sl].dma(e.dma_start(ctx.wgublk[sl].ap()[:, co:co + wd], ap))
                loaded[it] = r[sl].n
        blk.sync(f_sync)

        def f_act(e):
            def store(j):
                jl = j % 2
                TQj = sum(x[2] for x in items[j][0])
                e.wait_ge(ctx.eV.h, eV0 + 8 * (j - base + 1))
                w[jl].dma(e.dma_start(items[j][1], ctx.q_l[jl].ap()[:, 0:TQj]))
                assert w[jl].n == st_pred(j)
            for n_, it in enumerate(its):
                sl = it % 2
                TQ = sum(x[2] for x in items[it][0])
                if n_ >= 1:
                    store(its[n_ - 1])
                e.wait_ge(r[sl].h, loaded[it])
                e.wait_ge(ctx.eP.h, eP0 + n_ if n_ >= 1 else eP0)
                e.wait_ge(ctx.eV.h, eV0 + 8 * (n_ - 1) if n_ >= 2 else eV0)
                ctx.eA.inc(e.activation(ctx.q_wbf[sl].ap()[:, 0:TQ],
                                        ctx.wgublk[sl].ap()[:, 0:TQ], AF.Copy), 1)
                ctx.eA.inc(e.activation(ctx.q_wab[sl].ap()[:, 0:TQ],
                                        ctx.wgublk[sl].ap()[:, 0:TQ], AF.Abs), 1)
                assert ctx.eA.n == eA0 + 2 * (n_ + 1)
            store(its[-1])
        blk.scalar(f_act)

        def f_pe(e):
            for n_, it in enumerate(its):
                sl = it % 2
                TQ = sum(x[2] for x in items[it][0])
                e.wait_ge(ctx.eA.h, eA0 + 2 * (n_ + 1))
                e.wait_ge(ctx.eV.h, eV0 + 8 * (n_ - 2) + 1 if n_ >= 2 else eV0)
                ctx.eP.inc(e.matmul(ctx.ps[sl].ap()[:, 0:TQ], ctx.g2.ap(),
                                    ctx.q_wab[sl].ap()[:, 0:TQ],
                                    start=True, stop=True), 1)
                assert ctx.eP.n == eP0 + n_ + 1
        blk.tensor(f_pe)

        def f_vec(e):
            for n_, it in enumerate(its):
                sl = it % 2
                TQ = sum(x[2] for x in items[it][0])
                wbf = ctx.q_wbf[sl].ap()[:, 0:TQ]
                sb_ = ctx.q_sb[sl].ap()[:, 0:TQ]
                e.wait_ge(ctx.eP.h, eP0 + n_ + 1)
                e.wait_ge(w[sl].h, st_pred(its[n_ - 2]) if n_ >= 2 else wb[sl])
                ctx.eV.inc(e.tensor_scalar(sb_, ctx.ps[sl].ap()[:, 0:TQ], 1e-8,
                                           None, ALU.max), 1)
                e.wait_ge(ctx.eV.h, ctx.eV.n)
                ctx.eV.inc(e.scalar_tensor_tensor(ctx.q_a[sl].ap()[:, 0:TQ], wbf,
                                                  2.0, sb_, ALU.mult, ALU.is_gt), 1)
                ctx.eV.inc(e.scalar_tensor_tensor(ctx.q_b[sl].ap()[:, 0:TQ], wbf,
                                                  -2.0, sb_, ALU.mult, ALU.is_gt), 1)
                e.wait_ge(ctx.eV.h, ctx.eV.n)
                ctx.eV.inc(e.tensor_tensor(ctx.q_q[sl].ap()[:, 0:TQ],
                                           ctx.q_a[sl].ap()[:, 0:TQ],
                                           ctx.q_b[sl].ap()[:, 0:TQ], ALU.subtract), 1)
                e.wait_ge(ctx.eV.h, ctx.eV.n)
                ctx.eV.inc(e.tensor_tensor(ctx.q_wh[sl].ap()[:, 0:TQ],
                                           ctx.q_q[sl].ap()[:, 0:TQ], sb_,
                                           ALU.mult), 1)
                e.wait_ge(ctx.eV.h, ctx.eV.n)
                ctx.eV.inc(e.tensor_tensor(ctx.q_d[sl].ap()[:, 0:TQ],
                                           ctx.q_wh[sl].ap()[:, 0:TQ], wbf,
                                           ALU.subtract), 1)
                e.wait_ge(ctx.eV.h, ctx.eV.n)
                ctx.eV.inc(e.tensor_tensor(ctx.q_fwd[sl].ap()[:, 0:TQ], wbf,
                                           ctx.q_d[sl].ap()[:, 0:TQ], ALU.add), 1)
                e.wait_ge(ctx.eV.h, ctx.eV.n)
                ctx.eV.inc(e.tensor_copy(ctx.q_l[sl].ap()[:, 0:TQ].bitcast(U16),
                                         ctx.q_fwd[sl].ap()[:, 0:TQ].bitcast(U16)), 1)
                assert ctx.eV.n == eV0 + 8 * (n_ + 1)
        blk.vector(f_vec)

    def emit_rest(self):
        while self.pos < len(self.items):
            self.emit_batch(32)


def emit_P4b(ctx):
    c = ctx.cfg
    blk = ctx.blk
    ND, STRIP, NSTRIP, NB4, TC, H = (c["ND"], c["STRIP"], c["NSTRIP"], c["NB4"],
                                     c["TC"], c["H"])
    r = [ctx.r0, ctx.r1]
    rw = [ctx.r2, ctx.r3]
    w = [ctx.w0, ctx.w1]

    def f_vz(e):
        ctx.eV.inc(e.memset(ctx.racc_pw.ap(), 0.0), 1)
    blk.vector(f_vz)
    racc_zeroed = ctx.eV.n

    for sub in range(NSTRIP):
        h2_loaded = []

        def f_syncH(e, sub=sub):
            e.wait_ge(ctx.eP.h, ctx.eP.n)
            e.wait_ge(ctx.r0.h, ctx.r0.n)
            for dc in range(ND):
                r[0].dma(e.dma_start(ctx.bfA.ap()[:, dc * STRIP:(dc + 1) * STRIP],
                                     ctx.s_h2_hi[128 * dc:128 * (dc + 1),
                                                 sub * STRIP:(sub + 1) * STRIP]))
                r[0].dma(e.dma_start(ctx.bfB.ap()[:, dc * STRIP:(dc + 1) * STRIP],
                                     ctx.s_h2_lo[128 * dc:128 * (dc + 1),
                                                 sub * STRIP:(sub + 1) * STRIP]))
            h2_loaded.append(r[0].n)
        blk.sync(f_syncH)

        for band in range(NB4):
            bk = 4 * (band % 2)
            eP0 = ctx.eP.n
            q_cnt = {}

            def f_syncQ(e, sub=sub, band=band, eP0=eP0, q_cnt=q_cnt):
                e.wait_ge(ctx.r2.h, ctx.r2.n)
                e.wait_ge(ctx.r3.h, ctx.r3.n)
                for kd in range(ND):
                    sl = kd % 2
                    e.wait_ge(ctx.eP.h, eP0 + kd - 1 if kd >= 2 else eP0)
                    rw[sl].dma(e.dma_start(
                        ctx.qst[sl].ap()[:, 0:512],
                        ctx.s_qgu[band, kd, :, :]))
                    q_cnt[kd] = rw[sl].n
            blk.sync(f_syncQ)

            def f_pe(e, sub=sub, band=band, bk=bk, eP0=eP0, q_cnt=q_cnt,
                     h2_loaded=h2_loaded):
                e.wait_ge(r[0].h, h2_loaded[0])
                e.wait_ge(ctx.eV.h, ctx.p4_ev.get((sub, band - 2), ctx.eV.n))
                e.wait_ge(ctx.eA.h, ctx.p4_ea.get((sub, band - 2), ctx.eA.n))
                for kd in range(ND):
                    sl = kd % 2
                    e.wait_ge(rw[sl].h, q_cnt[kd])
                    xh = ctx.bfA.ap()[:, kd * STRIP:(kd + 1) * STRIP]
                    xl = ctx.bfB.ap()[:, kd * STRIP:(kd + 1) * STRIP]
                    for hc in range(4):
                        wq = ctx.qst[sl].ap()[:, 128 * hc:128 * (hc + 1)]
                        e.matmul(ctx.ps[bk + hc].ap()[:, 0:STRIP], wq, xh,
                                 start=(kd == 0), stop=False)
                        i = e.matmul(ctx.ps[bk + hc].ap()[:, 0:STRIP], wq, xl,
                                     start=False, stop=(kd == ND - 1))
                    ctx.eP.inc(i, 1)
                    assert ctx.eP.n == eP0 + kd + 1
            blk.tensor(f_pe)
            grp_done = ctx.eP.n

            eA0, eV0 = ctx.eA.n, ctx.eV.n
            wbs = [ctx.w0.n, ctx.w1.n]

            def hst_done(j):
                return wbs[j % 2] + 32 * (j // 2 + 1)

            def f_actE(e, sub=sub, band=band, bk=bk, grp_done=grp_done,
                       eA0=eA0, eV0=eV0, wbs=wbs, hst_done=hst_done):
                e.wait_ge(ctx.eP.h, grp_done)
                for j in range(2):
                    sl = j % 2
                    e.wait_ge(ctx.eV.h, ctx.p4_sga.get((sub, band - 1, j), eV0))
                    ctx.eA.inc(e.activation(ctx.pSa_sl[sl][:, 0:STRIP],
                                            ctx.ps[bk + j].ap()[:, 0:STRIP],
                                            AF.Silu), 1)
                    e.wait_ge(ctx.eV.h, eV0 + 3 * j + 1)
                    e.wait_ge(w[sl].h, wbs[sl])
                    ctx.eA.inc(e.activation(ctx.yhi_t[sl].ap()[:, 0:STRIP],
                                            ctx.pSb_sl[sl][:, 0:STRIP], AF.Copy), 1)
                    e.wait_ge(ctx.eA.h, ctx.eA.n)
                    hrow = 128 * (2 * band + j)
                    ws = w[sl]
                    ws.dma(e.dma_start(ctx.s_hm_hi[hrow:hrow + 128,
                                                   sub * STRIP:(sub + 1) * STRIP],
                                       ctx.yhi_t[sl].ap()[:, 0:STRIP]))
                    e.wait_ge(ctx.eV.h, eV0 + 3 * j + 2)
                    ws.dma(e.dma_start(ctx.s_hm_lo[hrow:hrow + 128,
                                                   sub * STRIP:(sub + 1) * STRIP],
                                       ctx.ylo_t[sl].ap()[:, 0:STRIP]))
                    assert ws.n == hst_done(j)
                    e.wait_ge(ctx.eV.h, ctx.p4_sqv.get((sub, band - 1, j), eV0))
                    ctx.eA.inc(e.activation(ctx.pSc_sl[sl][:, 0:STRIP],
                                            ctx.pSb_sl[sl][:, 0:STRIP], AF.Square), 1)
                    assert ctx.eA.n == eA0 + 3 * (j + 1)
            blk.scalar(f_actE)
            ctx.p4_ea[(sub, band)] = ctx.eA.n

            def f_vecE(e, sub=sub, band=band, bk=bk, eA0=eA0, eV0=eV0,
                       racc_zeroed=racc_zeroed):
                for j in range(2):
                    sl = j % 2
                    e.wait_ge(ctx.eA.h, eA0 + 3 * j + 1)
                    e.wait_ge(ctx.eA.h, ctx.p4_hmfree.get((sub, band - 1, j), eA0))
                    ctx.eV.inc(e.tensor_tensor(ctx.pSb_sl[sl][:, 0:STRIP],
                                               ctx.pSa_sl[sl][:, 0:STRIP],
                                               ctx.ps[bk + 2 + j].ap()[:, 0:STRIP],
                                               ALU.mult), 1)
                    ctx.p4_sga[(sub, band, j)] = ctx.eV.n
                    e.wait_ge(ctx.eA.h, eA0 + 3 * j + 2)
                    e.wait_ge(w[sl].h, hst_done(j - 2) if j >= 2 else wbs[sl])
                    ctx.eV.inc(e.tensor_tensor(ctx.ylo_t[sl].ap()[:, 0:STRIP],
                                               ctx.pSb_sl[sl][:, 0:STRIP],
                                               ctx.yhi_t[sl].ap()[:, 0:STRIP],
                                               ALU.subtract), 1)
                    e.wait_ge(ctx.eA.h, eA0 + 3 * (j + 1))
                    e.wait_ge(ctx.eV.h, max(ctx.eV.n, racc_zeroed))
                    ctx.eV.inc(e.tensor_tensor(
                        ctx.racc_pw.ap()[:, sub * STRIP:(sub + 1) * STRIP],
                        ctx.racc_pw.ap()[:, sub * STRIP:(sub + 1) * STRIP],
                        ctx.pSc_sl[sl][:, 0:STRIP], ALU.add), 1)
                    ctx.p4_sqv[(sub, band, j)] = ctx.eV.n
                    assert ctx.eV.n == eV0 + 3 * (j + 1)
            blk.vector(f_vecE)
            ctx.p4_ev[(sub, band)] = ctx.eV.n
            ctx.p4_hmfree[(sub, band, 0)] = eA0 + 3
            ctx.p4_hmfree[(sub, band, 1)] = eA0 + 6

    eVr = ctx.eV.n

    def f_peR(e, eVr=eVr):
        e.wait_ge(ctx.eV.h, eVr)
        e.wait_ge(ctx.eA.h, ctx.eA.n)
        for sub in range(NSTRIP):
            ctx.eP.inc(e.matmul(ctx.ps[sub].ap()[0:1, 0:STRIP], ctx.ones128.ap(),
                                ctx.racc_pw.ap()[:, sub * STRIP:(sub + 1) * STRIP],
                                start=True, stop=True), 1)
    blk.tensor(f_peR)
    pR = ctx.eP.n

    def f_vecR(e, pR=pR):
        e.wait_ge(ctx.eP.h, pR)
        for sub in range(NSTRIP):
            ctx.eV.inc(e.tensor_copy(
                ctx.rows1.ap()[:, sub * STRIP:(sub + 1) * STRIP],
                ctx.ps[sub].ap()[0:1, 0:STRIP]), 1)
    blk.vector(f_vecR)
    ctx.p4_gp_done = 0


def emit_P5b(ctx):
    c = ctx.cfg
    blk = ctx.blk
    ND, NH, STRIP, NSTRIP, TC, H, DCG = (c["ND"], c["NH"], c["STRIP"],
                                         c["NSTRIP"], c["TC"], c["H"], c["DCG"])
    NG = ND // DCG
    WD = DCG * 128
    r = [ctx.r0, ctx.r1]
    rw = [ctx.r2, ctx.r3]
    w = [ctx.w0, ctx.w1]

    def f_vI(e):
        e.wait_ge(ctx.eV.h, ctx.eV.n)
        ctx.eV.inc(e.tensor_scalar(ctx.rows1.ap()[:, 0:TC], ctx.rows1.ap()[:, 0:TC],
                                   1.0 / H, None, ALU.mult), 1)
    blk.vector(f_vI)

    def f_aI(e):
        e.wait_ge(ctx.eV.h, ctx.eV.n)
        ctx.eA.inc(e.activation(ctx.rows1.ap()[:, 0:TC], ctx.rows1.ap()[:, 0:TC],
                                AF.Sqrt, bias=ctx.epsrms.ap()), 1)
    blk.scalar(f_aI)

    def f_vI2(e):
        e.wait_ge(ctx.eA.h, ctx.eA.n)
        ctx.eV.inc(e.reciprocal(ctx.rows1.ap()[:, 0:TC], ctx.rows1.ap()[:, 0:TC]), 1)
    blk.vector(f_vI2)
    inv_ready = ctx.eV.n

    for sub in range(NSTRIP):
        def f_peI(e, sub=sub):
            e.wait_ge(ctx.eV.h, ctx.eV.n)
            e.wait_ge(ctx.eP.h, ctx.eP.n)
            ctx.eP.inc(e.matmul(ctx.ps[7].ap()[:, 0:STRIP], ctx.onesrow.ap(),
                                ctx.rows1.ap()[:, sub * STRIP:(sub + 1) * STRIP],
                                start=True, stop=True), 1)
        blk.tensor(f_peI)
        p_invb = ctx.eP.n

        def f_vInvb(e, sub=sub, p_invb=p_invb):
            e.wait_ge(ctx.eP.h, p_invb)
            e.wait_ge(ctx.eV.h, ctx.eV.n)
            ctx.eV.inc(e.tensor_copy(ctx.invb.ap()[:, 0:STRIP],
                                     ctx.ps[7].ap()[:, 0:STRIP]), 1)
        blk.vector(f_vInvb)
        ctx.p5_invb_free[sub] = ctx.eV.n
        invb_ready = ctx.eV.n

        for dch in range(NG):
            gidx = ctx.p5_gidx
            ctx.p5_gidx += 1
            bk5 = 0 if DCG * 2 > 8 else DCG * (gidx % 2)
            eP0 = ctx.eP.n
            ld_cnt = {}

            def f_syncL(e, sub=sub, dch=dch, eP0=eP0, ld_cnt=ld_cnt):
                for x_ in (ctx.r0, ctx.r1, ctx.r2, ctx.r3):
                    e.wait_ge(x_.h, x_.n)
                for hk in range(NH):
                    sl = hk % 2
                    e.wait_ge(ctx.eP.h, eP0 + hk - 1 if hk >= 2 else eP0)
                    rw[sl].dma(e.dma_start(
                        ctx.qst[sl].ap()[:, 0:WD],
                        ctx.s_qp[hk, :, dch * WD:(dch + 1) * WD]))
                    r[sl].dma(e.dma_start(
                        ctx.hm_hi[sl].ap()[:, 0:STRIP],
                        ctx.s_hm_hi[128 * hk:128 * (hk + 1),
                                    sub * STRIP:(sub + 1) * STRIP]))
                    r[sl].dma(e.dma_start(
                        ctx.hm_lo[sl].ap()[:, 0:STRIP],
                        ctx.s_hm_lo[128 * hk:128 * (hk + 1),
                                    sub * STRIP:(sub + 1) * STRIP]))
                    ld_cnt[hk] = (rw[sl].n, r[sl].n)
            blk.sync(f_syncL)

            def f_pe(e, sub=sub, dch=dch, eP0=eP0, ld_cnt=ld_cnt,
                     invb_ready=invb_ready, gidx=gidx, bk5=bk5):
                e.wait_ge(ctx.eV.h, invb_ready)
                prev = gidx - (2 if DCG * 2 <= 8 else 1)
                for hk in range(NH):
                    sl = hk % 2
                    e.wait_ge(rw[sl].h, ld_cnt[hk][0])
                    e.wait_ge(r[sl].h, ld_cnt[hk][1])
                    for dci in range(DCG):
                        if hk == 0:
                            v = ctx.p5_bankfree.get((prev, bk5 + dci), 0)
                            e.wait_ge(ctx.eV.h, v)
                        wq = ctx.qst[sl].ap()[:, 128 * dci:128 * (dci + 1)]
                        e.matmul(ctx.ps[bk5 + dci].ap()[:, 0:STRIP], wq,
                                 ctx.hm_hi[sl].ap()[:, 0:STRIP],
                                 start=(hk == 0), stop=False)
                        i = e.matmul(ctx.ps[bk5 + dci].ap()[:, 0:STRIP], wq,
                                     ctx.hm_lo[sl].ap()[:, 0:STRIP],
                                     start=False, stop=(hk == NH - 1))
                    ctx.eP.inc(i, 1)
                    assert ctx.eP.n == eP0 + hk + 1
            blk.tensor(f_pe)
            grp_done = ctx.eP.n

            x1_cnt = {}

            eVx = ctx.eV.n

            def f_syncX(e, sub=sub, dch=dch, x1_cnt=x1_cnt, eVx=eVx):
                e.wait_ge(ctx.dsb.h, ctx.dsb.n)
                e.wait_ge(ctx.dsc.h, ctx.dsc.n)
                for dci in range(DCG):
                    sl = dci % 2
                    xs = [ctx.dsb, ctx.dsc][sl]
                    e.wait_ge(ctx.eV.h, eVx + 2 * (dci - 1) if dci >= 2 else eVx)
                    xs.dma(e.dma_start(
                        ctx.pSc_sl[sl][:, 0:STRIP],
                        ctx.s_x1[128 * (dch * DCG + dci):128 * (dch * DCG + dci + 1),
                                 sub * STRIP:(sub + 1) * STRIP]))
                    x1_cnt[dci] = (xs, xs.n)
            blk.sync(f_syncX)

            eV0 = ctx.eV.n
            obs = [ctx.out_sem.n, ctx.out2.n]
            osm = [ctx.out_sem, ctx.out2]

            def ostp(dci, obs=obs):
                return obs[dci % 2] + 16 * (dci // 2 + 1)

            def f_vecE(e, sub=sub, dch=dch, grp_done=grp_done, x1_cnt=x1_cnt,
                       eV0=eV0, invb_ready=invb_ready, obs=obs, osm=osm, ostp=ostp,
                       bk5=bk5):
                e.wait_ge(ctx.eP.h, grp_done)
                e.wait_ge(ctx.eV.h, invb_ready)
                for dci in range(DCG):
                    sl = dci % 2
                    dc_g = dch * DCG + dci
                    e.wait_ge(x1_cnt[dci][0].h, x1_cnt[dci][1])
                    ctx.eV.inc(e.tensor_tensor(ctx.pSa_sl[sl][:, 0:STRIP],
                                               ctx.ps[bk5 + dci].ap()[:, 0:STRIP],
                                               ctx.invb.ap()[:, 0:STRIP],
                                               ALU.mult), 1)
                    ctx.p5_bankfree[(gidx, bk5 + dci)] = ctx.eV.n
                    e.wait_ge(ctx.eV.h, ctx.eV.n)
                    e.wait_ge(osm[sl].h,
                              ostp(dci - 2) if dci >= 2 else obs[sl])
                    ctx.eV.inc(e.scalar_tensor_tensor(
                        ctx.pSb_sl[sl][:, 0:STRIP], ctx.pSa_sl[sl][:, 0:STRIP],
                        ctx.gb_cl.ap()[:, dc_g:dc_g + 1], ctx.pSc_sl[sl][:, 0:STRIP],
                        ALU.mult, ALU.add), 1)
                    assert ctx.eV.n == eV0 + 2 * (dci + 1)
            blk.vector(f_vecE)
            ctx.p5_gdone[gidx] = ctx.eV.n

            def f_actS(e, sub=sub, dch=dch, eV0=eV0, obs=obs, osm=osm, ostp=ostp):
                for dci in range(DCG):
                    sl = dci % 2
                    dc_g = dch * DCG + dci
                    e.wait_ge(ctx.eV.h, eV0 + 2 * (dci + 1))
                    e.wait_ge(osm[sl].h,
                              ostp(dci - 2) if dci >= 2 else obs[sl])
                    osm[sl].dma(e.dma_start(
                        ctx.out_ap[128 * dc_g:128 * (dc_g + 1),
                                   sub * STRIP:(sub + 1) * STRIP],
                        ctx.pSb_sl[sl][:, 0:STRIP]))
                    assert osm[sl].n == ostp(dci)
            blk.scalar(f_actS)


def emit_fin(ctx):
    c = ctx.cfg

    def f_fin(e):
        e.wait_ge(ctx.out_sem.h, ctx.out_sem.n)
        e.wait_ge(ctx.out2.h, ctx.out2.n)
    ctx.blk.scalar(f_fin)

    def f_fin2(e):
        e.wait_ge(ctx.out_sem.h, ctx.out_sem.n)
        e.wait_ge(ctx.out2.h, ctx.out2.n)
    ctx.blk.gpsimd(f_fin2)


def build_nc(cfg, debug_scratch=(), phases=99):
    c = cfg
    D, TC, W, H = c["D"], c["TC"], c["W"], c["H"]
    ND, NH, NMOD, SPAN, SUBW, STRIP = (c["ND"], c["NH"], c["NMOD"], c["SPAN"],
                                       c["SUBW"], c["STRIP"])
    NB4 = c["NB4"]
    nc = bass.Bass("TRN2", target_bir_lowering=False)
    ctx = Ctx()
    ctx.nc = nc
    ctx.cfg = cfg

    def dram(name, shape, dt, kind="Internal"):
        if name in debug_scratch:
            kind = "ExternalOutput"
        return nc.dram_tensor(name, shape, dt, kind=kind).ap()

    ctx.in_xT = dram("xT", [D, c["HCOLS"]], F32, "ExternalInput")
    ctx.in_wkT = dram("wkT", [D, D], F32, "ExternalInput")
    ctx.in_wvT = dram("wvT", [D, D], F32, "ExternalInput")
    ctx.in_wrT = dram("wrT", [D, D], F32, "ExternalInput")
    ctx.in_woT = dram("woT", [D, D], F32, "ExternalInput")
    ctx.in_wguT = dram("wguT", [D, 2 * H], F32, "ExternalInput")
    ctx.in_wprT = dram("wprT", [H, D], F32, "ExternalInput")
    ctx.in_wmod = dram("wmod_t", [NMOD, 128, ND * 128], F32, "ExternalInput")
    ctx.in_pvecs = dram("pvecs", [128, 6 * ND], F32, "ExternalInput")
    ctx.in_wmask = dram("wmask", [1, SPAN], F32, "ExternalInput")
    ctx.out_ap = dram("outT", [D, TC], F32, "ExternalOutput")

    ctx.s_wk_hi = dram("s_wk_hi", [D, D], BF16)
    ctx.s_wk_lo = dram("s_wk_lo", [D, D], BF16)
    ctx.s_wv_hi = dram("s_wv_hi", [D, D], BF16)
    ctx.s_wv_lo = dram("s_wv_lo", [D, D], BF16)
    ctx.s_wr_hi = dram("s_wr_hi", [D, D], BF16)
    ctx.s_wr_lo = dram("s_wr_lo", [D, D], BF16)
    ctx.s_wo_hi = dram("s_wo_hi", [D, D], BF16)
    ctx.s_wo_lo = dram("s_wo_lo", [D, D], BF16)
    ctx.s_h_hi = dram("s_h_hi", [D, c["HCOLS"]], BF16)
    ctx.s_h_lo = dram("s_h_lo", [D, c["HCOLS"]], BF16)
    ctx.s_k = dram("s_k", [D, SPAN], F32)
    ctx.s_v = dram("s_v", [D, SPAN], F32)
    ctx.s_sr = dram("s_sr", [D, SPAN], F32)
    ctx.s_y_hi = dram("s_y_hi", [D, TC], BF16)
    ctx.s_y_lo = dram("s_y_lo", [D, TC], BF16)
    ctx.s_x1 = dram("s_x1", [D, TC], F32)
    ctx.s_h2_hi = dram("s_h2_hi", [D, TC], BF16)
    ctx.s_h2_lo = dram("s_h2_lo", [D, TC], BF16)
    ctx.s_qgu = dram("s_qgu", [NB4, ND, 128, 512], BF16)
    ctx.s_hm_hi = dram("s_hm_hi", [H, TC], BF16)
    ctx.s_hm_lo = dram("s_hm_lo", [H, TC], BF16)
    ctx.s_qp = dram("s_qp", [NH, 128, D], BF16)

    SW = max(SUBW + 1, STRIP)
    PFW = max(SPAN, 2 * SW)
    BFW = max(ND * (2 * SUBW + 1) + 2, ND * STRIP)
    WSTW = c["ECG"] * 128
    QSTW = max(512, c["DCG"] * 128)

    def sb(name, shape, dt):
        return nc.alloc_sbuf_tensor(name, shape, dt)

    ctx.vecs = sb("vecs", [128, 6 * ND], F32)
    ctx.mod_sb = sb("mod_sb", [128, NMOD], F32)
    for nm in ["c1a", "c1b", "ga_cl", "gb_cl", "tmpg", "tmpg2", "silu_c", "tw",
               "eu16", "lam16", "negm", "macc"]:
        setattr(ctx, nm, sb(nm, [128, ND], F32))
    ctx.tm1m = [sb(f"tm1m{i}", [128, ND], F32) for i in range(3)]
    ctx.zs = sb("zs", [128, ND], F32)
    ctx.ones128 = sb("ones128", [128, 1], F32)
    ctx.tmax = sb("tmax", [128, 1], F32)
    ctx.maskneg = sb("maskneg", [128, W], F32)
    ctx.pF_a = sb("pF_a", [128, PFW], F32)
    ctx.pF_b = sb("pF_b", [128, PFW], F32)
    ctx.pF_a2 = sb("pF_a2", [128, c["SPAN"]], F32)
    ctx.pF_b2 = sb("pF_b2", [128, c["SPAN"]], F32)
    ctx.pF_c = sb("pF_c", [128, PFW], F32)
    ctx.pF_d = sb("pF_d", [128, PFW], F32)
    ctx.pFa_sl = [ctx.pF_a.ap()[:, i * SW:(i + 1) * SW] for i in range(2)]
    ctx.pFb_sl = [ctx.pF_b.ap()[:, i * SW:(i + 1) * SW] for i in range(2)]
    ctx.pFc_sl = [ctx.pF_c.ap()[:, i * SW:(i + 1) * SW] for i in range(2)]
    ctx.pFd_sl = [ctx.pF_d.ap()[:, i * SW:(i + 1) * SW] for i in range(2)]
    ctx.pS_a = sb("pS_a", [128, 2 * STRIP], F32)
    ctx.pS_b = sb("pS_b", [128, 2 * STRIP], F32)
    ctx.pS_c = sb("pS_c", [128, 2 * STRIP], F32)
    ctx.pSa_sl = [ctx.pS_a.ap()[:, i * STRIP:(i + 1) * STRIP] for i in range(2)]
    ctx.pSb_sl = [ctx.pS_b.ap()[:, i * STRIP:(i + 1) * STRIP] for i in range(2)]
    ctx.pSc_sl = [ctx.pS_c.ap()[:, i * STRIP:(i + 1) * STRIP] for i in range(2)]
    ctx.wblkF = [sb(f"wblkF{i}", [128, max(D, ND * 128)], F32) for i in range(2)]
    ctx.slab = ctx.wblkF
    ctx.wgublk = [sb(f"wgublk{i}", [128, 512], F32) for i in range(2)]
    ctx.invb = sb("invb", [128, STRIP], F32)
    ctx.rows1 = sb("rows1", [1, TC], F32)
    ctx.rows2 = sb("rows2", [1, 3 * 512], F32)
    ctx.onesrow = sb("onesrow", [1, 128], F32)
    ctx.mrow = sb("mrow", [1, W], F32)
    ctx.epsln = sb("epsln", [1, 1], F32)
    ctx.epsrms = sb("epsrms", [1, 1], F32)
    ctx.racc_pw = sb("racc_pw", [128, TC], F32)

    ctx.bfA = sb("bfA", [128, BFW], BF16)
    ctx.bfB = sb("bfB", [128, BFW], BF16)
    ctx.XOFF = ND * (SUBW + 1)
    ctx.wst_hi = [sb(f"wsth{i}", [128, WSTW], BF16) for i in range(2)]
    ctx.wst_lo = [sb(f"wstl{i}", [128, WSTW], BF16) for i in range(2)]
    ctx.whi_t = [ctx.bfA.ap()[:, i * D:(i + 1) * D] for i in range(2)]
    ctx.wlo_t = [ctx.bfB.ap()[:, i * D:(i + 1) * D] for i in range(2)]
    ctx.hhi_t = [sb(f"hhit{i}", [128, SW], BF16) for i in range(2)]
    ctx.hlo_t = [sb(f"hlot{i}", [128, SW], BF16) for i in range(2)]
    ctx.yhi_t = [sb(f"yhit{i}", [128, STRIP], BF16) for i in range(2)]
    ctx.ylo_t = [sb(f"ylot{i}", [128, STRIP], BF16) for i in range(2)]
    for nm in ["q_wbf", "q_wab", "q_l"]:
        setattr(ctx, nm, [sb(f"{nm}{i}", [128, 512], BF16) for i in range(2)])
    for nm in ["q_sb", "q_a", "q_b", "q_q", "q_wh", "q_d", "q_fwd"]:
        t = sb(nm, [128, 512], BF16)
        setattr(ctx, nm, [t, t])
    ctx.qst = [sb(f"qst{i}", [128, QSTW], BF16) for i in range(2)]
    ctx.hm_hi = [sb(f"hmhi{i}", [128, STRIP], BF16) for i in range(2)]
    ctx.hm_lo = [sb(f"hmlo{i}", [128, STRIP], BF16) for i in range(2)]
    ctx.g2 = sb("g2", [128, 128], BF16)

    ctx.ps = [nc.alloc_psum_tensor(f"ps{i}", [128, 512], F32) for i in range(8)]

    ctx.sbuf_left = nc.sbuf_bytes_remaining

    ctx.slab_loaded = {}
    ctx.hst_done = {}
    ctx.p2_wl = {}
    ctx.p2_st = {}
    ctx.p2_gdone = {}
    ctx.p2_bankfree = {}
    ctx.p2_gidx = 0
    ctx.p3b_gdone = {}
    ctx.p3b_bankfree = {}
    ctx.p3b_gidx = 0
    ctx.p3b_st = {}
    ctx.p3b_xfree = {}
    ctx.p4_ev = {}
    ctx.p4_ea = {}
    ctx.p4_sga = {}
    ctx.p4_sqv = {}
    ctx.p4_hmfree = {}
    ctx.p4_gp_done = 0
    ctx.p5_gdone = {}
    ctx.p5_bankfree = {}
    ctx.p5_gidx = 0
    ctx.p5_st = {}
    ctx.p5_xfree = {}
    ctx.p5_invb_free = {}

    with ExitStack() as stack:
        led = Ledger()
        for nm in ["eS", "eA", "eV", "eP", "eG", "r0", "r1", "r2", "r3",
                   "w0", "w1", "dsa", "dsb", "dsc", "out_sem", "out2"]:
            setattr(ctx, nm, led.new(stack, nc, nm))
        blk = stack.enter_context(nc.Block())
        ctx.blk = blk

        ph = 0

        def nxt():
            nonlocal ph
            ph += 1
            return ph <= phases
        if nxt():
            emit_P0(ctx)
            barrier(ctx)
        if nxt():
            emit_P2a(ctx)
            barrier(ctx)
        if nxt():
            ln1w = [SUBW + 1] + [SUBW] * (c["NSUB"] - 1)
            emit_LN(ctx, ctx.in_xT, ctx.s_h_hi, ctx.s_h_lo, ctx.c1a.ap(), 0, ln1w, "ln1")
            barrier(ctx)
        if nxt():
            emit_P2(ctx)
            barrier(ctx)
        items4 = []
        for band in range(NB4):
            for kd in range(ND):
                items4.append((
                    [(ctx.in_wguT[128 * kd:128 * (kd + 1), 256 * band:256 * (band + 1)], 0, 256),
                     (ctx.in_wguT[128 * kd:128 * (kd + 1), H + 256 * band:H + 256 * (band + 1)], 256, 256)],
                    ctx.s_qgu[band, kd, :, :]))
        items5 = []
        QW = min(512, D)
        for hk in range(NH):
            for b in range(D // QW):
                items5.append((
                    [(ctx.in_wprT[128 * hk:128 * (hk + 1), QW * b:QW * (b + 1)], 0, QW)],
                    ctx.s_qp[hk, :, QW * b:QW * (b + 1)]))
        qs = QuantStream(ctx, items4 + items5, "q")
        ctx.quant_stream = qs
        if nxt():
            emit_P3(ctx)
            barrier(ctx)
        if nxt():
            emit_P3b(ctx)
            barrier(ctx)
        if nxt():
            ln2w = [STRIP] * c["NSTRIP"]
            emit_LN(ctx, ctx.s_x1, ctx.s_h2_hi, ctx.s_h2_lo, ctx.c1b.ap(), 3, ln2w, "ln2")
            barrier(ctx)
        if nxt():
            qs.emit_rest()
            barrier(ctx)
        if nxt():
            emit_P4b(ctx)
            barrier(ctx)
        if nxt():
            barrier(ctx)
        if nxt():
            emit_P5b(ctx)
        emit_fin(ctx)
    return nc




CFG = make_cfg(D=2048, TC=2048, W=192, H=8192, SUBW=448, n_cores=8)
_NC_CACHE = []


def kernel(**inputs):
    from concourse.bass_utils import run_bass_kernel_spmd
    cfg = CFG
    in_maps = host_prepare(cfg, inputs)
    if not _NC_CACHE:
        _NC_CACHE.append(build_nc(cfg))
    nc = _NC_CACHE[0]
    r = run_bass_kernel_spmd(nc, in_maps, core_ids=list(range(cfg["n_cores"])))
    return host_gather(cfg, r.results)


# version 08
# speedup vs baseline: 1.1668x; 1.0888x over previous
import numpy as np
import concourse.bass as bass
import concourse.mybir as mybir
from concourse.bass_isa import ReduceOp
from contextlib import ExitStack

F32 = mybir.dt.float32
BF16 = mybir.dt.bfloat16
U16 = mybir.dt.uint16
ALU = mybir.AluOpType
AF = mybir.ActivationFunctionType
AX = mybir.AxisListType

EPS64C = float(np.float32(np.finfo(np.float64).eps))


def make_cfg(D=2048, TC=2048, W=192, H=8192, SUBW=448, n_cores=8, ecg=None, dcg=None):
    c = {}
    c["D"] = D; c["TC"] = TC; c["W"] = W; c["H"] = H
    c["ND"] = D // 128
    c["NH"] = H // 128
    c["SPAN"] = W + TC
    assert c["SPAN"] % SUBW == 0
    c["SUBW"] = SUBW
    c["NSUB"] = c["SPAN"] // SUBW
    c["HCOLS"] = W + 1 + TC
    c["MODN"] = 6 * D
    c["NMOD"] = c["MODN"] // 128
    c["STRIP"] = min(512, TC)
    assert TC % c["STRIP"] == 0
    c["NSTRIP"] = TC // c["STRIP"]
    c["ECG"] = ecg or min(8, c["ND"])
    assert c["ND"] % c["ECG"] == 0
    c["NB4"] = H // 256
    c["DCG"] = dcg or min(8, c["ND"])
    c["n_cores"] = n_cores
    return c


class Ledger:
    def __init__(self):
        self.sems = []

    def new(self, stack, nc, name):
        h = stack.enter_context(nc.semaphore(name))
        s = Sem(h, name)
        self.sems.append(s)
        return s


class Sem:
    def __init__(self, h, name):
        self.h = h
        self.name = name
        self.n = 0

    def inc(self, instr, k=1):
        instr.then_inc(self.h, k)
        self.n += k
        return self.n

    def dma(self, instr):
        return self.inc(instr, 16)


def host_prepare(cfg, inputs):
    D, TC, W, H = cfg["D"], cfg["TC"], cfg["W"], cfg["H"]
    n = cfg["n_cores"]
    T = TC * n
    x = np.ascontiguousarray(inputs["L0_attn_x"].reshape(T, D))
    wkT = np.ascontiguousarray(inputs["L0_attn_Wk"].T)
    wvT = np.ascontiguousarray(inputs["L0_attn_Wv"].T)
    wrT = np.ascontiguousarray(inputs["L0_attn_Wr"].T)
    woT = np.ascontiguousarray(inputs["L0_attn_Wo"].T)
    wguT = np.ascontiguousarray(inputs["L0_ffn_w_gate_up"].T)
    wprT = np.ascontiguousarray(inputs["L0_ffn_w_proj"].T)
    wmod = inputs["L0_wmod"]
    ND, NMOD = cfg["ND"], cfg["NMOD"]
    wmod_t = np.ascontiguousarray(
        wmod.reshape(ND, 128, NMOD, 128).transpose(2, 1, 0, 3).reshape(NMOD, 128, ND * 128))

    def as_cols(v):
        return np.ascontiguousarray(np.asarray(v).reshape(ND, 128).T)

    pvecs = np.concatenate([
        as_cols(inputs["L0_c"]), as_cols(inputs["L0_attn_time_decay"]),
        as_cols(inputs["L0_attn_time_first"]), as_cols(inputs["L0_attn_time_mix_k"]),
        as_cols(inputs["L0_attn_time_mix_v"]), as_cols(inputs["L0_attn_time_mix_r"]),
    ], axis=1)

    in_maps = []
    for c in range(n):
        t0 = c * TC
        lo = t0 - W - 1
        xs = np.zeros((cfg["HCOLS"], D), dtype=np.float32)
        s = max(0, lo)
        xs[s - lo:] = x[s:t0 + TC]
        xT = np.ascontiguousarray(xs.T)
        wmask = np.ones((1, cfg["SPAN"]), dtype=np.float32)
        if c == 0:
            wmask[0, :W] = 0.0
        in_maps.append({
            "xT": xT, "wkT": wkT, "wvT": wvT, "wrT": wrT, "woT": woT,
            "wguT": wguT, "wprT": wprT, "wmod_t": wmod_t,
            "pvecs": pvecs, "wmask": wmask,
        })
    return in_maps


def host_gather(cfg, results):
    D, TC, n = cfg["D"], cfg["TC"], cfg["n_cores"]
    out = np.empty((1, TC * n, D), dtype=np.float32)
    for c in range(n):
        out[0, c * TC:(c + 1) * TC] = results[c]["outT"].T
    return out



class Ctx:
    pass


def barrier(ctx):
    nc, blk = ctx.nc, ctx.blk
    engs = [("sync", ctx.eS), ("scalar", ctx.eA), ("vector", ctx.eV),
            ("tensor", ctx.eP), ("gpsimd", ctx.eG)]
    for name, sem in engs:
        def mk(sem=sem, v=sem.n):
            def f(e):
                e.wait_ge(sem.h, v)
                sem.inc(e.nop(), 1)
            return f
        getattr(blk, name)(mk())
    dmas = [ctx.r0, ctx.r1, ctx.r2, ctx.r3, ctx.w0, ctx.w1,
            ctx.dsa, ctx.dsb, ctx.dsc, ctx.out_sem, ctx.out2]
    snap = [(sem, sem.n) for _, sem in engs] + [(x, x.n) for x in dmas]
    for name, _ in engs:
        def mk2():
            def f(e):
                for sem, v in snap:
                    e.wait_ge(sem.h, v)
            return f
        getattr(blk, name)(mk2())


def emit_P0(ctx):
    c = ctx.cfg
    nc, blk = ctx.nc, ctx.blk
    ND, NMOD, W = c["ND"], c["NMOD"], c["W"]

    vec_done = []

    def f_sync(e):
        ctx.dsa.dma(e.dma_start(ctx.vecs.ap(), ctx.in_pvecs))
        ctx.dsa.dma(e.dma_start(ctx.mrow.ap(), ctx.in_wmask[:, 0:W]))
        vec_done.append(ctx.dsa.n)
        pe_base = ctx.eP.n
        for m in range(NMOD):
            if m >= 2:
                e.wait_ge(ctx.eP.h, pe_base + m - 1)
            s = [ctx.dsb, ctx.dsc][m % 2]
            s.dma(e.dma_start(ctx.slab[m % 2].ap(), ctx.in_wmod[m, :, :]))
            ctx.slab_loaded[m] = s.n
    blk.sync(f_sync)

    def f_act(e):
        e.wait_ge(ctx.dsa.h, vec_done[0])
        v = ctx.vecs.ap()
        e.activation(ctx.silu_c.ap(), v[:, 0:ND], AF.Silu)
        e.activation(ctx.tw.ap(), v[:, ND:2 * ND], AF.Exp)
        ctx.eA.inc(e.activation(ctx.eu16.ap(), v[:, 2 * ND:3 * ND], AF.Exp), 1)
        e.wait_ge(ctx.eA.h, ctx.eA.n)
        ctx.eA.inc(e.activation(ctx.lam16.ap(), ctx.tw.ap(), AF.Exp, scale=-1.0), 1)
    blk.scalar(f_act)
    act_ready = ctx.eA.n

    def f_vec(e):
        e.wait_ge(ctx.dsa.h, vec_done[0])
        e.memset(ctx.g2.ap()[0:64, 64:128], 0.0)
        e.memset(ctx.g2.ap()[64:128, 0:64], 0.0)
        e.memset(ctx.g2.ap()[0:64, 0:64], 1.0 / 64.0)
        e.memset(ctx.g2.ap()[64:128, 64:128], 1.0 / 64.0)
        e.memset(ctx.zs.ap(), 0.0)
        e.memset(ctx.ones128.ap(), 1.0)
        e.memset(ctx.onesrow.ap(), 1.0)
        e.memset(ctx.epsln.ap(), 1e-5)
        e.memset(ctx.epsrms.ap(), EPS64C)
        e.memset(ctx.macc.ap(), -1e30)
        v = ctx.vecs.ap()
        for i in range(3):
            e.tensor_scalar(ctx.tm1m[i].ap(), v[:, (3 + i) * ND:(4 + i) * ND],
                            -1.0, 1.0, ALU.mult, ALU.add)
        ctx.eV.inc(e.tensor_scalar(ctx.mrow.ap(), ctx.mrow.ap(), -1.0, 1e30,
                                   ALU.add, ALU.mult), 1)
    blk.vector(f_vec)
    vec_ready = ctx.eV.n

    def f_pe(e):
        e.wait_ge(ctx.eA.h, act_ready)
        e.wait_ge(ctx.eV.h, vec_ready)
        for m in range(NMOD):
            s = [ctx.dsb, ctx.dsc][m % 2]
            e.wait_ge(s.h, ctx.slab_loaded[m])
            sl = ctx.slab[m % 2].ap()
            for dc in range(ND):
                i = e.matmul(ctx.ps[7].ap()[:, m:m + 1],
                             sl[:, 128 * dc:128 * (dc + 1)],
                             ctx.silu_c.ap()[:, dc:dc + 1],
                             start=(dc == 0), stop=(dc == ND - 1))
                if dc == ND - 1:
                    ctx.eP.inc(i, 1)
        ctx.eP.inc(e.matmul(ctx.ps[6].ap()[:, 0:W], ctx.onesrow.ap(),
                            ctx.mrow.ap(), start=True, stop=True), 1)
    blk.tensor(f_pe)
    pe_mod = ctx.eP.n

    def f_vec2(e):
        e.wait_ge(ctx.eP.h, pe_mod)
        ctx.eV.inc(e.tensor_copy(ctx.mod_sb.ap(), ctx.ps[7].ap()[:, 0:ctx.cfg["NMOD"]]), 1)
        e.tensor_copy(ctx.maskneg.ap(), ctx.ps[6].ap()[:, 0:W])
        e.wait_ge(ctx.eV.h, ctx.eV.n)
        m = ctx.mod_sb.ap()
        e.tensor_scalar(ctx.c1a.ap(), m[:, 1 * ND:2 * ND], 1.0, None, ALU.add)
        e.tensor_scalar(ctx.c1b.ap(), m[:, 4 * ND:5 * ND], 1.0, None, ALU.add)
        for (g_t, q) in ((ctx.ga_cl, 2), (ctx.gb_cl, 5)):
            mq = m[:, q * ND:(q + 1) * ND]
            e.wait_ge(ctx.eV.h, ctx.eV.n)
            ctx.eV.inc(e.tensor_tensor(ctx.tmpg.ap(), mq, ctx.zs.ap(), ALU.is_ge), 1)
            ctx.eV.inc(e.tensor_scalar(ctx.tmpg2.ap(), mq, -1.0, None, ALU.mult), 1)
            e.wait_ge(ctx.eV.h, ctx.eV.n)
            ctx.eV.inc(e.tensor_scalar(ctx.tmpg.ap(), ctx.tmpg.ap(), 2.0, -1.0,
                                       ALU.mult, ALU.add), 1)
            ctx.eV.inc(e.tensor_tensor(ctx.tmpg2.ap(), ctx.tmpg2.ap(), mq,
                                       ALU.max), 1)
            e.wait_ge(ctx.eV.h, ctx.eV.n)
            ctx.eV.inc(e.tensor_scalar(ctx.tmpg2.ap(), ctx.tmpg2.ap(), 0.25,
                                       None, ALU.max), 1)
            e.wait_ge(ctx.eV.h, ctx.eV.n)
            ctx.eV.inc(e.tensor_tensor(g_t.ap(), ctx.tmpg.ap(), ctx.tmpg2.ap(),
                                       ALU.mult), 1)
    blk.vector(f_vec2)


def emit_P2a(ctx):
    c = ctx.cfg
    blk = ctx.blk
    ND, D = c["ND"], c["D"]
    mats_in = [ctx.in_wkT, ctx.in_wvT, ctx.in_wrT, ctx.in_woT]
    mats_hi = [ctx.s_wk_hi, ctx.s_wv_hi, ctx.s_wr_hi, ctx.s_wo_hi]
    mats_lo = [ctx.s_wk_lo, ctx.s_wv_lo, ctx.s_wr_lo, ctx.s_wo_lo]
    n_it = 4 * ND
    eA0, eV0 = ctx.eA.n, ctx.eV.n
    r = [ctx.r0, ctx.r1]
    w = [ctx.w0, ctx.w1]
    loaded = {}

    def f_sync(e):
        e.wait_ge(ctx.r0.h, ctx.r0.n)
        e.wait_ge(ctx.r1.h, ctx.r1.n)
        for it in range(n_it):
            if it >= 2:
                e.wait_ge(ctx.eA.h, eA0 + (it - 1))
                e.wait_ge(ctx.eV.h, eV0 + (it - 1))
            mat, dc = divmod(it, ND)
            sl = it % 2
            r[sl].dma(e.dma_start(ctx.wblkF[sl].ap(),
                                  mats_in[mat][128 * dc:128 * (dc + 1), :]))
            loaded[it] = r[sl].n
    blk.sync(f_sync)

    whi_done = {}

    def f_act(e):
        for it in range(n_it):
            mat, dc = divmod(it, ND)
            sl = it % 2
            e.wait_ge(r[sl].h, loaded[it])
            if it >= 2:
                e.wait_ge(w[sl].h, whi_done.get(it - 2, 0))
            i = e.activation(ctx.whi_t[sl], ctx.wblkF[sl].ap(), AF.Copy)
            ctx.eA.inc(i, 1)
            assert ctx.eA.n == eA0 + it + 1
            e.wait_ge(ctx.eA.h, ctx.eA.n)
            w[sl].dma(e.dma_start(mats_hi[mat][128 * dc:128 * (dc + 1), :],
                                  ctx.whi_t[sl]))
            whi_done[it] = w[sl].n
            e.wait_ge(ctx.eV.h, eV0 + it + 1)
            w[sl].dma(e.dma_start(mats_lo[mat][128 * dc:128 * (dc + 1), :],
                                  ctx.wlo_t[sl]))
            whi_done[it] = w[sl].n
    blk.scalar(f_act)

    def f_vec(e):
        for it in range(n_it):
            sl = it % 2
            e.wait_ge(r[sl].h, loaded[it])
            e.wait_ge(ctx.eA.h, eA0 + it + 1)
            if it >= 2:
                e.wait_ge(w[sl].h, whi_done.get(it - 2, 0))
            i = e.tensor_tensor(ctx.wlo_t[sl], ctx.wblkF[sl].ap(),
                                ctx.whi_t[sl], ALU.subtract)
            ctx.eV.inc(i, 1)
            assert ctx.eV.n == eV0 + it + 1
    blk.vector(f_vec)


def emit_LN(ctx, src, dst_hi, dst_lo, c1_t, shift_comp, widths, tag):
    c = ctx.cfg
    blk = ctx.blk
    ND, D, W = c["ND"], c["D"], c["W"]
    NSUB = len(widths)
    offs = np.cumsum([0] + list(widths)).tolist()
    r = [ctx.r0, ctx.r1]
    w = [ctx.w0, ctx.w1]

    for sub in range(NSUB):
        wd, off = widths[sub], offs[sub]
        eP0, eA0, eV0 = ctx.eP.n, ctx.eA.n, ctx.eV.n
        loadedA = {}

        def f_syncA(e, wd=wd, off=off, eP0=eP0, loadedA=loadedA):
            e.wait_ge(ctx.r0.h, ctx.r0.n)
            e.wait_ge(ctx.r1.h, ctx.r1.n)
            e.wait_ge(ctx.eV.h, ctx.eV.n)
            for dc in range(ND):
                if dc >= 2:
                    e.wait_ge(ctx.eP.h, eP0 + dc - 1)
                sl = dc % 2
                r[sl].dma(e.dma_start(ctx.pFa_sl[sl][:, 0:wd],
                                      src[128 * dc:128 * (dc + 1), off:off + wd]))
                loadedA[dc] = r[sl].n
        blk.sync(f_syncA)

        def f_actA(e, wd=wd, eA0=eA0, eP0=eP0, loadedA=loadedA):
            for dc in range(ND):
                sl = dc % 2
                e.wait_ge(r[sl].h, loadedA[dc])
                if dc >= 2:
                    e.wait_ge(ctx.eP.h, eP0 + dc - 1)
                i = e.activation(ctx.pFb_sl[sl][:, 0:wd],
                                 ctx.pFa_sl[sl][:, 0:wd], AF.Square)
                ctx.eA.inc(i, 1)
                assert ctx.eA.n == eA0 + dc + 1
        blk.scalar(f_actA)

        eVpre = ctx.eV.n

        def f_peA(e, wd=wd, eA0=eA0, eP0=eP0, loadedA=loadedA, eVpre=eVpre):
            e.wait_ge(ctx.eV.h, eVpre)
            for dc in range(ND):
                sl = dc % 2
                e.wait_ge(r[sl].h, loadedA[dc])
                e.wait_ge(ctx.eA.h, eA0 + dc + 1)
                e.matmul(ctx.ps[6].ap()[0:1, 0:wd], ctx.ones128.ap(),
                         ctx.pFa_sl[sl][:, 0:wd],
                         start=(dc == 0), stop=(dc == ND - 1))
                i = e.matmul(ctx.ps[5].ap()[0:1, 0:wd], ctx.ones128.ap(),
                             ctx.pFb_sl[sl][:, 0:wd],
                             start=(dc == 0), stop=(dc == ND - 1))
                ctx.eP.inc(i, 1)
                assert ctx.eP.n == eP0 + dc + 1
        blk.tensor(f_peA)

        S = 512
        rws = ctx.rows2.ap()
        r0s, r1s, r2s = rws[:, 0:wd], rws[:, S:S + wd], rws[:, 2 * S:2 * S + wd]

        def f_vecS(e, wd=wd, eP0=eP0, r0s=r0s, r1s=r1s, r2s=r2s):
            e.wait_ge(ctx.eP.h, eP0 + ND)
            e.tensor_copy(r0s, ctx.ps[6].ap()[0:1, 0:wd])
            ctx.eV.inc(e.tensor_copy(r1s, ctx.ps[5].ap()[0:1, 0:wd]), 1)
            e.wait_ge(ctx.eV.h, ctx.eV.n)
            ctx.eV.inc(e.tensor_scalar(r2s, r0s, 1.0 / ctx.cfg["D"], None, ALU.mult), 1)
            e.wait_ge(ctx.eV.h, ctx.eV.n)
            ctx.eV.inc(e.tensor_tensor(r0s, r2s, r2s, ALU.mult), 1)
            e.wait_ge(ctx.eV.h, ctx.eV.n)
            ctx.eV.inc(e.scalar_tensor_tensor(r1s, r1s, 1.0 / ctx.cfg["D"], r0s,
                                              ALU.mult, ALU.subtract), 1)
        blk.vector(f_vecS)
        v_var = ctx.eV.n

        def f_actS(e, r1s=r1s, v_var=v_var):
            e.wait_ge(ctx.eV.h, v_var)
            ctx.eA.inc(e.activation(r1s, r1s, AF.Sqrt, bias=ctx.epsln.ap()), 1)
        blk.scalar(f_actS)
        a_sd = ctx.eA.n

        def f_vecS2(e, r1s=r1s, r2s=r2s, a_sd=a_sd):
            e.wait_ge(ctx.eA.h, a_sd)
            ctx.eV.inc(e.reciprocal(r1s, r1s), 1)
            ctx.eV.inc(e.tensor_scalar(r2s, r2s, -1.0, None, ALU.mult), 1)
        blk.vector(f_vecS2)
        v_rows = ctx.eV.n

        def f_peB(e, wd=wd, r1s=r1s, r2s=r2s, v_rows=v_rows):
            e.wait_ge(ctx.eV.h, v_rows)
            e.matmul(ctx.ps[4].ap()[:, 0:wd], ctx.onesrow.ap(), r2s,
                     start=True, stop=True)
            ctx.eP.inc(e.matmul(ctx.ps[3].ap()[:, 0:wd], ctx.onesrow.ap(), r1s,
                                start=True, stop=True), 1)
        blk.tensor(f_peB)
        p_bcast = ctx.eP.n

        eV1, eA1 = ctx.eV.n, ctx.eA.n
        loadedB = {}

        def f_syncB(e, wd=wd, off=off, eV1=eV1, loadedB=loadedB):
            e.wait_ge(ctx.r0.h, ctx.r0.n)
            e.wait_ge(ctx.r1.h, ctx.r1.n)
            e.wait_ge(ctx.eP.h, ctx.eP.n)
            for dc in range(ND):
                if dc >= 2:
                    e.wait_ge(ctx.eV.h, eV1 + 4 * (dc - 1))
                sl = dc % 2
                r[sl].dma(e.dma_start(ctx.pFa_sl[sl][:, 0:wd],
                                      src[128 * dc:128 * (dc + 1), off:off + wd]))
                loadedB[dc] = r[sl].n
        blk.sync(f_syncB)

        def f_vecB(e, wd=wd, eV1=eV1, eA1=eA1, p_bcast=p_bcast, loadedB=loadedB):
            e.wait_ge(ctx.eP.h, p_bcast)
            wb = [ctx.w0.n, ctx.w1.n]
            for dc in range(ND):
                sl = dc % 2
                e.wait_ge(r[sl].h, loadedB[dc])
                e.wait_ge(ctx.w0.h if sl == 0 else ctx.w1.h,
                          ctx.hst_done.get((tag, sub, dc - 2), wb[sl]))
                ctx.eV.inc(e.tensor_tensor(ctx.pFc_sl[sl][:, 0:wd],
                                           ctx.pFa_sl[sl][:, 0:wd],
                                           ctx.ps[4].ap()[:, 0:wd], ALU.add), 1)
                e.wait_ge(ctx.eV.h, ctx.eV.n)
                ctx.eV.inc(e.tensor_tensor(ctx.pFc_sl[sl][:, 0:wd],
                                           ctx.pFc_sl[sl][:, 0:wd],
                                           ctx.ps[3].ap()[:, 0:wd], ALU.mult), 1)
                e.wait_ge(ctx.eV.h, ctx.eV.n)
                ctx.eV.inc(e.scalar_tensor_tensor(
                    ctx.pFd_sl[sl][:, 0:wd], ctx.pFc_sl[sl][:, 0:wd],
                    c1_t[:, dc:dc + 1],
                    ctx.mod_sb.ap()[:, shift_comp * ND + dc:shift_comp * ND + dc + 1]
                    .broadcast_to([128, wd]),
                    ALU.mult, ALU.add), 1)
                e.wait_ge(ctx.eA.h, eA1 + dc + 1)
                ctx.eV.inc(e.tensor_tensor(ctx.hlo_t[sl].ap()[:, 0:wd],
                                           ctx.pFd_sl[sl][:, 0:wd],
                                           ctx.hhi_t[sl].ap()[:, 0:wd],
                                           ALU.subtract), 1)
                assert ctx.eV.n == eV1 + 4 * (dc + 1)
        blk.vector(f_vecB)

        def f_actB(e, wd=wd, off=off, eV1=eV1, eA1=eA1):
            wb = [ctx.w0.n, ctx.w1.n]
            for dc in range(ND):
                sl = dc % 2
                e.wait_ge(ctx.eV.h, eV1 + 4 * dc + 3)
                e.wait_ge(ctx.w0.h if sl == 0 else ctx.w1.h,
                          ctx.hst_done.get((tag, sub, dc - 2), wb[sl]))
                ctx.eA.inc(e.activation(ctx.hhi_t[sl].ap()[:, 0:wd],
                                        ctx.pFd_sl[sl][:, 0:wd], AF.Copy), 1)
                assert ctx.eA.n == eA1 + dc + 1
                e.wait_ge(ctx.eA.h, ctx.eA.n)
                ws = w[sl]
                ws.dma(e.dma_start(dst_hi[128 * dc:128 * (dc + 1), off:off + wd],
                                   ctx.hhi_t[sl].ap()[:, 0:wd]))
                e.wait_ge(ctx.eV.h, eV1 + 4 * (dc + 1))
                ws.dma(e.dma_start(dst_lo[128 * dc:128 * (dc + 1), off:off + wd],
                                   ctx.hlo_t[sl].ap()[:, 0:wd]))
                ctx.hst_done[(tag, sub, dc)] = ws.n
        blk.scalar(f_actB)



def emit_P2(ctx):
    c = ctx.cfg
    blk = ctx.blk
    ND, SUBW, NSUB, ECG = c["ND"], c["SUBW"], c["NSUB"], c["ECG"]
    NG = ND // ECG
    s_dst = [ctx.s_k, ctx.s_v, ctx.s_sr]
    w_hi = [ctx.s_wk_hi, ctx.s_wv_hi, ctx.s_wr_hi]
    w_lo = [ctx.s_wk_lo, ctx.s_wv_lo, ctx.s_wr_lo]
    WD = ECG * 128
    r = [ctx.r0, ctx.r1]
    rw = [ctx.r2, ctx.r3]
    w = [ctx.w0, ctx.w1]
    HS = SUBW + 1

    for sub in range(NSUB):
        off = sub * SUBW
        hs_loaded = []

        def f_syncH(e, off=off):
            e.wait_ge(ctx.eV.h, ctx.eV.n)
            e.wait_ge(ctx.r0.h, ctx.r0.n)
            for dc in range(ND):
                r[0].dma(e.dma_start(ctx.bfA.ap()[:, dc * HS:dc * HS + HS],
                                     ctx.s_h_hi[128 * dc:128 * (dc + 1), off:off + HS]))
                r[0].dma(e.dma_start(ctx.bfB.ap()[:, dc * HS:dc * HS + HS],
                                     ctx.s_h_lo[128 * dc:128 * (dc + 1), off:off + HS]))
            hs_loaded.append(r[0].n)
        blk.sync(f_syncH)

        for mat in range(3):
            eV0, eA0 = ctx.eV.n, ctx.eA.n
            XO = ctx.XOFF

            def f_vecM(e, mat=mat, eV0=eV0, eA0=eA0, hs_loaded=hs_loaded, XO=XO):
                e.wait_ge(r[0].h, hs_loaded[0])
                e.wait_ge(ctx.eP.h, ctx.eP.n)
                tm = ctx.vecs.ap()[:, (3 + mat) * ND:(4 + mat) * ND]
                for dc in range(ND):
                    e.wait_ge(ctx.eV.h, ctx.eV.n)
                    hh = ctx.bfA.ap()[:, dc * HS:dc * HS + HS]
                    hl = ctx.bfB.ap()[:, dc * HS:dc * HS + HS]
                    ctx.eV.inc(e.tensor_tensor(ctx.pFa_sl[0][:, 0:SUBW],
                                               hh[:, 1:HS], hl[:, 1:HS], ALU.add), 1)
                    ctx.eV.inc(e.tensor_tensor(ctx.pFb_sl[0][:, 0:SUBW],
                                               hh[:, 0:SUBW], hl[:, 0:SUBW], ALU.add), 1)
                    e.wait_ge(ctx.eV.h, ctx.eV.n)
                    ctx.eV.inc(e.tensor_scalar(ctx.pFb_sl[0][:, 0:SUBW],
                                               ctx.pFb_sl[0][:, 0:SUBW],
                                               ctx.tm1m[mat].ap()[:, dc:dc + 1],
                                               None, ALU.mult), 1)
                    e.wait_ge(ctx.eV.h, ctx.eV.n)
                    ctx.eV.inc(e.scalar_tensor_tensor(
                        ctx.pFc_sl[0][:, 0:SUBW], ctx.pFa_sl[0][:, 0:SUBW],
                        tm[:, dc:dc + 1], ctx.pFb_sl[0][:, 0:SUBW],
                        ALU.mult, ALU.add), 1)
                    e.wait_ge(ctx.eA.h, eA0 + dc + 1)
                    ctx.eV.inc(e.tensor_tensor(
                        ctx.bfB.ap()[:, XO + dc * SUBW:XO + (dc + 1) * SUBW],
                        ctx.pFc_sl[0][:, 0:SUBW],
                        ctx.bfA.ap()[:, XO + dc * SUBW:XO + (dc + 1) * SUBW],
                        ALU.subtract), 1)
                    assert ctx.eV.n == eV0 + 5 * (dc + 1)
            blk.vector(f_vecM)

            def f_actM(e, eV0=eV0, eA0=eA0, XO=XO):
                for dc in range(ND):
                    e.wait_ge(ctx.eV.h, eV0 + 5 * dc + 4)
                    ctx.eA.inc(e.activation(
                        ctx.bfA.ap()[:, XO + dc * SUBW:XO + (dc + 1) * SUBW],
                        ctx.pFc_sl[0][:, 0:SUBW], AF.Copy), 1)
                    assert ctx.eA.n == eA0 + dc + 1
            blk.scalar(f_actM)
            mix_done = ctx.eV.n

            for ecg in range(NG):
                gidx = ctx.p2_gidx
                ctx.p2_gidx += 1
                bk = 0 if ECG * 2 > 8 else ECG * (gidx % 2)
                eP0 = ctx.eP.n
                wl_cnt = {}

                def f_syncW(e, mat=mat, ecg=ecg, eP0=eP0, wl_cnt=wl_cnt):
                    e.wait_ge(ctx.r2.h, ctx.r2.n)
                    e.wait_ge(ctx.r3.h, ctx.r3.n)
                    for dc in range(ND):
                        sl = dc % 2
                        e.wait_ge(ctx.eP.h, eP0 + dc - 1 if dc >= 2 else eP0)
                        rw[sl].dma(e.dma_start(
                            ctx.wst_hi[sl].ap()[:, 0:WD],
                            w_hi[mat][128 * dc:128 * (dc + 1), ecg * WD:(ecg + 1) * WD]))
                        rw[sl].dma(e.dma_start(
                            ctx.wst_lo[sl].ap()[:, 0:WD],
                            w_lo[mat][128 * dc:128 * (dc + 1), ecg * WD:(ecg + 1) * WD]))
                        wl_cnt[dc] = rw[sl].n
                    ctx.p2_wl[(sub, mat, ecg)] = dict(wl_cnt)
                blk.sync(f_syncW)

                def f_peM(e, mat=mat, ecg=ecg, eP0=eP0, mix_done=mix_done,
                          wl_cnt=wl_cnt, XO=XO, gidx=gidx, bk=bk):
                    e.wait_ge(ctx.eV.h, mix_done)
                    prev = gidx - (2 if ECG * 2 <= 8 else 1)
                    for dc in range(ND):
                        sl = dc % 2
                        e.wait_ge(rw[sl].h, wl_cnt[dc])
                        xh = ctx.bfA.ap()[:, XO + dc * SUBW:XO + (dc + 1) * SUBW]
                        xl = ctx.bfB.ap()[:, XO + dc * SUBW:XO + (dc + 1) * SUBW]
                        for ec in range(ECG):
                            if dc == 0:
                                eng, v = ctx.p2_bankfree.get((prev, bk + ec), (None, 0))
                                if eng is not None:
                                    e.wait_ge(eng, v)
                            wh = ctx.wst_hi[sl].ap()[:, 128 * ec:128 * (ec + 1)]
                            wl = ctx.wst_lo[sl].ap()[:, 128 * ec:128 * (ec + 1)]
                            st = (dc == 0)
                            e.matmul(ctx.ps[bk + ec].ap()[:, 0:SUBW], wh, xh,
                                     start=st, stop=False)
                            e.matmul(ctx.ps[bk + ec].ap()[:, 0:SUBW], wh, xl,
                                     start=False, stop=False)
                            i = e.matmul(ctx.ps[bk + ec].ap()[:, 0:SUBW], wl, xh,
                                         start=False, stop=(dc == ND - 1))
                        ctx.eP.inc(i, 1)
                        assert ctx.eP.n == eP0 + dc + 1
                blk.tensor(f_peM)
                grp_done = ctx.eP.n

                def f_evac(e_v, e_a=None):
                    pass

                if mat < 2:
                    eVb = ctx.eV.n
                    wbE = [ctx.w0.n, ctx.w1.n]

                    def st_pred(ec, wbE=wbE):
                        return wbE[ec % 2] + 16 * (ec // 2 + 1)

                    def f_vecE(e, mat=mat, ecg=ecg, sub=sub, grp_done=grp_done,
                               eVb=eVb, wbE=wbE, st_pred=st_pred, bk=bk, gidx=gidx):
                        e.wait_ge(ctx.eP.h, grp_done)
                        for ec in range(ECG):
                            sl = ec % 2
                            e.wait_ge(w[sl].h,
                                      st_pred(ec - 2) if ec >= 2 else wbE[sl])
                            ctx.eV.inc(e.tensor_copy(ctx.pFd_sl[sl][:, 0:SUBW],
                                                     ctx.ps[bk + ec].ap()[:, 0:SUBW]), 1)
                            ctx.p2_bankfree[(gidx, bk + ec)] = (ctx.eV.h, ctx.eV.n)
                            if mat == 0:
                                e.wait_ge(ctx.eV.h, ctx.eV.n)
                                ec_g = ecg * ECG + ec
                                ctx.eV.inc(e.reduce_max(ctx.tmax.ap(),
                                                        ctx.pFd_sl[sl][:, 0:SUBW],
                                                        AX.X), 1)
                                e.wait_ge(ctx.eV.h, ctx.eV.n)
                                ctx.eV.inc(e.tensor_tensor(
                                    ctx.macc.ap()[:, ec_g:ec_g + 1],
                                    ctx.macc.ap()[:, ec_g:ec_g + 1],
                                    ctx.tmax.ap(), ALU.max), 1)
                            assert ctx.eV.n == eVb + (3 if mat == 0 else 1) * (ec + 1)
                    blk.vector(f_vecE)
                    ctx.p2_gdone[gidx] = (ctx.eV.n, ctx.eA.n)

                    def f_actE(e, mat=mat, ecg=ecg, sub=sub, eVb=eVb, grp=grp_done,
                               st_pred=st_pred):
                        k = 3 if mat == 0 else 1
                        for ec in range(ECG):
                            sl = ec % 2
                            e.wait_ge(ctx.eV.h, eVb + k * ec + 1)
                            ec_g = ecg * ECG + ec
                            ws = w[sl]
                            ws.dma(e.dma_start(
                                s_dst[mat][128 * ec_g:128 * (ec_g + 1),
                                           sub * SUBW:(sub + 1) * SUBW],
                                ctx.pFd_sl[sl][:, 0:SUBW]))
                            ctx.p2_st[(sub, mat, ecg, ec)] = ws.n
                            assert ws.n == st_pred(ec)
                    blk.scalar(f_actE)
                else:
                    eAb = ctx.eA.n

                    def f_actE2(e, ecg=ecg, sub=sub, grp_done=grp_done, eAb=eAb,
                                bk=bk, gidx=gidx):
                        e.wait_ge(ctx.eP.h, grp_done)
                        for ec in range(ECG):
                            sl = ec % 2
                            e.wait_ge(w[sl].h,
                                      ctx.p2_st.get((sub, 2, ecg, ec - 2), 0)
                                      if ec >= 2 else w[sl].n)
                            ctx.eA.inc(e.activation(ctx.pFd_sl[sl][:, 0:SUBW],
                                                    ctx.ps[bk + ec].ap()[:, 0:SUBW],
                                                    AF.Sigmoid), 1)
                            ctx.p2_bankfree[(gidx, bk + ec)] = (ctx.eA.h, ctx.eA.n)
                            e.wait_ge(ctx.eA.h, ctx.eA.n)
                            ec_g = ecg * ECG + ec
                            ws = w[sl]
                            ws.dma(e.dma_start(
                                ctx.s_sr[128 * ec_g:128 * (ec_g + 1),
                                         sub * SUBW:(sub + 1) * SUBW],
                                ctx.pFd_sl[sl][:, 0:SUBW]))
                            ctx.p2_st[(sub, 2, ecg, ec)] = ws.n
                            assert ctx.eA.n == eAb + ec + 1
                    blk.scalar(f_actE2)
                    ctx.p2_gdone[gidx] = (ctx.eV.n, ctx.eA.n)


def emit_P3(ctx):
    c = ctx.cfg
    blk = ctx.blk
    ND, W, SPAN, STRIP, NSTRIP = c["ND"], c["W"], c["SPAN"], c["STRIP"], c["NSTRIP"]
    r = [ctx.r0, ctx.r1]
    w = [ctx.w0, ctx.w1]
    ka_s = [ctx.pF_a.ap(), ctx.pF_a2.ap()]
    va_s = [ctx.pF_b.ap(), ctx.pF_b2.ap()]
    kvsem = [ctx.r0, ctx.dsb]
    vend = {}
    aend = {}
    sr_last = [0, 0]

    def f_neg(e):
        ctx.eV.inc(e.tensor_scalar(ctx.negm.ap(), ctx.macc.ap(), -1.0, None,
                                   ALU.mult), 1)
    blk.vector(f_neg)

    for ec in range(ND):
        cs = ec % 2
        ka, va = ka_s[cs], va_s[cs]
        kv_loaded = []
        vb = ctx.eV.n
        ab = ctx.eA.n
        wb = [ctx.w0.n, ctx.w1.n]
        a_kp = ab + 1

        def yh_done(si, a_kp=a_kp):
            return a_kp + si + 1

        def v_y(si, vb=vb):
            return vb + 1 + 3 + 6 * si + 6

        def sr_used(si, vb=vb):
            return vb + 1 + 3 + 6 * si + 5

        def yst_done(si, wb=wb):
            return wb[si % 2] + 32 * (si // 2 + 1)

        def f_sync(e, ec=ec, cs=cs, ka=ka, va=va, kv_loaded=kv_loaded):
            if ec >= 2:
                e.wait_ge(ctx.eV.h, vend[ec - 2])
                e.wait_ge(ctx.eA.h, aend[ec - 2])
            e.wait_ge(kvsem[cs].h, kvsem[cs].n)
            kvsem[cs].dma(e.dma_start(ka[:, 0:SPAN],
                                      ctx.s_k[128 * ec:128 * (ec + 1), :]))
            kvsem[cs].dma(e.dma_start(va[:, 0:SPAN],
                                      ctx.s_v[128 * ec:128 * (ec + 1), :]))
            kv_loaded.append(kvsem[cs].n)
        blk.sync(f_sync)

        sr_loaded = {}

        srl_prev = list(sr_last)

        def f_sync2(e, ec=ec, sr_used=sr_used, srl_prev=srl_prev):
            e.wait_ge(ctx.r1.h, ctx.r1.n)
            e.wait_ge(ctx.dsa.h, ctx.dsa.n)
            for si in range(NSTRIP):
                sl = si % 2
                ss = [ctx.r1, ctx.dsa][sl]
                e.wait_ge(ctx.eV.h,
                          sr_used(si - 2) if si >= 2 else srl_prev[sl])
                ss.dma(e.dma_start(ctx.pSc_sl[sl][:, 0:STRIP],
                                   ctx.s_sr[128 * ec:128 * (ec + 1),
                                            W + si * STRIP:W + (si + 1) * STRIP]))
                sr_loaded[si] = (ss, ss.n)
        blk.sync(f_sync2)
        for si in range(NSTRIP):
            sr_last[si % 2] = sr_used(si)

        def f_vec1(e, ec=ec, cs=cs, ka=ka, kv_loaded=kv_loaded):
            e.wait_ge(kvsem[cs].h, kv_loaded[0])
            ctx.eV.inc(e.tensor_tensor(ka[:, 0:W], ka[:, 0:W],
                                       ctx.maskneg.ap(), ALU.add), 1)
        blk.vector(f_vec1)
        v_mask = ctx.eV.n

        def f_act1(e, ec=ec, ka=ka, v_mask=v_mask, a_kp=a_kp):
            e.wait_ge(ctx.eV.h, v_mask)
            ctx.eA.inc(e.activation(ka[:, 0:SPAN], ka[:, 0:SPAN],
                                    AF.Exp, bias=ctx.negm.ap()[:, ec:ec + 1]), 1)
            assert ctx.eA.n == a_kp
        blk.scalar(f_act1)

        def f_vec2(e, ec=ec, cs=cs, ka=ka, va=va, sr_loaded=sr_loaded, a_kp=a_kp,
                   vb=vb, wb=wb, yh_done=yh_done, v_y=v_y, sr_used=sr_used,
                   yst_done=yst_done):
            e.wait_ge(ctx.eA.h, a_kp)
            ctx.eV.inc(e.tensor_tensor(va[:, 0:SPAN], ka[:, 0:SPAN],
                                       va[:, 0:SPAN], ALU.mult), 1)
            e.wait_ge(ctx.eV.h, ctx.eV.n)
            lam_b = ctx.lam16.ap()[:, ec:ec + 1].broadcast_to([128, SPAN])
            ctx.eV.inc(e.tensor_tensor_scan(ctx.pF_c.ap()[:, 0:SPAN], lam_b,
                                            va[:, 0:SPAN], 0.0,
                                            ALU.mult, ALU.add), 1)
            ctx.eV.inc(e.tensor_tensor_scan(ctx.pF_d.ap()[:, 0:SPAN], lam_b,
                                            ka[:, 0:SPAN], 0.0,
                                            ALU.mult, ALU.add), 1)
            e.wait_ge(ctx.eV.h, ctx.eV.n)
            eu_c = ctx.eu16.ap()[:, ec:ec + 1]
            for si in range(NSTRIP):
                sl = si % 2
                ks = W + si * STRIP
                if si >= 2:
                    e.wait_ge(ctx.eA.h, yh_done(si - 2))
                ctx.eV.inc(e.scalar_tensor_tensor(
                    ctx.pSa_sl[sl][:, 0:STRIP], va[:, ks:ks + STRIP],
                    eu_c, ctx.pF_c.ap()[:, ks - 1:ks - 1 + STRIP], ALU.mult, ALU.add), 1)
                ctx.eV.inc(e.scalar_tensor_tensor(
                    ctx.pSb_sl[sl][:, 0:STRIP], ka[:, ks:ks + STRIP],
                    eu_c, ctx.pF_d.ap()[:, ks - 1:ks - 1 + STRIP], ALU.mult, ALU.add), 1)
                e.wait_ge(ctx.eV.h, ctx.eV.n)
                ctx.eV.inc(e.reciprocal(ctx.pSb_sl[sl][:, 0:STRIP],
                                        ctx.pSb_sl[sl][:, 0:STRIP]), 1)
                e.wait_ge(ctx.eV.h, ctx.eV.n)
                ctx.eV.inc(e.tensor_tensor(ctx.pSa_sl[sl][:, 0:STRIP],
                                           ctx.pSa_sl[sl][:, 0:STRIP],
                                           ctx.pSb_sl[sl][:, 0:STRIP], ALU.mult), 1)
                e.wait_ge(ctx.eV.h, ctx.eV.n)
                e.wait_ge(sr_loaded[si][0].h, sr_loaded[si][1])
                ctx.eV.inc(e.tensor_tensor(ctx.pSa_sl[sl][:, 0:STRIP],
                                           ctx.pSa_sl[sl][:, 0:STRIP],
                                           ctx.pSc_sl[sl][:, 0:STRIP], ALU.mult), 1)
                assert ctx.eV.n == sr_used(si)
                e.wait_ge(ctx.eA.h, yh_done(si))
                e.wait_ge(w[sl].h, yst_done(si - 2) if si >= 2 else wb[sl])
                ctx.eV.inc(e.tensor_tensor(ctx.ylo_t[sl].ap()[:, 0:STRIP],
                                           ctx.pSa_sl[sl][:, 0:STRIP],
                                           ctx.yhi_t[sl].ap()[:, 0:STRIP],
                                           ALU.subtract), 1)
                assert ctx.eV.n == v_y(si)
        blk.vector(f_vec2)
        vend[ec] = ctx.eV.n

        def f_act2(e, ec=ec, a_kp=a_kp, wb=wb, yh_done=yh_done, v_y=v_y,
                   sr_used=sr_used, yst_done=yst_done):
            for si in range(NSTRIP):
                sl = si % 2
                e.wait_ge(ctx.eV.h, sr_used(si))
                e.wait_ge(w[sl].h, yst_done(si - 2) if si >= 2 else wb[sl])
                ctx.eA.inc(e.activation(ctx.yhi_t[sl].ap()[:, 0:STRIP],
                                        ctx.pSa_sl[sl][:, 0:STRIP], AF.Copy), 1)
                assert ctx.eA.n == yh_done(si)
                e.wait_ge(ctx.eA.h, ctx.eA.n)
                ws = w[sl]
                ws.dma(e.dma_start(ctx.s_y_hi[128 * ec:128 * (ec + 1),
                                              si * STRIP:(si + 1) * STRIP],
                                   ctx.yhi_t[sl].ap()[:, 0:STRIP]))
                e.wait_ge(ctx.eV.h, v_y(si))
                ws.dma(e.dma_start(ctx.s_y_lo[128 * ec:128 * (ec + 1),
                                              si * STRIP:(si + 1) * STRIP],
                                   ctx.ylo_t[sl].ap()[:, 0:STRIP]))
                assert ws.n == yst_done(si)
        blk.scalar(f_act2)
        aend[ec] = ctx.eA.n



def emit_P3b(ctx):
    c = ctx.cfg
    blk = ctx.blk
    ND, STRIP, NSTRIP, DCG, W = c["ND"], c["STRIP"], c["NSTRIP"], c["DCG"], c["W"]
    NG = ND // DCG
    WD = DCG * 128
    r = [ctx.r0, ctx.r1]
    rw = [ctx.r2, ctx.r3]
    w = [ctx.w0, ctx.w1]

    for sub in range(NSTRIP):
        y_loaded = []

        def f_syncY(e, sub=sub):
            e.wait_ge(ctx.eP.h, ctx.eP.n)
            e.wait_ge(ctx.r0.h, ctx.r0.n)
            for ec in range(ND):
                r[0].dma(e.dma_start(ctx.bfA.ap()[:, ec * STRIP:(ec + 1) * STRIP],
                                     ctx.s_y_hi[128 * ec:128 * (ec + 1),
                                                sub * STRIP:(sub + 1) * STRIP]))
                r[0].dma(e.dma_start(ctx.bfB.ap()[:, ec * STRIP:(ec + 1) * STRIP],
                                     ctx.s_y_lo[128 * ec:128 * (ec + 1),
                                                sub * STRIP:(sub + 1) * STRIP]))
            y_loaded.append(r[0].n)
        blk.sync(f_syncY)

        for dcg in range(NG):
            gidx = ctx.p3b_gidx
            ctx.p3b_gidx += 1
            bk = 0 if DCG * 2 > 8 else DCG * (gidx % 2)
            eP0 = ctx.eP.n
            wo_cnt = {}

            def f_syncW(e, dcg=dcg, eP0=eP0, wo_cnt=wo_cnt):
                e.wait_ge(ctx.r2.h, ctx.r2.n)
                e.wait_ge(ctx.r3.h, ctx.r3.n)
                for ec in range(ND):
                    sl = ec % 2
                    e.wait_ge(ctx.eP.h, eP0 + ec - 1 if ec >= 2 else eP0)
                    rw[sl].dma(e.dma_start(
                        ctx.wst_hi[sl].ap()[:, 0:WD],
                        ctx.s_wo_hi[128 * ec:128 * (ec + 1), dcg * WD:(dcg + 1) * WD]))
                    rw[sl].dma(e.dma_start(
                        ctx.wst_lo[sl].ap()[:, 0:WD],
                        ctx.s_wo_lo[128 * ec:128 * (ec + 1), dcg * WD:(dcg + 1) * WD]))
                    wo_cnt[ec] = rw[sl].n
            blk.sync(f_syncW)

            def f_pe(e, dcg=dcg, eP0=eP0, wo_cnt=wo_cnt, y_loaded=y_loaded,
                     gidx=gidx, bk=bk):
                e.wait_ge(r[0].h, y_loaded[0])
                prev = gidx - (2 if DCG * 2 <= 8 else 1)
                for ec in range(ND):
                    sl = ec % 2
                    e.wait_ge(rw[sl].h, wo_cnt[ec])
                    yh = ctx.bfA.ap()[:, ec * STRIP:(ec + 1) * STRIP]
                    yl = ctx.bfB.ap()[:, ec * STRIP:(ec + 1) * STRIP]
                    for dci in range(DCG):
                        if ec == 0:
                            v = ctx.p3b_bankfree.get((prev, bk + dci), 0)
                            e.wait_ge(ctx.eV.h, v)
                        wh = ctx.wst_hi[sl].ap()[:, 128 * dci:128 * (dci + 1)]
                        wl = ctx.wst_lo[sl].ap()[:, 128 * dci:128 * (dci + 1)]
                        e.matmul(ctx.ps[bk + dci].ap()[:, 0:STRIP], wh, yh,
                                 start=(ec == 0), stop=False)
                        e.matmul(ctx.ps[bk + dci].ap()[:, 0:STRIP], wh, yl,
                                 start=False, stop=False)
                        i = e.matmul(ctx.ps[bk + dci].ap()[:, 0:STRIP], wl, yh,
                                     start=False, stop=(ec == ND - 1))
                    ctx.eP.inc(i, 1)
                    assert ctx.eP.n == eP0 + ec + 1
            blk.tensor(f_pe)
            grp_done = ctx.eP.n

            xin_cnt = {}

            eVx = ctx.eV.n

            def f_syncX(e, sub=sub, dcg=dcg, xin_cnt=xin_cnt, eVx=eVx):
                e.wait_ge(ctx.dsb.h, ctx.dsb.n)
                e.wait_ge(ctx.dsc.h, ctx.dsc.n)
                for dci in range(DCG):
                    sl = dci % 2
                    xs = [ctx.dsb, ctx.dsc][sl]
                    e.wait_ge(ctx.eV.h, eVx + dci - 1 if dci >= 2 else eVx)
                    xs.dma(e.dma_start(
                        ctx.pSc_sl[sl][:, 0:STRIP],
                        ctx.in_xT[128 * (dcg * DCG + dci):128 * (dcg * DCG + dci + 1),
                                  W + 1 + sub * STRIP:W + 1 + (sub + 1) * STRIP]))
                    xin_cnt[dci] = (xs, xs.n)
            blk.sync(f_syncX)

            eV0 = ctx.eV.n
            wb3 = [ctx.w0.n, ctx.w1.n]

            def stp3(dci, wb3=wb3):
                return wb3[dci % 2] + 16 * (dci // 2 + 1)

            def f_vecE(e, sub=sub, dcg=dcg, grp_done=grp_done, xin_cnt=xin_cnt,
                       eV0=eV0, wb3=wb3, stp3=stp3, bk=bk):
                e.wait_ge(ctx.eP.h, grp_done)
                for dci in range(DCG):
                    sl = dci % 2
                    dc_g = dcg * DCG + dci
                    e.wait_ge(xin_cnt[dci][0].h, xin_cnt[dci][1])
                    e.wait_ge(ctx.w0.h if sl == 0 else ctx.w1.h,
                              stp3(dci - 2) if dci >= 2 else wb3[sl])
                    ctx.eV.inc(e.scalar_tensor_tensor(
                        ctx.pSa_sl[sl][:, 0:STRIP], ctx.ps[bk + dci].ap()[:, 0:STRIP],
                        ctx.ga_cl.ap()[:, dc_g:dc_g + 1], ctx.pSc_sl[sl][:, 0:STRIP],
                        ALU.mult, ALU.add), 1)
                    ctx.p3b_bankfree[(gidx, bk + dci)] = ctx.eV.n
                    assert ctx.eV.n == eV0 + dci + 1
            blk.vector(f_vecE)
            ctx.p3b_gdone[gidx] = ctx.eV.n

            def f_actS(e, sub=sub, dcg=dcg, eV0=eV0, stp3=stp3):
                for dci in range(DCG):
                    sl = dci % 2
                    dc_g = dcg * DCG + dci
                    e.wait_ge(ctx.eV.h, eV0 + dci + 1)
                    ws = w[sl]
                    ws.dma(e.dma_start(
                        ctx.s_x1[128 * dc_g:128 * (dc_g + 1),
                                 sub * STRIP:(sub + 1) * STRIP],
                        ctx.pSa_sl[sl][:, 0:STRIP]))
                    assert ws.n == stp3(dci)
            blk.scalar(f_actS)


class QuantStream:

    def __init__(self, ctx, items, tag):
        self.ctx = ctx
        self.items = items
        self.tag = tag
        self.pos = 0

    def emit_batch(self, k):
        ctx = self.ctx
        blk = ctx.blk
        if k % 2:
            k += 1
        its = list(range(self.pos, min(self.pos + k, len(self.items))))
        if len(its) % 2:
            its = its[:-1]
            if not its:
                its = [self.pos]
        self.pos = self.pos + len(its)
        if not its:
            return
        items = self.items
        r = [ctx.r2, ctx.r3]
        w = [ctx.w0, ctx.w1]
        eA0, eV0, eP0 = ctx.eA.n, ctx.eV.n, ctx.eP.n
        wb = [ctx.w0.n, ctx.w1.n]
        loaded = {}
        n_it = len(its)

        def TQof(it):
            return sum(x[2] for x in items[it][0])

        _off = {"s0": (1, 2), "s1a": (3, 5), "s1b": (4, 6), "s2": (7, 8),
                "s3": (9, 10), "s4": (11, 12), "s5": (13, 14), "s6": (15, 16)}

        def ev(n_, st):
            if n_ < 0:
                return eV0
            return eV0 + 16 * (n_ // 2) + _off[st][n_ % 2]

        def st_pred(n_):
            if n_ < 0:
                return wb[n_ % 2]
            return wb[n_ % 2] + 16 * (n_ // 2 + 1)

        def f_sync(e):
            e.wait_ge(ctx.r2.h, ctx.r2.n)
            e.wait_ge(ctx.r3.h, ctx.r3.n)
            for n_, it in enumerate(its):
                sl = n_ % 2
                e.wait_ge(ctx.eA.h, eA0 + 2 * (n_ - 1) if n_ >= 2 else eA0)
                for (ap, co, wd) in items[it][0]:
                    r[sl].dma(e.dma_start(ctx.wgublk[sl].ap()[:, co:co + wd], ap))
                loaded[n_] = r[sl].n
        blk.sync(f_sync)

        def f_act(e):
            def store(n_):
                jl = n_ % 2
                it = its[n_]
                e.wait_ge(ctx.eV.h, ev(n_, "s6"))
                w[jl].dma(e.dma_start(items[it][1],
                                      ctx.q_l[jl].ap()[:, 0:TQof(it)]))
                assert w[jl].n == st_pred(n_)
            for n_, it in enumerate(its):
                sl = n_ % 2
                TQ = TQof(it)
                if n_ >= 2:
                    store(n_ - 2)
                e.wait_ge(r[sl].h, loaded[n_])
                e.wait_ge(ctx.eP.h, eP0 + n_ if n_ >= 1 else eP0)
                e.wait_ge(ctx.eV.h, ev(n_ - 2, "s5"))
                ctx.eA.inc(e.activation(ctx.q_wbf[sl].ap()[:, 0:TQ],
                                        ctx.wgublk[sl].ap()[:, 0:TQ], AF.Copy), 1)
                ctx.eA.inc(e.activation(ctx.q_wab[sl].ap()[:, 0:TQ],
                                        ctx.wgublk[sl].ap()[:, 0:TQ], AF.Abs), 1)
                assert ctx.eA.n == eA0 + 2 * (n_ + 1)
            if n_it >= 2:
                store(n_it - 2)
            store(n_it - 1)
        blk.scalar(f_act)

        def f_pe(e):
            for n_, it in enumerate(its):
                sl = n_ % 2
                TQ = TQof(it)
                e.wait_ge(ctx.eA.h, eA0 + 2 * (n_ + 1))
                e.wait_ge(ctx.eV.h, ev(n_ - 2, "s0"))
                ctx.eP.inc(e.matmul(ctx.ps[sl].ap()[:, 0:TQ], ctx.g2.ap(),
                                    ctx.q_wab[sl].ap()[:, 0:TQ],
                                    start=True, stop=True), 1)
                assert ctx.eP.n == eP0 + n_ + 1
        blk.tensor(f_pe)

        def f_vec(e):
            def stage(st, n_, fn):
                if n_ >= n_it:
                    return
                fn(n_, its[n_])
                assert ctx.eV.n == ev(n_, st), (st, n_)

            def s0(n_, it):
                sl = n_ % 2
                TQ = TQof(it)
                e.wait_ge(ctx.eP.h, eP0 + n_ + 1)
                e.wait_ge(ctx.eV.h, ev(n_ - 2, "s3"))
                ctx.eV.inc(e.tensor_scalar(ctx.q_sb[sl].ap()[:, 0:TQ],
                                           ctx.ps[sl].ap()[:, 0:TQ], 1e-8,
                                           None, ALU.max), 1)

            def s1a(n_, it):
                sl = n_ % 2
                TQ = TQof(it)
                e.wait_ge(ctx.eV.h, ev(n_, "s0"))
                e.wait_ge(ctx.eV.h, ev(n_ - 2, "s5"))
                ctx.eV.inc(e.scalar_tensor_tensor(
                    ctx.q_a[sl].ap()[:, 0:TQ], ctx.q_wbf[sl].ap()[:, 0:TQ],
                    2.0, ctx.q_sb[sl].ap()[:, 0:TQ], ALU.mult, ALU.is_gt), 1)

            def s1b(n_, it):
                sl = n_ % 2
                TQ = TQof(it)
                e.wait_ge(ctx.eV.h, ev(n_ - 2, "s6"))
                ctx.eV.inc(e.scalar_tensor_tensor(
                    ctx.q_b[sl].ap()[:, 0:TQ], ctx.q_wbf[sl].ap()[:, 0:TQ],
                    -2.0, ctx.q_sb[sl].ap()[:, 0:TQ], ALU.mult, ALU.is_gt), 1)

            def s2(n_, it):
                sl = n_ % 2
                TQ = TQof(it)
                e.wait_ge(ctx.eV.h, ev(n_, "s1b"))
                ctx.eV.inc(e.tensor_tensor(ctx.q_a[sl].ap()[:, 0:TQ],
                                           ctx.q_a[sl].ap()[:, 0:TQ],
                                           ctx.q_b[sl].ap()[:, 0:TQ],
                                           ALU.subtract), 1)

            def s3(n_, it):
                sl = n_ % 2
                TQ = TQof(it)
                e.wait_ge(ctx.eV.h, ev(n_, "s2"))
                ctx.eV.inc(e.tensor_tensor(ctx.q_b[sl].ap()[:, 0:TQ],
                                           ctx.q_a[sl].ap()[:, 0:TQ],
                                           ctx.q_sb[sl].ap()[:, 0:TQ],
                                           ALU.mult), 1)

            def s4(n_, it):
                sl = n_ % 2
                TQ = TQof(it)
                e.wait_ge(ctx.eV.h, ev(n_, "s3"))
                ctx.eV.inc(e.tensor_tensor(ctx.q_a[sl].ap()[:, 0:TQ],
                                           ctx.q_b[sl].ap()[:, 0:TQ],
                                           ctx.q_wbf[sl].ap()[:, 0:TQ],
                                           ALU.subtract), 1)

            def s5(n_, it):
                sl = n_ % 2
                TQ = TQof(it)
                e.wait_ge(ctx.eV.h, ev(n_, "s4"))
                ctx.eV.inc(e.tensor_tensor(ctx.q_b[sl].ap()[:, 0:TQ],
                                           ctx.q_wbf[sl].ap()[:, 0:TQ],
                                           ctx.q_a[sl].ap()[:, 0:TQ],
                                           ALU.add), 1)

            def s6(n_, it):
                sl = n_ % 2
                TQ = TQof(it)
                e.wait_ge(ctx.eV.h, ev(n_, "s5"))
                e.wait_ge(w[sl].h, st_pred(n_ - 2))
                ctx.eV.inc(e.tensor_copy(
                    ctx.q_l[sl].ap()[:, 0:TQ].bitcast(U16),
                    ctx.q_b[sl].ap()[:, 0:TQ].bitcast(U16)), 1)

            for j in range(0, n_it, 2):
                stage("s0", j, s0)
                stage("s0", j + 1, s0)
                stage("s1a", j, s1a)
                stage("s1b", j, s1b)
                stage("s1a", j + 1, s1a)
                stage("s1b", j + 1, s1b)
                stage("s2", j, s2)
                stage("s2", j + 1, s2)
                stage("s3", j, s3)
                stage("s3", j + 1, s3)
                stage("s4", j, s4)
                stage("s4", j + 1, s4)
                stage("s5", j, s5)
                stage("s5", j + 1, s5)
                stage("s6", j, s6)
                stage("s6", j + 1, s6)
        blk.vector(f_vec)

    def emit_rest(self):
        while self.pos < len(self.items):
            self.emit_batch(64)


def emit_P4b(ctx):
    c = ctx.cfg
    blk = ctx.blk
    ND, STRIP, NSTRIP, NB4, TC, H = (c["ND"], c["STRIP"], c["NSTRIP"], c["NB4"],
                                     c["TC"], c["H"])
    r = [ctx.r0, ctx.r1]
    rw = [ctx.r2, ctx.r3]
    w = [ctx.w0, ctx.w1]

    def f_vz(e):
        ctx.eV.inc(e.memset(ctx.racc_pw.ap(), 0.0), 1)
    blk.vector(f_vz)
    racc_zeroed = ctx.eV.n

    for sub in range(NSTRIP):
        h2_loaded = []

        def f_syncH(e, sub=sub):
            e.wait_ge(ctx.eP.h, ctx.eP.n)
            e.wait_ge(ctx.r0.h, ctx.r0.n)
            for dc in range(ND):
                r[0].dma(e.dma_start(ctx.bfA.ap()[:, dc * STRIP:(dc + 1) * STRIP],
                                     ctx.s_h2_hi[128 * dc:128 * (dc + 1),
                                                 sub * STRIP:(sub + 1) * STRIP]))
                r[0].dma(e.dma_start(ctx.bfB.ap()[:, dc * STRIP:(dc + 1) * STRIP],
                                     ctx.s_h2_lo[128 * dc:128 * (dc + 1),
                                                 sub * STRIP:(sub + 1) * STRIP]))
            h2_loaded.append(r[0].n)
        blk.sync(f_syncH)

        for band in range(NB4):
            bk = 4 * (band % 2)
            eP0 = ctx.eP.n
            q_cnt = {}

            def f_syncQ(e, sub=sub, band=band, eP0=eP0, q_cnt=q_cnt):
                e.wait_ge(ctx.r2.h, ctx.r2.n)
                e.wait_ge(ctx.r3.h, ctx.r3.n)
                for kd in range(ND):
                    sl = kd % 2
                    e.wait_ge(ctx.eP.h, eP0 + kd - 1 if kd >= 2 else eP0)
                    rw[sl].dma(e.dma_start(
                        ctx.qst[sl].ap()[:, 0:512],
                        ctx.s_qgu[band, kd, :, :]))
                    q_cnt[kd] = rw[sl].n
            blk.sync(f_syncQ)

            def f_pe(e, sub=sub, band=band, bk=bk, eP0=eP0, q_cnt=q_cnt,
                     h2_loaded=h2_loaded):
                e.wait_ge(r[0].h, h2_loaded[0])
                e.wait_ge(ctx.eV.h, ctx.p4_ev.get((sub, band - 2), ctx.eV.n))
                e.wait_ge(ctx.eA.h, ctx.p4_ea.get((sub, band - 2), ctx.eA.n))
                for kd in range(ND):
                    sl = kd % 2
                    e.wait_ge(rw[sl].h, q_cnt[kd])
                    xh = ctx.bfA.ap()[:, kd * STRIP:(kd + 1) * STRIP]
                    xl = ctx.bfB.ap()[:, kd * STRIP:(kd + 1) * STRIP]
                    for hc in range(4):
                        wq = ctx.qst[sl].ap()[:, 128 * hc:128 * (hc + 1)]
                        e.matmul(ctx.ps[bk + hc].ap()[:, 0:STRIP], wq, xh,
                                 start=(kd == 0), stop=False)
                        i = e.matmul(ctx.ps[bk + hc].ap()[:, 0:STRIP], wq, xl,
                                     start=False, stop=(kd == ND - 1))
                    ctx.eP.inc(i, 1)
                    assert ctx.eP.n == eP0 + kd + 1
            blk.tensor(f_pe)
            grp_done = ctx.eP.n

            eA0, eV0 = ctx.eA.n, ctx.eV.n
            wbs = [ctx.w0.n, ctx.w1.n]

            def hst_done(j):
                return wbs[j % 2] + 32 * (j // 2 + 1)

            def f_actE(e, sub=sub, band=band, bk=bk, grp_done=grp_done,
                       eA0=eA0, eV0=eV0, wbs=wbs, hst_done=hst_done):
                e.wait_ge(ctx.eP.h, grp_done)
                for j in range(2):
                    sl = j % 2
                    e.wait_ge(ctx.eV.h, ctx.p4_sga.get((sub, band - 1, j), eV0))
                    ctx.eA.inc(e.activation(ctx.pSa_sl[sl][:, 0:STRIP],
                                            ctx.ps[bk + j].ap()[:, 0:STRIP],
                                            AF.Silu), 1)
                    e.wait_ge(ctx.eV.h, eV0 + 3 * j + 1)
                    e.wait_ge(w[sl].h, wbs[sl])
                    ctx.eA.inc(e.activation(ctx.yhi_t[sl].ap()[:, 0:STRIP],
                                            ctx.pSb_sl[sl][:, 0:STRIP], AF.Copy), 1)
                    e.wait_ge(ctx.eA.h, ctx.eA.n)
                    hrow = 128 * (2 * band + j)
                    ws = w[sl]
                    ws.dma(e.dma_start(ctx.s_hm_hi[hrow:hrow + 128,
                                                   sub * STRIP:(sub + 1) * STRIP],
                                       ctx.yhi_t[sl].ap()[:, 0:STRIP]))
                    e.wait_ge(ctx.eV.h, eV0 + 3 * j + 2)
                    ws.dma(e.dma_start(ctx.s_hm_lo[hrow:hrow + 128,
                                                   sub * STRIP:(sub + 1) * STRIP],
                                       ctx.ylo_t[sl].ap()[:, 0:STRIP]))
                    assert ws.n == hst_done(j)
                    e.wait_ge(ctx.eV.h, ctx.p4_sqv.get((sub, band - 1, j), eV0))
                    ctx.eA.inc(e.activation(ctx.pSc_sl[sl][:, 0:STRIP],
                                            ctx.pSb_sl[sl][:, 0:STRIP], AF.Square), 1)
                    assert ctx.eA.n == eA0 + 3 * (j + 1)
            blk.scalar(f_actE)
            ctx.p4_ea[(sub, band)] = ctx.eA.n

            def f_vecE(e, sub=sub, band=band, bk=bk, eA0=eA0, eV0=eV0,
                       racc_zeroed=racc_zeroed):
                for j in range(2):
                    sl = j % 2
                    e.wait_ge(ctx.eA.h, eA0 + 3 * j + 1)
                    e.wait_ge(ctx.eA.h, ctx.p4_hmfree.get((sub, band - 1, j), eA0))
                    ctx.eV.inc(e.tensor_tensor(ctx.pSb_sl[sl][:, 0:STRIP],
                                               ctx.pSa_sl[sl][:, 0:STRIP],
                                               ctx.ps[bk + 2 + j].ap()[:, 0:STRIP],
                                               ALU.mult), 1)
                    ctx.p4_sga[(sub, band, j)] = ctx.eV.n
                    e.wait_ge(ctx.eA.h, eA0 + 3 * j + 2)
                    e.wait_ge(w[sl].h, hst_done(j - 2) if j >= 2 else wbs[sl])
                    ctx.eV.inc(e.tensor_tensor(ctx.ylo_t[sl].ap()[:, 0:STRIP],
                                               ctx.pSb_sl[sl][:, 0:STRIP],
                                               ctx.yhi_t[sl].ap()[:, 0:STRIP],
                                               ALU.subtract), 1)
                    e.wait_ge(ctx.eA.h, eA0 + 3 * (j + 1))
                    e.wait_ge(ctx.eV.h, max(ctx.eV.n, racc_zeroed))
                    ctx.eV.inc(e.tensor_tensor(
                        ctx.racc_pw.ap()[:, sub * STRIP:(sub + 1) * STRIP],
                        ctx.racc_pw.ap()[:, sub * STRIP:(sub + 1) * STRIP],
                        ctx.pSc_sl[sl][:, 0:STRIP], ALU.add), 1)
                    ctx.p4_sqv[(sub, band, j)] = ctx.eV.n
                    assert ctx.eV.n == eV0 + 3 * (j + 1)
            blk.vector(f_vecE)
            ctx.p4_ev[(sub, band)] = ctx.eV.n
            ctx.p4_hmfree[(sub, band, 0)] = eA0 + 3
            ctx.p4_hmfree[(sub, band, 1)] = eA0 + 6

    eVr = ctx.eV.n

    def f_peR(e, eVr=eVr):
        e.wait_ge(ctx.eV.h, eVr)
        e.wait_ge(ctx.eA.h, ctx.eA.n)
        for sub in range(NSTRIP):
            ctx.eP.inc(e.matmul(ctx.ps[sub].ap()[0:1, 0:STRIP], ctx.ones128.ap(),
                                ctx.racc_pw.ap()[:, sub * STRIP:(sub + 1) * STRIP],
                                start=True, stop=True), 1)
    blk.tensor(f_peR)
    pR = ctx.eP.n

    def f_vecR(e, pR=pR):
        e.wait_ge(ctx.eP.h, pR)
        for sub in range(NSTRIP):
            ctx.eV.inc(e.tensor_copy(
                ctx.rows1.ap()[:, sub * STRIP:(sub + 1) * STRIP],
                ctx.ps[sub].ap()[0:1, 0:STRIP]), 1)
    blk.vector(f_vecR)
    ctx.p4_gp_done = 0


def emit_P5b(ctx):
    c = ctx.cfg
    blk = ctx.blk
    ND, NH, STRIP, NSTRIP, TC, H, DCG = (c["ND"], c["NH"], c["STRIP"],
                                         c["NSTRIP"], c["TC"], c["H"], c["DCG"])
    NG = ND // DCG
    WD = DCG * 128
    r = [ctx.r0, ctx.r1]
    rw = [ctx.r2, ctx.r3]
    w = [ctx.w0, ctx.w1]

    def f_vI(e):
        e.wait_ge(ctx.eV.h, ctx.eV.n)
        ctx.eV.inc(e.tensor_scalar(ctx.rows1.ap()[:, 0:TC], ctx.rows1.ap()[:, 0:TC],
                                   1.0 / H, None, ALU.mult), 1)
    blk.vector(f_vI)

    def f_aI(e):
        e.wait_ge(ctx.eV.h, ctx.eV.n)
        ctx.eA.inc(e.activation(ctx.rows1.ap()[:, 0:TC], ctx.rows1.ap()[:, 0:TC],
                                AF.Sqrt, bias=ctx.epsrms.ap()), 1)
    blk.scalar(f_aI)

    def f_vI2(e):
        e.wait_ge(ctx.eA.h, ctx.eA.n)
        ctx.eV.inc(e.reciprocal(ctx.rows1.ap()[:, 0:TC], ctx.rows1.ap()[:, 0:TC]), 1)
    blk.vector(f_vI2)
    inv_ready = ctx.eV.n

    for sub in range(NSTRIP):
        def f_peI(e, sub=sub):
            e.wait_ge(ctx.eV.h, ctx.eV.n)
            e.wait_ge(ctx.eP.h, ctx.eP.n)
            ctx.eP.inc(e.matmul(ctx.ps[7].ap()[:, 0:STRIP], ctx.onesrow.ap(),
                                ctx.rows1.ap()[:, sub * STRIP:(sub + 1) * STRIP],
                                start=True, stop=True), 1)
        blk.tensor(f_peI)
        p_invb = ctx.eP.n

        def f_vInvb(e, sub=sub, p_invb=p_invb):
            e.wait_ge(ctx.eP.h, p_invb)
            e.wait_ge(ctx.eV.h, ctx.eV.n)
            ctx.eV.inc(e.tensor_copy(ctx.invb.ap()[:, 0:STRIP],
                                     ctx.ps[7].ap()[:, 0:STRIP]), 1)
        blk.vector(f_vInvb)
        ctx.p5_invb_free[sub] = ctx.eV.n
        invb_ready = ctx.eV.n

        for dch in range(NG):
            gidx = ctx.p5_gidx
            ctx.p5_gidx += 1
            bk5 = 0 if DCG * 2 > 8 else DCG * (gidx % 2)
            eP0 = ctx.eP.n
            ld_cnt = {}

            def f_syncL(e, sub=sub, dch=dch, eP0=eP0, ld_cnt=ld_cnt):
                for x_ in (ctx.r0, ctx.r1, ctx.r2, ctx.r3):
                    e.wait_ge(x_.h, x_.n)
                for hk in range(NH):
                    sl = hk % 2
                    e.wait_ge(ctx.eP.h, eP0 + hk - 1 if hk >= 2 else eP0)
                    rw[sl].dma(e.dma_start(
                        ctx.qst[sl].ap()[:, 0:WD],
                        ctx.s_qp[hk, :, dch * WD:(dch + 1) * WD]))
                    r[sl].dma(e.dma_start(
                        ctx.hm_hi[sl].ap()[:, 0:STRIP],
                        ctx.s_hm_hi[128 * hk:128 * (hk + 1),
                                    sub * STRIP:(sub + 1) * STRIP]))
                    r[sl].dma(e.dma_start(
                        ctx.hm_lo[sl].ap()[:, 0:STRIP],
                        ctx.s_hm_lo[128 * hk:128 * (hk + 1),
                                    sub * STRIP:(sub + 1) * STRIP]))
                    ld_cnt[hk] = (rw[sl].n, r[sl].n)
            blk.sync(f_syncL)

            def f_pe(e, sub=sub, dch=dch, eP0=eP0, ld_cnt=ld_cnt,
                     invb_ready=invb_ready, gidx=gidx, bk5=bk5):
                e.wait_ge(ctx.eV.h, invb_ready)
                prev = gidx - (2 if DCG * 2 <= 8 else 1)
                for hk in range(NH):
                    sl = hk % 2
                    e.wait_ge(rw[sl].h, ld_cnt[hk][0])
                    e.wait_ge(r[sl].h, ld_cnt[hk][1])
                    for dci in range(DCG):
                        if hk == 0:
                            v = ctx.p5_bankfree.get((prev, bk5 + dci), 0)
                            e.wait_ge(ctx.eV.h, v)
                        wq = ctx.qst[sl].ap()[:, 128 * dci:128 * (dci + 1)]
                        e.matmul(ctx.ps[bk5 + dci].ap()[:, 0:STRIP], wq,
                                 ctx.hm_hi[sl].ap()[:, 0:STRIP],
                                 start=(hk == 0), stop=False)
                        i = e.matmul(ctx.ps[bk5 + dci].ap()[:, 0:STRIP], wq,
                                     ctx.hm_lo[sl].ap()[:, 0:STRIP],
                                     start=False, stop=(hk == NH - 1))
                    ctx.eP.inc(i, 1)
                    assert ctx.eP.n == eP0 + hk + 1
            blk.tensor(f_pe)
            grp_done = ctx.eP.n

            x1_cnt = {}

            eVx = ctx.eV.n

            def f_syncX(e, sub=sub, dch=dch, x1_cnt=x1_cnt, eVx=eVx):
                e.wait_ge(ctx.dsb.h, ctx.dsb.n)
                e.wait_ge(ctx.dsc.h, ctx.dsc.n)
                for dci in range(DCG):
                    sl = dci % 2
                    xs = [ctx.dsb, ctx.dsc][sl]
                    e.wait_ge(ctx.eV.h, eVx + 2 * (dci - 1) if dci >= 2 else eVx)
                    xs.dma(e.dma_start(
                        ctx.pSc_sl[sl][:, 0:STRIP],
                        ctx.s_x1[128 * (dch * DCG + dci):128 * (dch * DCG + dci + 1),
                                 sub * STRIP:(sub + 1) * STRIP]))
                    x1_cnt[dci] = (xs, xs.n)
            blk.sync(f_syncX)

            eV0 = ctx.eV.n
            obs = [ctx.out_sem.n, ctx.out2.n]
            osm = [ctx.out_sem, ctx.out2]

            def ostp(dci, obs=obs):
                return obs[dci % 2] + 16 * (dci // 2 + 1)

            def f_vecE(e, sub=sub, dch=dch, grp_done=grp_done, x1_cnt=x1_cnt,
                       eV0=eV0, invb_ready=invb_ready, obs=obs, osm=osm, ostp=ostp,
                       bk5=bk5):
                e.wait_ge(ctx.eP.h, grp_done)
                e.wait_ge(ctx.eV.h, invb_ready)
                for dci in range(DCG):
                    sl = dci % 2
                    dc_g = dch * DCG + dci
                    e.wait_ge(x1_cnt[dci][0].h, x1_cnt[dci][1])
                    ctx.eV.inc(e.tensor_tensor(ctx.pSa_sl[sl][:, 0:STRIP],
                                               ctx.ps[bk5 + dci].ap()[:, 0:STRIP],
                                               ctx.invb.ap()[:, 0:STRIP],
                                               ALU.mult), 1)
                    ctx.p5_bankfree[(gidx, bk5 + dci)] = ctx.eV.n
                    e.wait_ge(ctx.eV.h, ctx.eV.n)
                    e.wait_ge(osm[sl].h,
                              ostp(dci - 2) if dci >= 2 else obs[sl])
                    ctx.eV.inc(e.scalar_tensor_tensor(
                        ctx.pSb_sl[sl][:, 0:STRIP], ctx.pSa_sl[sl][:, 0:STRIP],
                        ctx.gb_cl.ap()[:, dc_g:dc_g + 1], ctx.pSc_sl[sl][:, 0:STRIP],
                        ALU.mult, ALU.add), 1)
                    assert ctx.eV.n == eV0 + 2 * (dci + 1)
            blk.vector(f_vecE)
            ctx.p5_gdone[gidx] = ctx.eV.n

            def f_actS(e, sub=sub, dch=dch, eV0=eV0, obs=obs, osm=osm, ostp=ostp):
                for dci in range(DCG):
                    sl = dci % 2
                    dc_g = dch * DCG + dci
                    e.wait_ge(ctx.eV.h, eV0 + 2 * (dci + 1))
                    e.wait_ge(osm[sl].h,
                              ostp(dci - 2) if dci >= 2 else obs[sl])
                    osm[sl].dma(e.dma_start(
                        ctx.out_ap[128 * dc_g:128 * (dc_g + 1),
                                   sub * STRIP:(sub + 1) * STRIP],
                        ctx.pSb_sl[sl][:, 0:STRIP]))
                    assert osm[sl].n == ostp(dci)
            blk.scalar(f_actS)


def emit_fin(ctx):
    c = ctx.cfg

    def f_fin(e):
        e.wait_ge(ctx.out_sem.h, ctx.out_sem.n)
        e.wait_ge(ctx.out2.h, ctx.out2.n)
    ctx.blk.scalar(f_fin)

    def f_fin2(e):
        e.wait_ge(ctx.out_sem.h, ctx.out_sem.n)
        e.wait_ge(ctx.out2.h, ctx.out2.n)
    ctx.blk.gpsimd(f_fin2)


def build_nc(cfg, debug_scratch=(), phases=99):
    c = cfg
    D, TC, W, H = c["D"], c["TC"], c["W"], c["H"]
    ND, NH, NMOD, SPAN, SUBW, STRIP = (c["ND"], c["NH"], c["NMOD"], c["SPAN"],
                                       c["SUBW"], c["STRIP"])
    NB4 = c["NB4"]
    nc = bass.Bass("TRN2", target_bir_lowering=False)
    ctx = Ctx()
    ctx.nc = nc
    ctx.cfg = cfg

    def dram(name, shape, dt, kind="Internal"):
        if name in debug_scratch:
            kind = "ExternalOutput"
        return nc.dram_tensor(name, shape, dt, kind=kind).ap()

    ctx.in_xT = dram("xT", [D, c["HCOLS"]], F32, "ExternalInput")
    ctx.in_wkT = dram("wkT", [D, D], F32, "ExternalInput")
    ctx.in_wvT = dram("wvT", [D, D], F32, "ExternalInput")
    ctx.in_wrT = dram("wrT", [D, D], F32, "ExternalInput")
    ctx.in_woT = dram("woT", [D, D], F32, "ExternalInput")
    ctx.in_wguT = dram("wguT", [D, 2 * H], F32, "ExternalInput")
    ctx.in_wprT = dram("wprT", [H, D], F32, "ExternalInput")
    ctx.in_wmod = dram("wmod_t", [NMOD, 128, ND * 128], F32, "ExternalInput")
    ctx.in_pvecs = dram("pvecs", [128, 6 * ND], F32, "ExternalInput")
    ctx.in_wmask = dram("wmask", [1, SPAN], F32, "ExternalInput")
    ctx.out_ap = dram("outT", [D, TC], F32, "ExternalOutput")

    ctx.s_wk_hi = dram("s_wk_hi", [D, D], BF16)
    ctx.s_wk_lo = dram("s_wk_lo", [D, D], BF16)
    ctx.s_wv_hi = dram("s_wv_hi", [D, D], BF16)
    ctx.s_wv_lo = dram("s_wv_lo", [D, D], BF16)
    ctx.s_wr_hi = dram("s_wr_hi", [D, D], BF16)
    ctx.s_wr_lo = dram("s_wr_lo", [D, D], BF16)
    ctx.s_wo_hi = dram("s_wo_hi", [D, D], BF16)
    ctx.s_wo_lo = dram("s_wo_lo", [D, D], BF16)
    ctx.s_h_hi = dram("s_h_hi", [D, c["HCOLS"]], BF16)
    ctx.s_h_lo = dram("s_h_lo", [D, c["HCOLS"]], BF16)
    ctx.s_k = dram("s_k", [D, SPAN], F32)
    ctx.s_v = dram("s_v", [D, SPAN], F32)
    ctx.s_sr = dram("s_sr", [D, SPAN], F32)
    ctx.s_y_hi = dram("s_y_hi", [D, TC], BF16)
    ctx.s_y_lo = dram("s_y_lo", [D, TC], BF16)
    ctx.s_x1 = dram("s_x1", [D, TC], F32)
    ctx.s_h2_hi = dram("s_h2_hi", [D, TC], BF16)
    ctx.s_h2_lo = dram("s_h2_lo", [D, TC], BF16)
    ctx.s_qgu = dram("s_qgu", [NB4, ND, 128, 512], BF16)
    ctx.s_hm_hi = dram("s_hm_hi", [H, TC], BF16)
    ctx.s_hm_lo = dram("s_hm_lo", [H, TC], BF16)
    ctx.s_qp = dram("s_qp", [NH, 128, D], BF16)

    SW = max(SUBW + 1, STRIP)
    PFW = max(SPAN, 2 * SW)
    BFW = max(ND * (2 * SUBW + 1) + 2, ND * STRIP)
    WSTW = c["ECG"] * 128
    QSTW = max(512, c["DCG"] * 128)

    def sb(name, shape, dt):
        return nc.alloc_sbuf_tensor(name, shape, dt)

    ctx.vecs = sb("vecs", [128, 6 * ND], F32)
    ctx.mod_sb = sb("mod_sb", [128, NMOD], F32)
    for nm in ["c1a", "c1b", "ga_cl", "gb_cl", "tmpg", "tmpg2", "silu_c", "tw",
               "eu16", "lam16", "negm", "macc"]:
        setattr(ctx, nm, sb(nm, [128, ND], F32))
    ctx.tm1m = [sb(f"tm1m{i}", [128, ND], F32) for i in range(3)]
    ctx.zs = sb("zs", [128, ND], F32)
    ctx.ones128 = sb("ones128", [128, 1], F32)
    ctx.tmax = sb("tmax", [128, 1], F32)
    ctx.maskneg = sb("maskneg", [128, W], F32)
    ctx.pF_a = sb("pF_a", [128, PFW], F32)
    ctx.pF_b = sb("pF_b", [128, PFW], F32)
    ctx.pF_a2 = sb("pF_a2", [128, c["SPAN"]], F32)
    ctx.pF_b2 = sb("pF_b2", [128, c["SPAN"]], F32)
    ctx.pF_c = sb("pF_c", [128, PFW], F32)
    ctx.pF_d = sb("pF_d", [128, PFW], F32)
    ctx.pFa_sl = [ctx.pF_a.ap()[:, i * SW:(i + 1) * SW] for i in range(2)]
    ctx.pFb_sl = [ctx.pF_b.ap()[:, i * SW:(i + 1) * SW] for i in range(2)]
    ctx.pFc_sl = [ctx.pF_c.ap()[:, i * SW:(i + 1) * SW] for i in range(2)]
    ctx.pFd_sl = [ctx.pF_d.ap()[:, i * SW:(i + 1) * SW] for i in range(2)]
    ctx.pS_a = sb("pS_a", [128, 2 * STRIP], F32)
    ctx.pS_b = sb("pS_b", [128, 2 * STRIP], F32)
    ctx.pS_c = sb("pS_c", [128, 2 * STRIP], F32)
    ctx.pSa_sl = [ctx.pS_a.ap()[:, i * STRIP:(i + 1) * STRIP] for i in range(2)]
    ctx.pSb_sl = [ctx.pS_b.ap()[:, i * STRIP:(i + 1) * STRIP] for i in range(2)]
    ctx.pSc_sl = [ctx.pS_c.ap()[:, i * STRIP:(i + 1) * STRIP] for i in range(2)]
    ctx.wblkF = [sb(f"wblkF{i}", [128, max(D, ND * 128)], F32) for i in range(2)]
    ctx.slab = ctx.wblkF
    ctx.wgublk = [sb(f"wgublk{i}", [128, 512], F32) for i in range(2)]
    ctx.invb = sb("invb", [128, STRIP], F32)
    ctx.rows1 = sb("rows1", [1, TC], F32)
    ctx.rows2 = sb("rows2", [1, 3 * 512], F32)
    ctx.onesrow = sb("onesrow", [1, 128], F32)
    ctx.mrow = sb("mrow", [1, W], F32)
    ctx.epsln = sb("epsln", [1, 1], F32)
    ctx.epsrms = sb("epsrms", [1, 1], F32)
    ctx.racc_pw = sb("racc_pw", [128, TC], F32)

    ctx.bfA = sb("bfA", [128, BFW], BF16)
    ctx.bfB = sb("bfB", [128, BFW], BF16)
    ctx.XOFF = ND * (SUBW + 1)
    ctx.wst_hi = [sb(f"wsth{i}", [128, WSTW], BF16) for i in range(2)]
    ctx.wst_lo = [sb(f"wstl{i}", [128, WSTW], BF16) for i in range(2)]
    ctx.whi_t = [ctx.bfA.ap()[:, i * D:(i + 1) * D] for i in range(2)]
    ctx.wlo_t = [ctx.bfB.ap()[:, i * D:(i + 1) * D] for i in range(2)]
    ctx.hhi_t = [sb(f"hhit{i}", [128, SW], BF16) for i in range(2)]
    ctx.hlo_t = [sb(f"hlot{i}", [128, SW], BF16) for i in range(2)]
    ctx.yhi_t = [sb(f"yhit{i}", [128, STRIP], BF16) for i in range(2)]
    ctx.ylo_t = [sb(f"ylot{i}", [128, STRIP], BF16) for i in range(2)]
    for nm in ["q_wbf", "q_wab", "q_l", "q_sb", "q_a", "q_b"]:
        setattr(ctx, nm, [sb(f"{nm}{i}", [128, 512], BF16) for i in range(2)])
    ctx.qst = [sb(f"qst{i}", [128, QSTW], BF16) for i in range(2)]
    ctx.hm_hi = [sb(f"hmhi{i}", [128, STRIP], BF16) for i in range(2)]
    ctx.hm_lo = [sb(f"hmlo{i}", [128, STRIP], BF16) for i in range(2)]
    ctx.g2 = sb("g2", [128, 128], BF16)

    ctx.ps = [nc.alloc_psum_tensor(f"ps{i}", [128, 512], F32) for i in range(8)]

    ctx.sbuf_left = nc.sbuf_bytes_remaining

    ctx.slab_loaded = {}
    ctx.hst_done = {}
    ctx.p2_wl = {}
    ctx.p2_st = {}
    ctx.p2_gdone = {}
    ctx.p2_bankfree = {}
    ctx.p2_gidx = 0
    ctx.p3b_gdone = {}
    ctx.p3b_bankfree = {}
    ctx.p3b_gidx = 0
    ctx.p3b_st = {}
    ctx.p3b_xfree = {}
    ctx.p4_ev = {}
    ctx.p4_ea = {}
    ctx.p4_sga = {}
    ctx.p4_sqv = {}
    ctx.p4_hmfree = {}
    ctx.p4_gp_done = 0
    ctx.p5_gdone = {}
    ctx.p5_bankfree = {}
    ctx.p5_gidx = 0
    ctx.p5_st = {}
    ctx.p5_xfree = {}
    ctx.p5_invb_free = {}

    with ExitStack() as stack:
        led = Ledger()
        for nm in ["eS", "eA", "eV", "eP", "eG", "r0", "r1", "r2", "r3",
                   "w0", "w1", "dsa", "dsb", "dsc", "out_sem", "out2"]:
            setattr(ctx, nm, led.new(stack, nc, nm))
        blk = stack.enter_context(nc.Block())
        ctx.blk = blk

        ph = 0

        def nxt():
            nonlocal ph
            ph += 1
            return ph <= phases
        if nxt():
            emit_P0(ctx)
            barrier(ctx)
        if nxt():
            emit_P2a(ctx)
            barrier(ctx)
        if nxt():
            ln1w = [SUBW + 1] + [SUBW] * (c["NSUB"] - 1)
            emit_LN(ctx, ctx.in_xT, ctx.s_h_hi, ctx.s_h_lo, ctx.c1a.ap(), 0, ln1w, "ln1")
            barrier(ctx)
        if nxt():
            emit_P2(ctx)
            barrier(ctx)
        items4 = []
        for band in range(NB4):
            for kd in range(ND):
                items4.append((
                    [(ctx.in_wguT[128 * kd:128 * (kd + 1), 256 * band:256 * (band + 1)], 0, 256),
                     (ctx.in_wguT[128 * kd:128 * (kd + 1), H + 256 * band:H + 256 * (band + 1)], 256, 256)],
                    ctx.s_qgu[band, kd, :, :]))
        items5 = []
        QW = min(512, D)
        for hk in range(NH):
            for b in range(D // QW):
                items5.append((
                    [(ctx.in_wprT[128 * hk:128 * (hk + 1), QW * b:QW * (b + 1)], 0, QW)],
                    ctx.s_qp[hk, :, QW * b:QW * (b + 1)]))
        qs = QuantStream(ctx, items4 + items5, "q")
        ctx.quant_stream = qs
        if nxt():
            emit_P3(ctx)
            barrier(ctx)
        if nxt():
            emit_P3b(ctx)
            barrier(ctx)
        if nxt():
            ln2w = [STRIP] * c["NSTRIP"]
            emit_LN(ctx, ctx.s_x1, ctx.s_h2_hi, ctx.s_h2_lo, ctx.c1b.ap(), 3, ln2w, "ln2")
            barrier(ctx)
        if nxt():
            qs.emit_rest()
            barrier(ctx)
        if nxt():
            emit_P4b(ctx)
            barrier(ctx)
        if nxt():
            barrier(ctx)
        if nxt():
            emit_P5b(ctx)
        emit_fin(ctx)
    return nc




CFG = make_cfg(D=2048, TC=2048, W=192, H=8192, SUBW=448, n_cores=8)
_NC_CACHE = []


def kernel(**inputs):
    from concourse.bass_utils import run_bass_kernel_spmd
    cfg = CFG
    in_maps = host_prepare(cfg, inputs)
    if not _NC_CACHE:
        _NC_CACHE.append(build_nc(cfg))
    nc = _NC_CACHE[0]
    r = run_bass_kernel_spmd(nc, in_maps, core_ids=list(range(cfg["n_cores"])))
    return host_gather(cfg, r.results)
